# version 28 (final)
# baseline (speedup 1.0000x reference)
import math
import numpy as np
import ml_dtypes
import concourse.bass as bass
import concourse.mybir as mybir
from concourse.bass_utils import run_bass_kernel_spmd
from contextlib import ExitStack

F32 = mybir.dt.float32
BF16 = mybir.dt.bfloat16
AF = mybir.ActivationFunctionType
ALU = mybir.AluOpType
AX = mybir.AxisListType
bf16 = ml_dtypes.bfloat16

T = 2048
D = 1024
NCH = 8
TTN = 16
QGN = 4
BPC = 2
ENGS = ("tensor", "vector", "scalar", "sync", "gpsimd")


class Sched:
    def __init__(self, nc, ctx, blk):
        self.nc, self.ctx, self.blk = nc, ctx, blk
        self.cnt, self.sem = {}, {}
        for name in ("mm", "ve", "ev", "gv", "wq", "wg", "wg2", "wg3", "out"):
            self.sem[name] = ctx.enter_context(nc.semaphore("s_" + name))
            self.cnt[name] = 0
        self.ops = {e: [] for e in ENGS}

    def op(self, eng, build, waits=(), inc=None, k=1, chain=True):
        waits = list(waits)
        if inc in ("wq", "wg", "wg2", "wg3", "gv", "out"):
            waits.append((inc, self.cnt[inc]))
        if eng == "vector":
            if chain is True:
                if self.cnt["ve"]:
                    waits.append(("ve", self.cnt["ve"]))
            elif chain:
                waits.append(("ve", chain))
            inc = inc or "ve"
        elif eng == "scalar":
            if self.cnt["ev"]:
                waits.append(("ev", self.cnt["ev"]))
            inc = inc or "ev"
        after = None
        if inc is not None:
            self.cnt[inc] += k
            after = self.cnt[inc]
        self.ops[eng].append((tuple(waits), build, inc, k))
        return after

    def fence_all(self):
        waits = tuple((n, v) for n, v in self.cnt.items() if v > 0)
        for e in ENGS:
            self.ops[e].append((waits, None, None, 0))

    def emit(self):
        for e in ENGS:
            ops = self.ops[e]
            if not ops:
                continue
            self.ops[e] = []

            def runner(eng, ops=ops):
                for waits, build, inc, k in ops:
                    for sname, val in waits:
                        eng.wait_ge(self.sem[sname], val)
                    if build is not None:
                        ins = build(eng)
                        if inc is not None:
                            ins.then_inc(self.sem[inc], k)
            getattr(self.blk, e)(runner)


def _rcp_bf16(nc, v, out, in_):
    with nc.allow_low_precision(reason="bf16 reciprocal of softmax denominators; 0.4% rel on a well-conditioned scale"):
        return v.reciprocal(out, in_)


def _stt_bf16(nc, v, out, in0, scalar, in1):
    with nc.allow_low_precision(reason="bf16 softmax denominator accumulate; ~0.4% rel on a well-conditioned positive sum"):
        return v.scalar_tensor_tensor(out, in0, scalar, in1, ALU.mult, ALU.add)


def _mk_consts(nc):
    c = {}
    c["identbf"] = nc.inline_tensor(np.eye(128, dtype=bf16), name="identbf")
    c["ident32"] = nc.inline_tensor(np.eye(128, dtype=np.float32), name="ident32")
    c["onesbf"] = nc.inline_tensor(np.ones((128, 512), dtype=bf16), name="onesbf")
    c["ones32"] = nc.inline_tensor(np.ones((128, 128), dtype=np.float32), name="ones32")
    m = np.zeros((128, 4 * 512), dtype=bf16)
    ll = np.arange(128)[:, None]
    jj = np.arange(512)[None, :]
    for o in range(4):
        m[:, o * 512:(o + 1) * 512] = (jj >= ll + 128 * o).astype(bf16)
    c["maskc"] = nc.inline_tensor(m, name="maskc")
    mr = np.zeros((128, 8 * 512), dtype=bf16)
    for o in range(4):
        mr[:, 1024 * o:1024 * o + 512] = m[:, 512 * o:512 * o + 512]
        mr[:, 1024 * o + 512:1024 * o + 1024] = m[:, 512 * o:512 * o + 512]
    c["maskcR"] = nc.inline_tensor(mr, name="maskcR")
    half = 16
    inv = 1.0 / (10000.0 ** (np.arange(half, dtype=np.float64) / half))
    ang = np.arange(T, dtype=np.float64)[:, None] * inv[None, :]
    cosv = np.cos(ang).T.astype(np.float32)
    sinv = np.sin(ang).T.astype(np.float32)
    C = np.zeros((128, T), dtype=np.float32)
    Sm = np.zeros((128, T), dtype=np.float32)
    for p in range(4):
        C[32 * p:32 * p + 16] = cosv
        C[32 * p + 16:32 * p + 32] = cosv
        Sm[32 * p:32 * p + 16] = -sinv
        Sm[32 * p + 16:32 * p + 32] = sinv
    c["ropeC"] = nc.inline_tensor(C.astype(bf16), name="ropeC")
    c["ropeS"] = nc.inline_tensor(Sm.astype(bf16), name="ropeS")
    P = np.zeros((128, 128), dtype=bf16)
    for r in range(128):
        P[(r // 32) * 32 + (r % 32 + 16) % 32, r] = 1.0
    c["pswap"] = nc.inline_tensor(P, name="pswap")
    return c


_WDMA_ROT = ["wg", "wg2", "wg3"]


def wdma(S, dst, src_ap, waits=()):
    S._wrot = (getattr(S, "_wrot", 0) + 1) % 3
    sem = _WDMA_ROT[S._wrot]
    v = S.op("gpsimd", lambda g: g.dma_start(out=dst, in_=src_ap),
             waits=waits, inc=sem, k=16)
    return (sem, v)


def cc3(ap2d):
    return ap2d.rearrange("(c p) n -> p c n", p=128)


def sb3(buf, n):
    return buf[:].rearrange("p (c n) -> p c n", n=n)


def bias_col(ap1d):
    return ap1d.rearrange("(c a) -> a c", a=128)



def phase_xload(st, b):
    S, nc = st["S"], st["nc"]
    x32, xbf, ident32 = st["x32"], st["xbf"], st["ident32"]
    xin = st["xin"]
    S.fence_all()
    with nc.psum_tensor("pxl%d" % b, [128, 1024], F32) as pT, \
            nc.sbuf_tensor("xtok_l%d" % b, [128, 2 * D], F32) as xtok:
        ev = {}
        for j in range(TTN):
            w = [("ev", ev[(j - 2, NCH - 1)])] if j >= 2 else ()
            dj = S.op("sync", lambda s, j=j: s.dma_start(
                out=xtok[:, (j % 2) * D:(j % 2) * D + D],
                in_=xin.ap()[b, 128 * j:128 * j + 128, :]), waits=w, inc="wq", k=16)
            for c in range(NCH):
                k = 8 * j + c
                w2 = [("wq", dj)] if c == 0 else []
                if k >= 2:
                    w2.append(("ev", ev[((k - 2) // 8, (k - 2) % 8)]))
                mmk = S.op("tensor", lambda t, j=j, c=c, k=k: t.transpose(
                    pT[:, (k % 2) * 512:(k % 2) * 512 + 128],
                    xtok[:, (j % 2) * D + 128 * c:(j % 2) * D + 128 * c + 128],
                    ident32[:]), waits=w2, inc="mm")
                ev[(j, c)] = S.op("scalar", lambda sc, j=j, c=c, k=k: sc.activation(
                    x32[:, T * c + 128 * j:T * c + 128 * j + 128],
                    pT[:, (k % 2) * 512:(k % 2) * 512 + 128], AF.Identity),
                    waits=[("mm", mmk)], inc="ev")
        evl = S.cnt["ev"]
        for c in range(NCH):
            S.op("vector", lambda v, c=c: v.tensor_copy(
                xbf[:, T * c:T * c + T], x32[:, T * c:T * c + T]),
                waits=[("ev", evl)] if c == 0 else (), inc="ve")
        S.fence_all()
        S.emit()


def phase_store(st, b):
    S, nc = st["S"], st["nc"]
    x32, ident32 = st["x32"], st["ident32"]
    xout = st["xout"]
    S.fence_all()
    with nc.psum_tensor("pst%d" % b, [128, 1024], F32) as pT, \
            nc.sbuf_tensor("xtok_s%d" % b, [128, 2 * D], F32) as xtok:
        ev = {}
        dmas = {}
        for j in range(TTN):
            for c in range(NCH):
                k = 8 * j + c
                w = []
                if k >= 2:
                    w.append(("ev", ev[k - 2]))
                mmk = S.op("tensor", lambda t, j=j, c=c, k=k: t.transpose(
                    pT[:, (k % 2) * 512:(k % 2) * 512 + 128],
                    x32[:, T * c + 128 * j:T * c + 128 * j + 128],
                    ident32[:]), waits=w, inc="mm")
                w3 = [("mm", mmk)]
                if j >= 2 and c == 0:
                    w3.append(("out", dmas[j - 2]))
                ev[k] = S.op("scalar", lambda sc, j=j, c=c, k=k: sc.activation(
                    xtok[:, (j % 2) * D + 128 * c:(j % 2) * D + 128 * c + 128],
                    pT[:, (k % 2) * 512:(k % 2) * 512 + 128], AF.Identity),
                    waits=w3, inc="ev")
            dmas[j] = S.op("sync", lambda s, j=j: s.dma_start(
                out=xout.ap()[b, 128 * j:128 * j + 128, :],
                in_=xtok[:, (j % 2) * D:(j % 2) * D + D]),
                waits=[("ev", ev[8 * j + NCH - 1])], inc="out", k=16)
        S.fence_all()
        S.emit()


def ln_boundary(st, tag, affine=None):
    S, nc = st["S"], st["nc"]
    x32, xbf, onesbf, ones32 = st["x32"], st["xbf"], st["onesbf"], st["ones32"]
    modify_x = affine is None
    S.fence_all()
    es = ExitStack()
    sbt = lambda n, s, d: es.enter_context(nc.sbuf_tensor(n + tag, s, d))
    pst = lambda n, s, d: es.enter_context(nc.psum_tensor(n + tag, s, d))
    u2 = [sbt("u2%d" % i, [128, T], BF16) for i in range(2)]
    rowA = sbt("rowA", [1, T], F32)
    rowB = sbt("rowB", [1, T], F32)
    if not modify_x:
        t1 = sbt("t1", [128, 1024], BF16)
    psr = pst("psr", [128, T], F32)
    psm = pst("psm", [128, T], F32)
    psS = psr[0:1, :]

    for c in range(NCH):
        for qg in range(QGN):
            S.op("tensor", lambda t, c=c, qg=qg: t.matmul(
                psS[0:1, 512 * qg:512 * qg + 512], ones32[:, 0:1],
                x32[:, T * c + 512 * qg:T * c + 512 * qg + 512],
                start=(c == 0), stop=(c == NCH - 1)),
                inc="mm" if (c == NCH - 1 and qg == QGN - 1) else None)
    vA = S.op("vector", lambda v: v.tensor_scalar(
        rowA[0:1, :], psS[0:1, :], 1.0 / D, None, ALU.mult),
        waits=[("mm", S.cnt["mm"])], inc="ve", chain=False)
    sq_ev = {}
    for c in range(NCH):
        w = [("mm", mm_u2[c - 2])] if c >= 2 else []
        sq_ev[c] = S.op("scalar", lambda sc, c=c: sc.activation(
            u2[c % 2][:, :], x32[:, T * c:T * c + T], AF.Square), waits=w, inc="ev")
        if c == 0:
            mm_u2 = {}
        for qg in range(QGN):
            mm_u2[c] = S.op("tensor", lambda t, c=c, qg=qg: t.matmul(
                psS[0:1, 512 * qg:512 * qg + 512], onesbf[:, 0:1],
                u2[c % 2][:, 512 * qg:512 * qg + 512],
                start=(c == 0), stop=(c == NCH - 1)),
                waits=[("ev", sq_ev[c])] + ([("ve", vA)] if c == 0 else [])
                if qg == 0 else (),
                inc="mm" if qg == QGN - 1 else None)
    eQ = S.op("scalar", lambda sc: sc.activation(
        rowB[0:1, :], rowA[0:1, :], AF.Square), waits=[("ve", vA)], inc="ev")
    vV = S.op("vector", lambda v: v.scalar_tensor_tensor(
        rowB[0:1, :], psS[0:1, :], 1.0 / D, rowB[0:1, :],
        ALU.mult, ALU.subtract),
        waits=[("mm", S.cnt["mm"]), ("ev", eQ)], inc="ve", chain=False)
    er = S.op("scalar", lambda sc: sc.activation(
        rowB[0:1, :], rowB[0:1, :], AF.Sqrt, bias=np.float32(1e-5)),
        waits=[("ve", vV)], inc="ev")
    vR = S.op("vector", lambda v: v.reciprocal(rowB[0:1, :], rowB[0:1, :]),
              waits=[("ev", er)], inc="ve", chain=False)
    rows_ve = S.op("vector", lambda v: v.tensor_tensor(
        rowA[0:1, :], rowA[0:1, :], rowB[0:1, :], ALU.mult),
        inc="ve", chain=vR)

    for qg in range(QGN):
        S.op("tensor", lambda t, qg=qg: t.matmul(
            psr[:, 512 * qg:512 * qg + 512], ones32[0:1, 0:128],
            rowB[0:1, 512 * qg:512 * qg + 512],
            start=True, stop=True),
            waits=[("ve", rows_ve)] if qg == 0 else ())
        S.op("tensor", lambda t, qg=qg: t.matmul(
            psm[:, 512 * qg:512 * qg + 512], ones32[0:1, 0:128],
            rowA[0:1, 512 * qg:512 * qg + 512],
            start=True, stop=True), inc="mm" if qg == QGN - 1 else None)
    bdone = S.cnt["mm"]
    if modify_x:
        mids = {}
        for c in range(NCH):
            mids[c] = S.op("vector", lambda v, c=c: v.tensor_tensor(
                x32[:, T * c:T * c + T], x32[:, T * c:T * c + T], psr[:, :],
                ALU.mult), waits=[("mm", bdone)] if c == 0 else (),
                inc="ve", chain=False)
        for c in range(NCH):
            sub = S.op("vector", lambda v, c=c: v.tensor_tensor(
                x32[:, T * c:T * c + T], x32[:, T * c:T * c + T], psm[:, :],
                ALU.subtract), inc="ve", chain=mids[c])
            S.op("scalar", lambda sc, c=c: sc.activation(
                xbf[:, T * c:T * c + T], x32[:, T * c:T * c + T], AF.Identity),
                waits=[("ve", sub)], inc="ev")
    else:
        wcol, bcol, dst = affine
        prev = False
        prev_ea = None
        for c in range(NCH):
            for ch in range(2):
                sl = slice(T * c + 1024 * ch, T * c + 1024 * ch + 1024)
                bsl = slice(1024 * ch, 1024 * ch + 1024)
                m1 = S.op("vector", lambda v, sl=sl, bsl=bsl: v.tensor_tensor(
                    t1[:, :], x32[:, sl], psr[:, bsl], ALU.mult),
                    waits=[("mm", bdone)] if prev_ea is None
                    else [("ev", prev_ea)],
                    inc="ve", chain=prev)
                w2 = S.op("vector", lambda v, bsl=bsl: v.tensor_tensor(
                    t1[:, :], t1[:, :], psm[:, bsl], ALU.subtract),
                    inc="ve", chain=m1)
                prev_ea = S.op("scalar", lambda sc, sl=sl, c=c: sc.activation(
                    dst[:, sl], t1[:, :], AF.Identity,
                    bias=bcol[:, c:c + 1], scale=wcol[:, c:c + 1]),
                    waits=[("ve", w2)], inc="ev")
                prev = w2
    S.fence_all()
    S.emit()
    es.close()


def bcalc(S, psp, pss, sc1, nB, q2, qbuf, kbuf, rows, scale, onesbf, ones32,
          asm_deps=()):
    banks = [(psp, 0), (psp, 512), (pss, 0), (pss, 512)]
    red = {}
    for i in range(2):
        src = qbuf if i == 0 else kbuf
        sq = S.op("scalar", lambda sc, src=src, rows=rows: sc.activation(
            q2[0:rows, :], src[0:rows, :], AF.Square),
            waits=list(asm_deps) if i == 0 else [("mm", S.cnt["mm"])], inc="ev")
        for qg in range(QGN):
            buf, off = banks[qg]
            w = []
            if qg == 0:
                w = [("ev", sq)]
                if i == 1:
                    w.append(("ve", red[1]))
            S.op("tensor", lambda t, qg=qg, buf=buf, off=off, rows=rows: t.matmul(
                buf[:, off:off + 512], onesbf[0:rows, 0:128],
                q2[0:rows, 512 * qg:512 * qg + 512], start=True, stop=True),
                waits=w, inc="mm" if qg == 3 else None)
        mq = S.cnt["mm"]
        red[2 * i] = S.op("vector", lambda v, i=i: v.reduce_max(
            sc1[:, 2 * i:2 * i + 1], psp[:, 0:1024], axis=AX.X),
            waits=[("mm", mq)], inc="ve", chain=False)
        red[2 * i + 1] = S.op("vector", lambda v, i=i: v.reduce_max(
            sc1[:, 2 * i + 1:2 * i + 2], pss[:, 0:1024], axis=AX.X),
            inc="ve", chain=False)
    redq = S.op("vector", lambda v: v.reduce_max(
        sc1[:, 4:5], sc1[:, 0:2], axis=AX.X), inc="ve", chain=red[3])
    redk = S.op("vector", lambda v: v.reduce_max(
        sc1[:, 5:6], sc1[:, 2:4], axis=AX.X), inc="ve", chain=False)
    vb = S.op("vector", lambda v: v.tensor_tensor(
        sc1[:, 6:7], sc1[:, 4:5], sc1[:, 5:6], ALU.mult),
        inc="ve", chain=redk)
    sq2 = S.op("scalar", lambda sc: sc.activation(
        sc1[:, 7:8], sc1[:, 6:7], AF.Sqrt), waits=[("ve", vb)], inc="ev")
    return S.op("scalar", lambda sc, scale=scale: sc.activation(
        nB[:, 0:1], sc1[:, 7:8], AF.Identity, bias=np.float32(50.0),
        scale=np.float32(-scale)), waits=[("ev", sq2)], inc="ev")


def bcalc2(S, psp, pssb, sc1, nB, stage, qbuf, kbuf, scale, onesbf, ones32,
           asm_deps=()):
    banks = [(psp, 0), (psp, 512), (pssb, 0), (pssb, 512)]
    red = {}
    for i in range(2):
        src = qbuf if i == 0 else kbuf
        sq = S.op("scalar", lambda sc, src=src: sc.activation(
            stage[:, :], src[:, 0:2048], AF.Square),
            waits=list(asm_deps) if i == 0 else [("mm", S.cnt["mm"])], inc="ev")
        for qg in range(QGN):
            buf, off = banks[qg]
            w = []
            if qg == 0:
                w = [("ev", sq)]
                if i == 1:
                    w.append(("ve", red[1]))
            S.op("tensor", lambda t, qg=qg, buf=buf, off=off: t.matmul(
                buf[:, off:off + 512], onesbf[:, 0:128],
                stage[:, 512 * qg:512 * qg + 512], start=True, stop=True),
                waits=w, inc="mm" if qg == 3 else None)
        mq = S.cnt["mm"]
        red[2 * i] = S.op("vector", lambda v, i=i: v.reduce_max(
            sc1[:, 2 * i:2 * i + 1], psp[:, 0:1024], axis=AX.X),
            waits=[("mm", mq)], inc="ve", chain=False)
        red[2 * i + 1] = S.op("vector", lambda v, i=i: v.reduce_max(
            sc1[:, 2 * i + 1:2 * i + 2], pssb[:, 0:1024], axis=AX.X),
            inc="ve", chain=False)
    redq = S.op("vector", lambda v: v.reduce_max(
        sc1[:, 4:5], sc1[:, 0:2], axis=AX.X), inc="ve", chain=red[3])
    redk = S.op("vector", lambda v: v.reduce_max(
        sc1[:, 5:6], sc1[:, 2:4], axis=AX.X), inc="ve", chain=False)
    vb = S.op("vector", lambda v: v.tensor_tensor(
        sc1[:, 6:7], sc1[:, 4:5], sc1[:, 5:6], ALU.mult),
        inc="ve", chain=redk)
    sq2 = S.op("scalar", lambda sc: sc.activation(
        sc1[:, 7:8], sc1[:, 6:7], AF.Sqrt), waits=[("ve", vb)], inc="ev")
    return S.op("scalar", lambda sc, scale=scale: sc.activation(
        nB[:, 0:1], sc1[:, 7:8], AF.Identity, bias=np.float32(50.0),
        scale=np.float32(-scale)), waits=[("ev", sq2)], inc="ev")



def phase_l0a(st, b):
    S, nc, W = st["S"], st["nc"], st["W"]
    x32, xbf, arena = st["x32"], st["xbf"], st["arena"]
    onesbf, ones32, maskc = st["onesbf"], st["ones32"], st["maskc"]
    ident32 = st["ident32"]
    es = ExitStack()
    sbt = lambda n, s, d: es.enter_context(nc.sbuf_tensor(n + "_l0a%d" % b, s, d))
    pst = lambda n, s, d: es.enter_context(nc.psum_tensor(n + "_l0a%d" % b, s, d))

    vall = sbt("vall", [128, TTN * 1040], BF16)
    qt = sbt("qt", [128, T], BF16)
    kt = sbt("kt", [128, T], BF16)
    ppb = sbt("ppb", [128, 2048], BF16)
    atte = sbt("atte", [128, 1024], F32)
    atteB = sbt("atteB", [128, 512], BF16)
    rrowB = sbt("rrowB", [128, 1024], BF16)
    dcolF = sbt("dcolF", [128, 8], F32)
    nB = sbt("nB", [128, 1], F32)
    sc1 = sbt("sc1", [128, 8], F32)
    wqk = [sbt("wqkA%d" % i, [128, 1024], BF16) for i in range(2)]
    wqk2 = [sbt("wqkB%d" % i, [128, 1024], BF16) for i in range(2)]
    wv = sbt("wv", [128, NCH * 512], BF16)
    bq = sbt("bq", [128, 16], F32)
    bvrow = sbt("bvrow", [1, D], BF16)
    borow = sbt("borow", [1, D], BF16)

    psp = pst("psp", [128, 1024], F32)
    pssb = pst("pssb", [128, 2048], F32)
    psab = pst("psab", [128, 1024], F32)

    S.fence_all()
    S.op("sync", lambda s: s.dma_start(
        out=bq[:, :], in_=W["l0a_bqkv"].ap()[0:2048].rearrange("(a b) -> b a", b=128),
        allow_slow_non_contiguous=True), inc="wq", k=16)
    bv_k = wdma(S, bvrow[0:1, :], W["l0a_bqkv"].ap()[2048:3072].unsqueeze(0))
    bias_wg = wdma(S, borow[0:1, :], W["l0a_bout"].ap().unsqueeze(0))
    bias_wq = S.cnt["wq"]

    vev = {}
    for vg in range(2):
        wvd = wdma(S, sb3(wv, 512), cc3(W["l0a_wqkv"].ap()[:, 2048 + 512 * vg:2560 + 512 * vg]),
                   waits=[("mm", S.cnt["mm"])] if vg == 1 else ())
        for tt in range(TTN):
            par = tt % 2
            idx = vg * TTN + tt
            for c in range(NCH):
                w = []
                if c == 0:
                    w = [wvd, bv_k, bias_wg, ("wq", bias_wq)]
                    if idx >= 2:
                        w.append(("ev", vev[idx - 2]))
                S.op("tensor", lambda t, tt=tt, c=c, par=par: t.matmul(
                    psp[:, 512 * par:512 * par + 512],
                    xbf[:, T * c + 128 * tt:T * c + 128 * tt + 128],
                    wv[:, 512 * c:512 * c + 512], start=(c == 0), stop=False),
                    waits=w)
            mmv = S.op("tensor", lambda t, vg=vg, par=par: t.matmul(
                psp[:, 512 * par:512 * par + 512], onesbf[0:1, 0:128],
                bvrow[0:1, 512 * vg:512 * vg + 512], start=False, stop=True),
                inc="mm")
            dst = vall[:].rearrange("p (t h n) -> p t h n", t=TTN, n=65)
            vev[idx] = S.op(
                "scalar", lambda sc, tt=tt, vg=vg, par=par, dst=dst: sc.activation(
                    dst[:, tt, 8 * vg:8 * vg + 8, 0:64],
                    psp[:, 512 * par:512 * par + 512].rearrange("p (h n) -> p h n", n=64),
                    AF.Identity),
                waits=[("mm", mmv)], inc="ev")
    ones_ve = S.op("vector", lambda v: v.memset(
        vall[:].rearrange("p (a n) -> p a n", n=65)[:, :, 64:65], 1.0),
        waits=[("ev", S.cnt["ev"])], inc="ve")

    def qkcol(p, which):
        return cc3(W["l0a_wqkv"].ap()[:, 1024 * which + 128 * p:1024 * which + 128 * p + 128])

    wqd = [wdma(S, sb3(wqk[0], 128), qkcol(0, 0), waits=[("mm", S.cnt["mm"])]),
           wdma(S, sb3(wqk[1], 128), qkcol(0, 1))]
    for p in range(8):
        cur = wqk if p % 2 == 0 else wqk2
        nxt = wqk2 if p % 2 == 0 else wqk
        wq_ready = wqd
        if p < 7:
            wqd = [wdma(S, sb3(nxt[0], 128), qkcol(p + 1, 0), waits=[("mm", S.cnt["mm"])]),
                   wdma(S, sb3(nxt[1], 128), qkcol(p + 1, 1))]

        evq = {}
        for which, dstbuf, bcol in ((0, qt, p), (1, kt, 8 + p)):
            for qh in range(2):
                for qg2 in range(2):
                    qg = 2 * qh + qg2
                    for c in range(NCH):
                        w = []
                        if qh == 0 and qg2 == 0 and c == 0:
                            w = [wq_ready[which], ("ev", S.cnt["ev"]),
                                 ("ve", S.cnt["ve"])]
                        elif qg2 == 0 and c == 0 and qh == 1:
                            w = [("ev", evq[(which, 0)])]
                        S.op("tensor", lambda t, qg=qg, qg2=qg2, c=c, cb=cur[which]: t.matmul(
                            psp[:, 512 * qg2:512 * qg2 + 512],
                            cb[:, 128 * c:128 * c + 128],
                            xbf[:, T * c + 512 * qg:T * c + 512 * qg + 512],
                            start=(c == 0), stop=(c == NCH - 1)),
                            waits=w, inc="mm" if (qg2 == 1 and c == NCH - 1) else None)
                evq[(which, qh)] = S.op(
                    "scalar", lambda sc, dstbuf=dstbuf, bcol=bcol, qh=qh: sc.activation(
                        dstbuf[:, 1024 * qh:1024 * qh + 1024], psp[:, :], AF.Identity,
                        bias=bq[:, bcol:bcol + 1]),
                    waits=[("mm", S.cnt["mm"])], inc="ev")
        ebn = bcalc2(S, psp, pssb, sc1, nB, ppb, qt, kt, 0.125, onesbf, ones32,
                     asm_deps=[("ev", evq[(1, 1)])])

        hA, hB = 2 * p, 2 * p + 1
        prev_tail = dict(ev=S.cnt["ev"], ve=max(S.cnt["ve"], ones_ve),
                         gv=S.cnt["gv"], mask=0, e2=0, tt=S.cnt["ve"])
        for qg in range(QGN):
            KT = 4 * (qg + 1)
            expB = {}
            maskve = {}
            deps = {}

            def emit_att(kti, par, KT=None):
                dep = deps[kti]
                S.op("tensor", lambda t, kti=kti, par=par, hA=hA, KT=KT: t.matmul(
                    psab[0:65, 0:512],
                    vall[:, 1040 * kti + 65 * hA:1040 * kti + 65 * hA + 65],
                    ppb[:, 1024 * par:1024 * par + 512],
                    start=(kti == 0), stop=(kti == KT - 1)), waits=dep)
                return S.op("tensor", lambda t, kti=kti, par=par, hB=hB, KT=KT: t.matmul(
                    psab[0:65, 512:1024],
                    vall[:, 1040 * kti + 65 * hB:1040 * kti + 65 * hB + 65],
                    ppb[:, 1024 * par + 512:1024 * par + 1024],
                    start=(kti == 0), stop=(kti == KT - 1)), inc="mm")

            for kti in range(KT):
                par = kti % 2
                w = []
                if kti == 0:
                    w = [("ev", max(ebn, prev_tail["ev"]))]
                if kti >= 2:
                    w.append(("ev", expB[kti - 2]))
                S.op("tensor", lambda t, qg=qg, kti=kti, par=par: t.matmul(
                    pssb[:, 1024 * par:1024 * par + 512],
                    kt[0:64, 128 * kti:128 * kti + 128],
                    qt[0:64, 512 * qg:512 * qg + 512], start=True, stop=True,
                    tile_position=(0, 0)), waits=w)
                sB = S.op("tensor", lambda t, qg=qg, kti=kti, par=par: t.matmul(
                    pssb[:, 1024 * par + 512:1024 * par + 1024],
                    kt[64:128, 128 * kti:128 * kti + 128],
                    qt[64:128, 512 * qg:512 * qg + 512], start=True, stop=True,
                    tile_position=(64, 0)), inc="mm")
                ew = [("mm", sB), ("ev", ebn)]
                if kti == 0 and prev_tail.get("mask"):
                    ew.append(("ve", prev_tail["mask"]))
                if kti - 2 in maskve:
                    ew.append(("ve", maskve[kti - 2]))
                eB = S.op("scalar", lambda sc, kti=kti, par=par: sc.activation(
                    ppb[:, 1024 * par:1024 * par + 1024],
                    pssb[:, 1024 * par:1024 * par + 1024],
                    AF.Exp, bias=nB[:, 0:1], scale=np.float32(0.125)),
                    waits=ew, inc="ev")
                expB[kti] = eB
                dep = [("ev", eB)]
                if kti >= KT - 4:
                    o = kti - (KT - 4)
                    S.op("vector", lambda v, par=par, o=o: v.tensor_tensor(
                        ppb[:, 1024 * par:1024 * par + 512],
                        ppb[:, 1024 * par:1024 * par + 512],
                        maskc[:, 512 * o:512 * o + 512], ALU.mult),
                        waits=[("ev", eB)], inc="ve", chain=False)
                    mB = S.op("vector", lambda v, par=par, o=o: v.tensor_tensor(
                        ppb[:, 1024 * par + 512:1024 * par + 1024],
                        ppb[:, 1024 * par + 512:1024 * par + 1024],
                        maskc[:, 512 * o:512 * o + 512], ALU.mult),
                        inc="ve", chain=False)
                    maskve[kti] = mB
                    dep = [("ve", mB)]
                if kti == 0 and prev_tail.get("e2"):
                    dep = dep + [("ev", prev_tail["e2"])]
                deps[kti] = dep
                if kti >= 1:
                    am = emit_att(kti - 1, (kti - 1) % 2, KT)
            am = emit_att(KT - 1, (KT - 1) % 2, KT)
            e2 = S.op("scalar", lambda sc: sc.activation(
                atte[0:65, 0:1024], psab[0:65, 0:1024], AF.Identity),
                waits=[("mm", am), ("ve", prev_tail["tt"])], inc="ev")
            dts = {}
            for j in range(8):
                dts[j] = S.op("tensor", lambda t, j=j: t.transpose(
                    psp[:, j:j + 1], atte[64:65, 128 * j:128 * j + 128],
                    ident32[64:65, 64:65], tile_position=(64, 0)),
                    waits=[("ev", e2)] if j == 0 else (), inc="mm")
            r2 = S.op("vector", lambda v: v.reciprocal(
                dcolF[:, 0:8], psp[:, 0:8]),
                waits=[("mm", dts[7])], inc="ve", chain=False)
            for j in range(8):
                dts[j] = S.op("tensor", lambda t, j=j: t.transpose(
                    psp[0:1, 128 * j:128 * j + 128], dcolF[:, j:j + 1],
                    ident32[:]),
                    waits=[("ve", r2)] if j == 0 else (), inc="mm")
            erow = S.op("scalar", lambda sc: sc.activation(
                rrowB[0:1, 0:1024], psp[0:1, 0:1024], AF.Identity),
                waits=[("mm", dts[7])], inc="ev")
            S.op("tensor", lambda t: t.matmul(
                psp[0:64, 0:512], onesbf[0:1, 0:64], rrowB[0:1, 0:512],
                start=True, stop=True),
                waits=[("ev", erow)])
            b2m = S.op("tensor", lambda t: t.matmul(
                psp[0:64, 512:1024], onesbf[0:1, 0:64],
                rrowB[0:1, 512:1024], start=True, stop=True), inc="mm")
            S.op("vector", lambda v, p=p, qg=qg: v.tensor_tensor(
                arena[0:64, T * p + 512 * qg:T * p + 512 * qg + 512],
                atte[0:64, 0:512], psp[0:64, 0:512], ALU.mult),
                waits=[("mm", b2m)], inc="ve", chain=False)
            vB = S.op("vector", lambda v, p=p: v.tensor_tensor(
                atteB[0:64, :], atte[0:64, 512:1024],
                psp[0:64, 512:1024], ALU.mult),
                waits=[("gv", prev_tail["gv"])], inc="ve", chain=False)
            dB = S.op("sync", lambda g, p=p, qg=qg: g.dma_start(
                out=arena[64:128, T * p + 512 * qg:T * p + 512 * qg + 512],
                in_=atteB[0:64, :]), waits=[("ve", vB)], inc="gv", k=16)
            prev_tail = dict(ev=expB[KT - 1], ve=S.cnt["ve"], gv=dB,
                             mask=maskve[KT - 1], e2=e2, tt=S.cnt["ve"])

    S.fence_all()
    wo = [wqk[0], wqk2[0]]
    wod = wdma(S, sb3(wo[0], 128), cc3(W["l0a_wout"].ap()[:, 0:128]))
    vres = {}
    for ft in range(NCH):
        cur = wo[ft % 2]
        wcur = wod
        if ft < 7:
            wod = wdma(S, sb3(wo[(ft + 1) % 2], 128),
                       cc3(W["l0a_wout"].ap()[:, 128 * (ft + 1):128 * (ft + 1) + 128]),
                       waits=[("mm", S.cnt["mm"])])
        for qg in range(QGN):
            k = 4 * ft + qg
            par = k % 2
            w = []
            if qg == 0:
                w.append(wcur)
            if k >= 2:
                w.append(("ve", vres[k - 2]))
            for c in range(NCH):
                S.op("tensor", lambda t, qg=qg, c=c, cur=cur, par=par: t.matmul(
                    psp[:, 512 * par:512 * par + 512], cur[:, 128 * c:128 * c + 128],
                    arena[:, T * c + 512 * qg:T * c + 512 * qg + 512],
                    start=(c == 0), stop=False), waits=w if c == 0 else ())
            mo = S.op("tensor", lambda t, ft=ft, par=par: t.matmul(
                psp[:, 512 * par:512 * par + 512], borow[0:1, 128 * ft:128 * ft + 128],
                onesbf[0:1, 0:512], start=False, stop=True), inc="mm")
            vres[k] = S.op("vector", lambda v, ft=ft, qg=qg, par=par: v.tensor_tensor(
                x32[:, T * ft + 512 * qg:T * ft + 512 * qg + 512],
                psp[:, 512 * par:512 * par + 512],
                x32[:, T * ft + 512 * qg:T * ft + 512 * qg + 512], ALU.add),
                waits=[("mm", mo)], inc="ve")
    S.fence_all()
    S.emit()
    es.close()
    ln_boundary(st, "_l0a%d" % b)



def phase_l0f(st, b):
    S, nc, W = st["S"], st["nc"], st["W"]
    x32, xbf, arena = st["x32"], st["xbf"], st["arena"]
    onesbf, ones32, identbf = st["onesbf"], st["ones32"], st["identbf"]
    es = ExitStack()
    sbt = lambda n, s, d: es.enter_context(nc.sbuf_tensor(n + "_l0f%d" % b, s, d))
    pst = lambda n, s, d: es.enter_context(nc.psum_tensor(n + "_l0f%d" % b, s, d))

    gwb = sbt("gwb", [128, 64], BF16)
    rsc = sbt("rsc", [128, 1], F32)
    gateT = sbt("gateT", [8, T], BF16)
    grows = [sbt("grows%d" % i, [128, T], BF16) for i in range(2)]
    esg = ExitStack()
    sbg = lambda n, sh, d: esg.enter_context(nc.sbuf_tensor(n + "_l0fg%d" % b, sh, d))
    gs = sbg("gs", [128, 128], F32)
    m1s = sbg("m1s", [128, 16], F32)
    m2s = sbg("m2s", [128, 16], F32)
    d8 = sbg("d8", [128, 128], F32)
    ind1 = sbg("ind1", [128, 128], F32)
    xp8 = sbg("xp8", [128, 128], F32)
    ind2 = sbg("ind2", [128, 128], F32)
    gfull = xp8
    rden = sbg("rden", [128, 16], F32)
    gfb = sbg("gfb", [128, 128], BF16)
    psg = esg.enter_context(nc.psum_tensor("psg_l0f%d" % b, [128, 1024], F32))
    psgt = esg.enter_context(nc.psum_tensor("psgt_l0f%d" % b, [8, 2048], BF16))


    S.fence_all()
    S.op("sync", lambda s: s.dma_start(
        out=rsc[:, :], in_=W["l0f_rs"].ap().to_broadcast((128, 1))), inc="wq", k=16)
    ld0 = wdma(S, sb3(gwb, 8), cc3(W["l0f_gw"].ap()))
    wq0 = S.cnt["wq"]

    gev = {}
    for tt in range(TTN):
        par = tt % 2
        for c in range(NCH):
            w = []
            if tt == 0 and c == 0:
                w = [ld0, ("wq", wq0)]
            if c == 0 and tt >= 2:
                w.append(("ev", gev[tt - 2]))
            S.op("tensor", lambda t, tt=tt, c=c, par=par: t.matmul(
                psg[:, 512 * par:512 * par + 8],
                xbf[:, T * c + 128 * tt:T * c + 128 * tt + 128],
                gwb[:, 8 * c:8 * c + 8], start=(c == 0), stop=(c == NCH - 1)),
                waits=w, inc="mm" if c == NCH - 1 else None)
        gev[tt] = S.op("scalar", lambda sc, tt=tt, par=par: sc.activation(
            gs[:, 8 * tt:8 * tt + 8], psg[:, 512 * par:512 * par + 8], AF.Sigmoid),
            waits=[("mm", S.cnt["mm"])], inc="ev")
    S.op("vector", lambda v: v.tensor_scalar(gs[:, :], gs[:, :], rsc[:, 0:1],
                                             None, ALU.mult),
         waits=[("ev", S.cnt["ev"]), ("wq", wq0)], inc="ve")
    for tt in range(TTN):
        S.op("vector", lambda v, tt=tt: v.reduce_max(
            m1s[:, tt:tt + 1], gs[:, 8 * tt:8 * tt + 8], axis=AX.X),
            inc="ve" if tt == TTN - 1 else None)
    g1 = S.cnt["ve"]
    for tt in range(TTN):
        S.op("vector", lambda v, tt=tt: v.tensor_scalar(
            d8[:, 8 * tt:8 * tt + 8], gs[:, 8 * tt:8 * tt + 8],
            m1s[:, tt:tt + 1], 1e15, ALU.subtract, ALU.mult),
            waits=[("ve", g1)] if tt == 0 else (),
            inc="ve" if tt == TTN - 1 else None)
    e1 = S.op("scalar", lambda sc: sc.activation(ind1[:, :], d8[:, :], AF.Exp),
              waits=[("ve", S.cnt["ve"])], inc="ev")
    S.op("vector", lambda v: v.scalar_tensor_tensor(
        xp8[:, :], ind1[:, :], -1e22, gs[:, :], ALU.mult, ALU.add),
        waits=[("ev", e1)], inc="ve")
    for tt in range(TTN):
        S.op("vector", lambda v, tt=tt: v.reduce_max(
            m2s[:, tt:tt + 1], xp8[:, 8 * tt:8 * tt + 8], axis=AX.X),
            inc="ve" if tt == TTN - 1 else None)
    g2 = S.cnt["ve"]
    for tt in range(TTN):
        S.op("vector", lambda v, tt=tt: v.tensor_scalar(
            d8[:, 8 * tt:8 * tt + 8], xp8[:, 8 * tt:8 * tt + 8],
            m2s[:, tt:tt + 1], 1e15, ALU.subtract, ALU.mult),
            waits=[("ve", g2)] if tt == 0 else (),
            inc="ve" if tt == TTN - 1 else None)
    e2 = S.op("scalar", lambda sc: sc.activation(ind2[:, :], d8[:, :], AF.Exp),
              waits=[("ve", S.cnt["ve"])], inc="ev")
    S.op("vector", lambda v: v.tensor_tensor(ind1[:, :], ind1[:, :], ind2[:, :],
                                             ALU.add),
         waits=[("ev", e2)], inc="ve")
    for tt in range(TTN):
        S.op("vector", lambda v, tt=tt: v.tensor_scalar(
            rden[:, tt:tt + 1], m1s[:, tt:tt + 1], m2s[:, tt:tt + 1], None, ALU.add),
            inc="ve" if tt == TTN - 1 else None)
    vpre = S.cnt["ve"]
    S.op("vector", lambda v: v.reciprocal(rden[:, :], rden[:, :]),
         waits=[("ve", vpre)], inc="ve")
    g3 = S.op("vector", lambda v: v.tensor_tensor(gfull[:, :], gs[:, :], ind1[:, :],
                                                  ALU.mult), inc="ve")
    for tt in range(TTN):
        S.op("vector", lambda v, tt=tt: v.tensor_scalar(
            gfull[:, 8 * tt:8 * tt + 8], gfull[:, 8 * tt:8 * tt + 8],
            rden[:, tt:tt + 1], None, ALU.mult),
            waits=[("ve", g3)] if tt == 0 else (),
            inc="ve" if tt == TTN - 1 else None)
    vgf = S.op("vector", lambda v: v.tensor_copy(gfb[:, :], gfull[:, :]),
               waits=[("ve", S.cnt["ve"])], inc="ve")
    gtev = {}
    for tt in range(TTN):
        par = tt % 2
        w = [("ve", vgf)] if tt == 0 else []
        if tt >= 2:
            w.append(("ev", gtev[tt - 2]))
        mt = S.op("tensor", lambda t, tt=tt, par=par: t.transpose(
            psgt[0:8, 1024 * par:1024 * par + 128], gfb[:, 8 * tt:8 * tt + 8],
            identbf[:]), waits=w, inc="mm")
        gtev[tt] = S.op("scalar", lambda sc, tt=tt, par=par: sc.activation(
            gateT[0:8, 128 * tt:128 * tt + 128], psgt[0:8, 1024 * par:1024 * par + 128],
            AF.Identity), waits=[("mm", mt)], inc="ev")
    grow_wg = None
    for e in range(8):
        grow_wg = S.op("gpsimd", lambda g, e=e: g.dma_start(
            out=grows[e // 4][32 * (e % 4):32 * (e % 4) + 1, :],
            in_=gateT[e:e + 1, :]),
            waits=[("ev", S.cnt["ev"])] if e == 0 else (), inc="wg", k=16)

    S.emit()
    esg.close()
    hext = sbt("hext", [128, 20 * 1024], BF16)
    w1b = [sbt("w1b%d" % i, [128, NCH * 256], BF16) for i in range(2)]
    w2b = [sbt("w2b%d" % i, [128, 4 * 128], BF16) for i in range(2)]
    b1c = sbt("b1c", [128, 4], F32)
    b2sb = sbt("b2sb", [8, D], BF16)
    bs2row = sbt("bs2row", [1, D], BF16)
    psh = pst("psh", [128, 1024], F32)
    psgb = pst("psgb", [128, 1024], F32)
    psy = pst("psy", [128, 1024], F32)

    def hslot(e, hc):
        slot = 4 * e + hc
        if slot < 16:
            return arena[:, 1024 * slot:1024 * slot + 1024]
        return hext[:, 1024 * (slot - 16):1024 * (slot - 16) + 1024]

    b2k = wdma(S, b2sb[:, :], W["l0f_b2"].ap())
    b2_wg = wdma(S, bs2row[0:1, :], W["l0f_bs2"].ap().unsqueeze(0))

    def w1src(e, half):
        wsl = W["l0f_w1"].ap()[e] if e < 8 else W["l0f_ws1"].ap()
        return cc3(wsl[:, 256 * half:256 * half + 256])

    for th in range(2):
        w1d_next = wdma(S, sb3(w1b[0], 256), w1src(0, 0),
                        waits=[("mm", S.cnt["mm"]), ("ve", S.cnt["ve"]),
                               ("ev", S.cnt["ev"])])
        gb = None
        hdone = {}
        for e in range(9):
            bsrc = bias_col(W["l0f_b1"].ap()[e]) if e < 8 else bias_col(W["l0f_bs1"].ap())
            b1d = S.op("sync", lambda s, bsrc=bsrc: s.dma_start(
                out=b1c[:, :], in_=bsrc, allow_slow_non_contiguous=True),
                waits=[("ev", S.cnt["ev"])], inc="wq", k=16)
            if e < 8:
                S.op("tensor", lambda t, e=e, th=th: t.matmul(
                    psgb[:, 0:512], onesbf[32 * (e % 4):32 * (e % 4) + 1, 0:128],
                    grows[e // 4][32 * (e % 4):32 * (e % 4) + 1,
                                  1024 * th:1024 * th + 512],
                    start=True, stop=True, tile_position=(32 * (e % 4), 0)),
                    waits=[("wg", grow_wg), ("ve", S.cnt["ve"])])
                gb = S.op("tensor", lambda t, e=e, th=th: t.matmul(
                    psgb[:, 512:1024], onesbf[32 * (e % 4):32 * (e % 4) + 1, 0:128],
                    grows[e // 4][32 * (e % 4):32 * (e % 4) + 1,
                                  1024 * th + 512:1024 * th + 1024],
                    start=True, stop=True, tile_position=(32 * (e % 4), 0)), inc="mm")
            for half in range(2):
                wcur = w1d_next
                cbuf = w1b[(2 * e + half) % 2]
                if not (e == 8 and half == 1):
                    ne, nh = (e, 1) if half == 0 else (e + 1, 0)
                    w1d_next = wdma(S, sb3(w1b[(2 * e + half + 1) % 2], 256),
                                    w1src(ne, nh), waits=[("mm", S.cnt["mm"])])
                for hc in (2 * half, 2 * half + 1):
                    for qg in range(2):
                        for c in range(NCH):
                            w = []
                            if qg == 0 and c == 0:
                                w = [wcur, ("wq", b1d), ("ev", S.cnt["ev"])]
                            S.op("tensor", lambda t, hc=hc, qg=qg, c=c, cbuf=cbuf,
                                 half=half, th=th: t.matmul(
                                psh[:, 512 * qg:512 * qg + 512],
                                cbuf[:, 256 * c + 128 * (hc % 2):256 * c + 128 * (hc % 2) + 128],
                                xbf[:, T * c + 1024 * th + 512 * qg:
                                    T * c + 1024 * th + 512 * qg + 512],
                                start=(c == 0), stop=(c == NCH - 1)),
                                waits=w, inc="mm" if (qg == 1 and c == NCH - 1) else None)
                    he = S.op("scalar", lambda sc, e=e, hc=hc: sc.activation(
                        hslot(e, hc), psh[:, :], AF.Gelu, bias=b1c[:, hc:hc + 1]),
                        waits=[("mm", S.cnt["mm"])], inc="ev")
                    if e < 8:
                        hdone[(e, hc)] = S.op("vector", lambda v, e=e, hc=hc: v.tensor_tensor(
                            hslot(e, hc), hslot(e, hc), psgb[:, :], ALU.mult),
                            waits=[("ev", he), ("mm", gb)], inc="ve")
                    else:
                        hdone[(e, hc)] = S.cnt["ve"]
        hready = dict(ve=S.cnt["ve"], ev=S.cnt["ev"])

        def w2src(e, ft):
            if e < 8:
                return cc3(W["l0f_w2"].ap()[e][:, 128 * ft:128 * ft + 128])
            return cc3(W["l0f_ws2"].ap()[:, 128 * ft:128 * ft + 128])

        w2d_next = wdma(S, sb3(w2b[0], 128), w2src(0, 0), waits=[("mm", S.cnt["mm"])])
        vres = {}
        for ft in range(NCH):
            for e in range(9):
                idx = ft * 9 + e
                wcur = w2d_next
                cbuf = w2b[idx % 2]
                pre_mm = S.cnt["mm"]
                for qg in range(2):
                    for hc in range(4):
                        w = []
                        if e == 0 and hc == 0:
                            w = [wcur, ("ve", hready["ve"]), ("ev", hready["ev"])]
                            if ft >= 1 and qg == 0:
                                w.append(("ve", vres[(ft - 1, 1)]))
                        elif hc == 0 and qg == 0:
                            w = [wcur]
                        last = (qg == 1 and hc == 3)
                        S.op("tensor", lambda t, e=e, hc=hc, qg=qg, cbuf=cbuf: t.matmul(
                            psy[:, 512 * qg:512 * qg + 512],
                            cbuf[:, 128 * hc:128 * hc + 128],
                            hslot(e, hc)[:, 512 * qg:512 * qg + 512],
                            start=(e == 0 and hc == 0), stop=False), waits=w,
                            inc="mm" if last else None)
                nft, ne = (ft, e + 1) if e < 8 else (ft + 1, 0)
                if nft < NCH:
                    w2d_next = wdma(S, sb3(w2b[(idx + 1) % 2], 128), w2src(ne, nft),
                                    waits=[("mm", pre_mm)])
            for qg in range(2):
                S.op("tensor", lambda t, ft=ft, th=th, qg=qg: t.matmul(
                    psy[:, 512 * qg:512 * qg + 512], b2sb[0:8, 128 * ft:128 * ft + 128],
                    gateT[0:8, 1024 * th + 512 * qg:1024 * th + 512 * qg + 512],
                    start=False, stop=False),
                    waits=[b2k] if (ft == 0 and qg == 0) else ())
                my = S.op("tensor", lambda t, ft=ft, qg=qg: t.matmul(
                    psy[:, 512 * qg:512 * qg + 512], bs2row[0:1, 128 * ft:128 * ft + 128],
                    onesbf[0:1, 0:512], start=False, stop=True),
                    waits=[b2_wg] if (ft == 0 and qg == 0) else (), inc="mm")
                vres[(ft, qg)] = S.op("vector", lambda v, ft=ft, th=th, qg=qg: v.tensor_tensor(
                    x32[:, T * ft + 1024 * th + 512 * qg:T * ft + 1024 * th + 512 * qg + 512],
                    psy[:, 512 * qg:512 * qg + 512],
                    x32[:, T * ft + 1024 * th + 512 * qg:T * ft + 1024 * th + 512 * qg + 512],
                    ALU.add), waits=[("mm", my)], inc="ve")
        S.fence_all()
    S.emit()
    es.close()
    ln_boundary(st, "_l0f%d" % b)



def phase_l1a(st, b):
    S, nc, W = st["S"], st["nc"], st["W"]
    x32, xbf, arena = st["x32"], st["xbf"], st["arena"]
    onesbf, ones32, maskc, identbf = st["onesbf"], st["ones32"], st["maskc"], st["identbf"]
    pswap, CD = st["pswap"], st["CD"]
    es = ExitStack()
    sbt = lambda n, s, d: es.enter_context(nc.sbuf_tensor(n + "_l1a%d" % b, s, d))
    pst = lambda n, s, d: es.enter_context(nc.psum_tensor(n + "_l1a%d" % b, s, d))
    SC = 1.0 / math.sqrt(96.0)
    ropeC = sbt("ropeC", [128, T], BF16)
    ropeS = sbt("ropeS", [128, T], BF16)

    latT = sbt("latT", [128, 2 * T], BF16)
    qrp = [sbt("qrp%d" % i, [128, T], BF16) for i in range(2)]
    qcp = sbt("qcp", [128, T], BF16)
    kcp = sbt("kcp", [128, T], BF16)
    qasm = [sbt("qasm%d" % i, [128, T], BF16) for i in range(2)]
    kasm = [sbt("kasm%d" % i, [128, T], BF16) for i in range(2)]
    vaug = [sbt("vaug%d" % i, [128, TTN * 65], BF16) for i in range(2)]
    mega = sbt("mega", [128, 16 * 512], BF16)
    ppr = mega
    q2full = nc.virtual_sbuf_tensor("q2_l1a%d" % b, [128, 16 * 512], BF16)
    nc.back_virtual_tensor_with(q2full, mega)
    q2 = q2full[:, 0:T]
    krT = q2full[:, T:2 * T]
    macc = sbt("macc", [128, 1536], BF16)
    pmaxs = sbt("pmaxs", [128, 4], BF16)
    scr = sbt("scr", [128, 512], BF16)
    dcol = sbt("dcol", [128, 4], BF16)
    atteS = sbt("atteS", [128, 512], BF16)
    scrB = atteS[64:128, :]
    nB = sbt("nB", [128, 1], F32)
    ccol = sbt("ccol", [128, 16], F32)
    lrow = sbt("lrow", [1, 16], F32)
    sc1 = sbt("sc1", [128, 8], F32)
    wtv = nc.virtual_sbuf_tensor("wtv_l1a%d" % b, [128, 16 * 512], BF16)
    nc.back_virtual_tensor_with(wtv, mega)
    wt = [wtv[:, 4096:5120], wtv[:, 5120:6144]]
    wkv = wtv[:, 6144:6656]

    pss = pst("pss", [128, 2048], F32)
    psp = pss[:, 0:1024]
    psmt = pst("psmt", [128, 1024], BF16)
    psatF = pst("psatF", [128, 512], F32)
    prow = pst("prow", [128, 512], BF16)
    psbc = pst("psbc", [128, 512], F32)

    S.fence_all()
    S.op("sync", lambda s: s.dma_start(out=ropeC[:], in_=CD["ropeC"].ap()), inc="wq", k=16)
    rop_wq = S.op("sync", lambda s: s.dma_start(out=ropeS[:], in_=CD["ropeS"].ap()),
                  inc="wq", k=16)
    ldl = S.op("sync", lambda s: s.dma_start(out=lrow[0:1, :], in_=W["l1a_lobo"].ap()),
               inc="wq", k=16)
    el = S.op("scalar", lambda sc: sc.activation(lrow[0:1, :], lrow[0:1, :], AF.Exp),
              waits=[("wq", ldl)], inc="ev")
    ml = S.op("tensor", lambda t: t.matmul(psp[:, 0:16], ones32[0:1, 0:128], lrow[0:1, :],
                                           start=True, stop=True),
              waits=[("ev", el)], inc="mm")
    ecc = S.op("scalar", lambda sc: sc.activation(ccol[:, :], psp[:, 0:16], AF.Identity),
               waits=[("mm", ml)], inc="ev")

    def proj_fm(dst, wbuf, nchunks, src, srcT, evwaits, tag):
        evs = []
        for qh in range(2):
            for qg2 in range(2):
                qg = 2 * qh + qg2
                for c in range(nchunks):
                    w = []
                    if qh == 0 and qg2 == 0 and c == 0:
                        w = list(evwaits)
                    elif qh == 1 and qg2 == 0 and c == 0:
                        w = [("ev", evs[0])]
                    S.op("tensor", lambda t, qg=qg, qg2=qg2, c=c: t.matmul(
                        psp[:, 512 * qg2:512 * qg2 + 512],
                        wbuf[:, 128 * c:128 * c + 128],
                        src[:, srcT * c + 512 * qg:srcT * c + 512 * qg + 512],
                        start=(c == 0), stop=(c == nchunks - 1)),
                        waits=w, inc="mm" if (qg2 == 1 and c == nchunks - 1) else None)
            evs.append(S.op("scalar", lambda sc, qh=qh: sc.activation(
                dst[:, 1024 * qh:1024 * qh + 1024], psp[:, :], AF.Identity),
                waits=[("mm", S.cnt["mm"])], inc="ev"))
        return evs[-1]

    for lc in range(2):
        wd = wdma(S, sb3(wt[lc], 128), cc3(W["l1a_wkvd"].ap()[:, 128 * lc:128 * lc + 128]),
                  waits=[("mm", S.cnt["mm"])] if lc == 0 else ())
        proj_fm(_slice2(latT, T * lc, T),
                wt[lc], NCH, xbf, T,
                [wd, ("ev", max(ecc, S.cnt["ev"]))], "lat%d" % lc)
    wd = wdma(S, sb3(wt[0], 32)[:, 0:8, :], cc3(W["l1a_wkr"].ap()),
              waits=[("ev", S.cnt["ev"])])
    krev = {}
    for qg in range(QGN):
        for c in range(NCH):
            w = []
            if qg == 0 and c == 0:
                w = [wd]
            elif qg >= 2 and c == 0:
                w = [("ev", krev[qg - 2])]
            S.op("tensor", lambda t, qg=qg, c=c: t.matmul(
                psp[0:32, 512 * (qg % 2):512 * (qg % 2) + 512], wt[0][:, 32 * c:32 * c + 32],
                xbf[:, T * c + 512 * qg:T * c + 512 * qg + 512],
                start=(c == 0), stop=(c == NCH - 1)), waits=w,
                inc="mm" if c == NCH - 1 else None)
        krev[qg] = S.op("scalar", lambda sc, qg=qg: sc.activation(
            krT[0:32, 512 * qg:512 * qg + 512], psp[0:32, 512 * (qg % 2):512 * (qg % 2) + 512],
            AF.Identity), waits=[("mm", S.cnt["mm"])], inc="ev")
    ekr = S.cnt["ev"]

    def rope2(srcbuf, rows, evwait):
        lastve = None
        for qg in range(QGN):
            half = qg % 2
            mm1 = S.op("tensor", lambda t, qg=qg, half=half: t.matmul(
                psp[0:rows, 512 * half:512 * half + 512], pswap[0:rows, 0:rows],
                srcbuf[0:rows, 512 * qg:512 * qg + 512], start=True, stop=True),
                waits=([("ev", evwait), ("wq", rop_wq)] if qg == 0 else []) +
                      ([("ve", lastve)] if qg >= 2 else []), inc="mm")
            S.op("vector", lambda v, qg=qg, half=half: v.tensor_tensor(
                q2[0:rows, 512 * half:512 * half + 512],
                psp[0:rows, 512 * half:512 * half + 512],
                ropeS[0:rows, 512 * qg:512 * qg + 512], ALU.mult),
                waits=[("mm", mm1)], inc="ve")
            S.op("vector", lambda v, qg=qg: v.tensor_tensor(
                srcbuf[0:rows, 512 * qg:512 * qg + 512],
                srcbuf[0:rows, 512 * qg:512 * qg + 512],
                ropeC[0:rows, 512 * qg:512 * qg + 512], ALU.mult), inc="ve")
            lastve = S.op("vector", lambda v, qg=qg, half=half: v.tensor_tensor(
                srcbuf[0:rows, 512 * qg:512 * qg + 512],
                srcbuf[0:rows, 512 * qg:512 * qg + 512],
                q2[0:rows, 512 * half:512 * half + 512], ALU.add), inc="ve")
        return lastve

    kr_ve = rope2(krT, 32, ekr)
    krd0 = S.op("sync", lambda g: g.dma_start(
        out=kasm[0][64:96, :], in_=krT[0:32, :]), waits=[("ve", kr_ve)],
        inc="gv", k=16)
    krd1 = S.op("sync", lambda g: g.dma_start(
        out=kasm[1][64:96, :], in_=krT[0:32, :]), inc="gv", k=16)

    qr_ve = {}

    def make_qr_pack(pk, buf):
        wd = wdma(S, sb3(wt[1], 128), cc3(W["l1a_wqr"].ap()[:, 128 * pk:128 * pk + 128]),
                  waits=[("mm", S.cnt["mm"]), ("ve", S.cnt["ve"])])
        ev = proj_fm(buf, wt[1], NCH, xbf, T,
                     [wd, ("ve", S.cnt["ve"]), ("ev", S.cnt["ev"])], "qr")
        return rope2(buf, 128, ev)

    qr_ve[0] = make_qr_pack(0, qrp[0])

    prev_eex = None
    prev_e4 = None
    prev_att = 0
    prev_vm = None
    prev_vma = None
    prev_KT = 4
    prev_evat = 0
    prev_tt = 0
    prev_trans_mm = 0
    prev_odma = 0
    for h in range(16):
        pr = h // 2
        pk = h // 4
        am = h % 2
        if h % 2 == 0:
            wv_pre = ([("mm", prev_att), ("ve", prev_vm)] if prev_vm is not None
                      else [("mm", S.cnt["mm"]), ("ve", S.cnt["ve"])])
            wd = wdma(S, sb3(wt[0], 128), cc3(W["l1a_wq"].ap()[:, 128 * pr:128 * pr + 128]),
                      waits=list(wv_pre))
            eqc = proj_fm(qcp, wt[0], NCH, xbf, T,
                          [wd, ("ev", S.cnt["ev"])], "qc")
            wd2 = wdma(S, sb3(wkv, 128)[:, 0:2, :],
                       cc3(W["l1a_wku"].ap()[:, 128 * pr:128 * pr + 128]),
                       waits=list(wv_pre))
            ekc = proj_fm(kcp, wkv, 2, latT, T,
                          [wd2, ("ev", S.cnt["ev"])], "kc")
            wd3 = wdma(S, sb3(wkv, 128)[:, 2:4, :],
                       cc3(W["l1a_wvu"].ap()[:, 128 * pr:128 * pr + 128]),
                       waits=list(wv_pre))
            vv = {}
            for tt in range(TTN):
                par = tt % 2
                w = [wd3, ("ev", S.cnt["ev"])] if tt == 0 else []
                if tt >= 2:
                    w.append(("ev", vv[tt - 2]))
                for lc in range(2):
                    S.op("tensor", lambda t, tt=tt, lc=lc, par=par: t.matmul(
                        psp[:, 512 * par:512 * par + 128],
                        latT[:, T * lc + 128 * tt:T * lc + 128 * tt + 128],
                        wkv[:, 256 + 128 * lc:256 + 128 * lc + 128],
                        start=(lc == 0), stop=(lc == 1)),
                        waits=w if lc == 0 else (), inc="mm" if lc == 1 else None)
                mmv = S.cnt["mm"]
                S.op("scalar", lambda sc, tt=tt, par=par: sc.activation(
                    vaug[0][:, 65 * tt:65 * tt + 64], psp[:, 512 * par:512 * par + 64],
                    AF.Identity), waits=[("mm", mmv)], inc="ev")
                vv[tt] = S.op("scalar", lambda sc, tt=tt, par=par: sc.activation(
                    vaug[1][:, 65 * tt:65 * tt + 64],
                    psp[:, 512 * par + 64:512 * par + 128], AF.Identity), inc="ev")
            S.op("vector", lambda v: v.memset(
                vaug[0][:].rearrange("p (a n) -> p a n", n=65)[:, :, 64:65], 1.0),
                waits=[("ev", S.cnt["ev"])], inc="ve")
            S.op("vector", lambda v: v.memset(
                vaug[1][:].rearrange("p (a n) -> p a n", n=65)[:, :, 64:65], 1.0),
                inc="ve")
        if h % 4 == 0 and pk + 1 < 4:
            qr_ve[pk + 1] = make_qr_pack(pk + 1, qrp[(pk + 1) % 2])

        asm_deps = []
        if h % 2 == 0:
            a1 = S.op("sync", lambda g, am=am: g.dma_start(
                out=qasm[am][0:64, :], in_=qcp[0:64, :]),
                waits=[("ev", S.cnt["ev"]), ("mm", S.cnt["mm"])], inc="gv", k=16)
            a2 = S.op("sync", lambda g, am=am: g.dma_start(
                out=kasm[am][0:64, :], in_=kcp[0:64, :]), inc="gv", k=16)
            asm_deps.append(("gv", a2))
        else:
            a1 = S.op("sync", lambda g, am=am: g.dma_start(
                out=qasm[am][0:64, :], in_=qcp[64:128, :]),
                waits=[("ev", S.cnt["ev"]), ("mm", S.cnt["mm"]), ("ve", S.cnt["ve"])],
                inc="gv", k=16)
            a2 = S.op("sync", lambda g, am=am: g.dma_start(
                out=kasm[am][0:64, :], in_=kcp[64:128, :]), inc="gv", k=16)
            asm_deps.append(("gv", a2))
        a3 = S.op("sync", lambda g, am=am, pk=pk, h=h: g.dma_start(
            out=qasm[am][64:96, :], in_=qrp[pk % 2][32 * (h % 4):32 * (h % 4) + 32, :]),
            waits=[("ve", qr_ve[pk])], inc="gv", k=16)
        asm_deps.append(("gv", a3))

        ebn = bcalc(S, pss[:, 1024:2048], pss[:, 0:1024], sc1, nB, q2,
                    qasm[am], kasm[am], 96, SC, onesbf, ones32,
                    asm_deps=asm_deps + [("mm", S.cnt["mm"]), ("ve", S.cnt["ve"])])

        for qg in range(QGN):
            KT = 4 * (qg + 1)
            eex = {}
            for kti in range(KT):
                w = []
                if kti == 0:
                    w = [("ev", ebn)]
                    if prev_vm is not None:
                        w.append(("ve", prev_vm if prev_KT == 4 else prev_vma))
                if kti < 4 and prev_e4 is not None:
                    w.append(("ev", prev_e4[kti]))
                if prev_vm is not None and kti == prev_KT - 4 and kti > 0:
                    w.append(("ve", prev_vm))
                if kti >= 4:
                    w.append(("ev", eex[kti - 4]))
                smm = S.op("tensor", lambda t, am=am, qg=qg, kti=kti: t.matmul(
                    pss[:, 512 * (kti % 4):512 * (kti % 4) + 512],
                    kasm[am][0:96, 128 * kti:128 * kti + 128],
                    qasm[am][0:96, 512 * qg:512 * qg + 512], start=True, stop=True),
                    waits=w, inc="mm")
                eex[kti] = S.op("scalar", lambda sc, kti=kti: sc.activation(
                    ppr[:, 512 * kti:512 * kti + 512],
                    pss[:, 512 * (kti % 4):512 * (kti % 4) + 512],
                    AF.Exp, bias=nB[:, 0:1], scale=np.float32(SC)),
                    waits=[("mm", smm), ("ev", ebn)], inc="ev")
            prev_eex = eex[KT - 1]
            prev_e4 = [eex[KT - 4 + j] for j in range(4)]
            ch0 = (prev_vm if prev_vm is not None else False)
            ND = KT - 4
            if ND >= 8:
                mid = ND // 2
                va1 = S.op("vector", lambda v, mid=mid: v.reduce_max(
                    macc[:, 0:512],
                    ppr[:, 0:512 * mid].rearrange("p (t q) -> p q t", t=mid),
                    axis=AX.X),
                    waits=[("ev", eex[mid - 1]), ("mm", prev_trans_mm)],
                    inc="ve", chain=ch0)
                va2 = S.op("vector", lambda v, mid=mid, ND=ND: v.reduce_max(
                    macc[:, 512:1024],
                    ppr[:, 512 * mid:512 * ND].rearrange(
                        "p (t q) -> p q t", t=ND - mid),
                    axis=AX.X),
                    waits=[("ev", eex[ND - 1])], inc="ve", chain=False)
                vma = S.op("vector", lambda v: v.reduce_max(
                    macc[:, 1024:1536],
                    macc[:, 0:1024].rearrange("p (b q) -> p q b", b=2),
                    axis=AX.X), inc="ve", chain=va2)
            elif ND > 0:
                vma = S.op("vector", lambda v, ND=ND: v.reduce_max(
                    macc[:, 1024:1536],
                    ppr[:, 0:512 * ND].rearrange("p (t q) -> p q t", t=ND),
                    axis=AX.X),
                    waits=[("ev", eex[ND - 1]), ("mm", prev_trans_mm)],
                    inc="ve", chain=ch0)
            else:
                vma = S.op("vector", lambda v: v.memset(macc[:, 1024:1536], 0.0),
                           waits=[("mm", prev_trans_mm)],
                           inc="ve", chain=ch0)
            mve = S.op("vector", lambda v, KT=KT: v.tensor_tensor(
                ppr[:, 512 * (KT - 4):512 * KT], ppr[:, 512 * (KT - 4):512 * KT],
                maskc[:, :], ALU.mult), waits=[("ev", prev_eex)], inc="ve",
                chain=False)
            for kti in range(KT):
                w = []
                if kti == KT - 4:
                    w.append(("ve", mve))
                elif kti < KT - 4:
                    w.append(("ev", eex[kti]))
                if kti == 0:
                    w.append(("ev", prev_evat))
                S.op("tensor", lambda t, kti=kti, h=h, KT=KT: t.matmul(
                    psatF[0:65, 0:512],
                    vaug[h % 2][:, 65 * kti:65 * kti + 65],
                    ppr[:, 512 * kti:512 * kti + 512],
                    start=(kti == 0), stop=(kti == KT - 1)),
                    waits=w, inc="mm" if kti == KT - 1 else None)
            att_done = S.cnt["mm"]
            ev1 = S.op("scalar", lambda sc: sc.activation(
                scr[64:65, 0:512], psatF[64:65, 0:512], AF.Identity),
                waits=[("mm", att_done), ("ve", prev_tt)], inc="ev")
            evat = S.op("scalar", lambda sc: sc.activation(
                atteS[0:64, 0:512], psatF[0:64, 0:512], AF.Identity),
                waits=[("gv", prev_odma)], inc="ev")
            vm = S.op("vector", lambda v, KT=KT: v.reduce_max(
                macc[:, 512:1024],
                ppr[:, 512 * (KT - 4):512 * KT].rearrange("p (t q) -> p q t", t=4),
                axis=AX.X), inc="ve", chain=mve)
            for jt in range(4):
                mt = S.op("tensor", lambda t, jt=jt: t.transpose(
                    psmt[:, 256 * jt:256 * jt + 128],
                    macc[:, 1024 + 128 * jt:1024 + 128 * jt + 128], identbf[:]),
                    waits=[("ve", vm)] if jt == 0 else (), inc="mm")
                mt = S.op("tensor", lambda t, jt=jt: t.transpose(
                    psmt[:, 256 * jt + 128:256 * jt + 256],
                    macc[:, 512 + 128 * jt:512 + 128 * jt + 128], identbf[:]),
                    inc="mm")
            mtr = S.op("vector", lambda v: v.reduce_max(
                pmaxs[:, 0:4], psmt[:, :].rearrange("p (b c) -> p b c", c=256),
                axis=AX.X), waits=[("mm", mt)], inc="ve", chain=False)
            dts = {}
            for j in range(4):
                dts[j] = S.op("tensor", lambda t, j=j: t.transpose(
                    psmt[:, 2 * j:2 * j + 1], scr[64:65, 128 * j:128 * j + 128],
                    identbf[64:65, 64:65], tile_position=(64, 0)),
                    waits=([("ve", mtr), ("ev", ev1)] if j == 0 else ()),
                    inc="mm")
            stt = S.op("vector", lambda v, h=h: _stt_bf16(
                nc, v,
                dcol[:, 0:4].rearrange("p (a b) -> p a b", b=1),
                pmaxs[:, 0:4].rearrange("p (a b) -> p a b", b=1),
                ccol[:, h:h + 1],
                psmt[:, 0:8].rearrange("p (a b) -> p a b", b=2)[:, :, 0:1]),
                waits=[("mm", dts[3])], inc="ve", chain=False)
            rcp = S.op("vector", lambda v: _rcp_bf16(
                nc, v, dcol[:, 0:4], dcol[:, 0:4]),
                inc="ve", chain=stt)
            prt = {}
            for jt in range(4):
                prt[jt] = S.op("tensor", lambda t, jt=jt: t.transpose(
                    prow[64:65, 128 * jt:128 * jt + 128], dcol[:, jt:jt + 1],
                    identbf[:], tile_position=(0, 64)),
                    waits=[("ve", rcp)] if jt == 0 else (), inc="mm")
            prev_trans_mm = S.cnt["mm"]
            erow = S.op("scalar", lambda sc: sc.activation(
                scrB[0:1, 0:512], prow[64:65, 0:512], AF.Identity),
                waits=[("mm", prt[3])], inc="ev")
            bc = S.op("tensor", lambda t: t.matmul(
                psbc[0:64, 0:512], onesbf[64:65, 0:64], scrB[0:1, 0:512],
                start=True, stop=True), waits=[("ev", erow)], inc="mm")
            acol = T * (h // 2) + 512 * qg
            if h % 2 == 0:
                tt = S.op("vector", lambda v, acol=acol: v.tensor_tensor(
                    arena[0:64, acol:acol + 512], atteS[0:64, 0:512],
                    psbc[0:64, 0:512], ALU.mult),
                    waits=[("mm", bc), ("ev", evat)], inc="ve", chain=False)
            else:
                tt = S.op("vector", lambda v: v.tensor_tensor(
                    atteS[0:64, 0:512], atteS[0:64, 0:512],
                    psbc[0:64, 0:512], ALU.mult),
                    waits=[("mm", bc), ("ev", evat)], inc="ve", chain=False)
                prev_odma = S.op("sync", lambda g, acol=acol: g.dma_start(
                    out=arena[64:128, acol:acol + 512], in_=atteS[0:64, 0:512]),
                    waits=[("ve", tt)], inc="gv", k=16)
            prev_tt = tt
            prev_evat = evat
            prev_att = att_done
            prev_vm = vm
            prev_vma = vma
            prev_KT = KT

    S.fence_all()
    wo = [wt[0], wt[1]]
    wod = wdma(S, sb3(wo[0], 128), cc3(W["l1a_wo"].ap()[:, 0:128]))
    vres = {}
    for ft in range(NCH):
        cur = wo[ft % 2]
        wcur = wod
        if ft < 7:
            wod = wdma(S, sb3(wo[(ft + 1) % 2], 128),
                       cc3(W["l1a_wo"].ap()[:, 128 * (ft + 1):128 * (ft + 1) + 128]),
                       waits=[("mm", S.cnt["mm"])])
        for qg in range(QGN):
            k = 4 * ft + qg
            par = k % 2
            w = []
            if qg == 0:
                w.append(wcur)
            if k >= 2:
                w.append(("ve", vres[k - 2]))
            for c in range(NCH):
                S.op("tensor", lambda t, qg=qg, c=c, cur=cur, par=par: t.matmul(
                    psp[:, 512 * par:512 * par + 512], cur[:, 128 * c:128 * c + 128],
                    arena[:, T * c + 512 * qg:T * c + 512 * qg + 512],
                    start=(c == 0), stop=(c == NCH - 1)),
                    waits=w if c == 0 else (),
                    inc="mm" if c == NCH - 1 else None)
            mo = S.cnt["mm"]
            vres[k] = S.op("vector", lambda v, ft=ft, qg=qg, par=par: v.tensor_tensor(
                x32[:, T * ft + 512 * qg:T * ft + 512 * qg + 512],
                psp[:, 512 * par:512 * par + 512],
                x32[:, T * ft + 512 * qg:T * ft + 512 * qg + 512], ALU.add),
                waits=[("mm", mo)], inc="ve")
    S.fence_all()
    S.emit()
    es.close()
    ln_boundary(st, "_l1a%d" % b)


def _slice2(buf, off, n):
    return buf[:, off:off + n]



def phase_l1f(st, b):
    S, nc, W = st["S"], st["nc"], st["W"]
    x32, xbf, arena = st["x32"], st["xbf"], st["arena"]
    onesbf, ones32, identbf = st["onesbf"], st["ones32"], st["identbf"]
    es = ExitStack()
    sbt = lambda n, s, d: es.enter_context(nc.sbuf_tensor(n + "_l1f%d" % b, s, d))
    SC = 1.0 / 32.0

    nwc = sbt("nwc", [128, 8], F32)
    nbc = sbt("nbc", [128, 8], F32)
    qbc = sbt("qbc", [128, 8], F32)
    kbc = sbt("kbc", [128, 8], F32)
    vbrow = sbt("vbrow", [1, D], BF16)
    obrow = sbt("obrow", [1, D], BF16)
    wraw = sbt("wraw", [128, 1024], BF16)
    qwT = sbt("qwT", [128, 1024], BF16)
    kwT = sbt("kwT", [128, 1024], BF16)
    vwT = sbt("vwT", [128, 1024], BF16)
    owT = sbt("owT", [128, 1024], BF16)
    qtb = sbt("qtb", [128, T], BF16)
    ktb = sbt("ktb", [128, T], BF16)
    vaug = sbt("vaug", [128, TTN * 129], BF16)
    ppr = sbt("ppr", [128, 16 * 512], BF16)
    attsc = sbt("attsc", [128, 4 * 128], BF16)
    sccol = sbt("sccol", [128, 4], F32)
    nB = sbt("nB", [128, 1], F32)
    q2 = sbt("q2", [128, T], BF16)
    sc1 = sbt("sc1", [128, 8], F32)

    S.fence_all()
    for dst, src in ((nwc, "l1f_nw"), (nbc, "l1f_nb"), (qbc, "l1f_qb"),
                     (kbc, "l1f_kb")):
        S.op("sync", lambda s, dst=dst, src=src: s.dma_start(
            out=dst[:, :], in_=bias_col(W[src].ap()),
            allow_slow_non_contiguous=True), inc="wq", k=16)
    vb_k = wdma(S, vbrow[0:1, :], W["l1f_vb"].ap().unsqueeze(0))
    ldw = wdma(S, obrow[0:1, :], W["l1f_ob"].ap().unsqueeze(0))
    wq0 = S.cnt["wq"]
    S.op("sync", lambda s: None, waits=[("wq", wq0), ldw])
    ln_boundary(st, "_l1fn%d" % b, affine=(nwc, nbc, arena))

    pst = lambda n, s, d: es.enter_context(nc.psum_tensor(n + "_l1f%d" % b, s, d))
    pss = pst("pss", [128, 2048], F32)
    psp = pss[:, 0:1024]
    psat = [pst("psat%d" % i, [128, 144], F32) for i in range(2)]
    psTs = [pst("psT%d" % i, [128, 128], BF16) for i in range(2)]

    wtr_state = dict(mm=0, ev=0)

    def wtrans(dst, wname, rowblk, mcols, extra=()):
        wd = wdma(S, wraw[:, 0:mcols],
                  W[wname].ap()[128 * rowblk:128 * rowblk + 128, 0:mcols],
                  waits=list(extra) + [("mm", wtr_state["mm"])])
        ev0 = S.cnt["ev"]
        evs = None
        for j in range(mcols // 128):
            w = [wd] if j == 0 else []
            if j < 2:
                w.append(("ev", ev0))
            else:
                w.append(("ev", evw[j - 2]))
            mt = S.op("tensor", lambda t, j=j: t.transpose(
                psTs[j % 2][:, :], wraw[:, 128 * j:128 * j + 128],
                identbf[:]), waits=w, inc="mm")
            if j == 0:
                evw = {}
            evw[j] = S.op("scalar", lambda sc, j=j, dst=dst: sc.activation(
                dst[:, 128 * j:128 * j + 128], psTs[j % 2][:, :], AF.Identity),
                waits=[("mm", mt)], inc="ev")
            evs = evw[j]
        wtr_state["mm"] = S.cnt["mm"]
        wtr_state["ev"] = evs
        return evs

    prev_tail = dict(mm=S.cnt["mm"], ve=S.cnt["ve"], ev=S.cnt["ev"], wg=S.cnt["wg"])
    prev_e4f = None
    for h in range(8):
        wtrans(qwT, "l1f_qw", h, 1024)
        wtrans(kwT, "l1f_kw", h, 1024)
        e3 = wtrans(vwT, "l1f_vw", h, 1024)
        qkev = dict(q=prev_tail["ev"], k=prev_tail["ev"])
        for e in range(4):
            mch = NCH >> e
            for c in range(mch):
                if e == 0 and c == 0:
                    w = [("ev", e3), ("ve", prev_tail["ve"])]
                elif c == 0:
                    w = [("ev", qkev["q"])]
                else:
                    w = ()
                S.op("tensor", lambda t, e=e, c=c, mch=mch: t.matmul(
                    psp[:, 0:512], qwT[:, 128 * c:128 * c + 128],
                    arena[:, T * c + 512 * e:T * c + 512 * e + 512],
                    start=(c == 0), stop=(c == mch - 1)), waits=w,
                    inc="mm" if c == mch - 1 else None)
            qkev["q"] = S.op("scalar", lambda sc, e=e, h=h: sc.activation(
                qtb[:, 512 * e:512 * e + 512], psp[:, 0:512], AF.Identity,
                bias=qbc[:, h:h + 1]), waits=[("mm", S.cnt["mm"])], inc="ev")
            for c in range(mch):
                S.op("tensor", lambda t, e=e, c=c, mch=mch: t.matmul(
                    psp[:, 512:1024], kwT[:, 128 * c:128 * c + 128],
                    arena[:, T * c + 512 * e:T * c + 512 * e + 512],
                    start=(c == 0), stop=(c == mch - 1)),
                    waits=[("ev", qkev["k"])] if c == 0 else (),
                    inc="mm" if c == mch - 1 else None)
            qkev["k"] = S.op("scalar", lambda sc, e=e, h=h: sc.activation(
                ktb[:, 512 * e:512 * e + 512], psp[:, 512:1024], AF.Identity,
                bias=kbc[:, h:h + 1]), waits=[("mm", S.cnt["mm"])], inc="ev")
            for j in range(4):
                tt = 4 * e + j
                par = tt % 2
                w = []
                if tt >= 2:
                    w.append(("ev", vtt[tt - 2]))
                for c in range(mch):
                    S.op("tensor", lambda t, c=c, tt=tt, par=par: t.matmul(
                        pss[:, 1024 + 512 * par:1024 + 512 * par + 128],
                        arena[:, T * c + 128 * tt:T * c + 128 * tt + 128],
                        vwT[:, 128 * c:128 * c + 128],
                        start=(c == 0), stop=False), waits=w if c == 0 else ())
                mv = S.op("tensor", lambda t, h=h, par=par, tt=tt: t.matmul(
                    pss[:, 1024 + 512 * par:1024 + 512 * par + 128], onesbf[0:1, 0:128],
                    vbrow[0:1, 128 * h:128 * h + 128], start=False, stop=True),
                    waits=[("wq", wq0), vb_k] if tt == 0 else (), inc="mm")
                if tt == 0:
                    vtt = {}
                vtt[tt] = S.op("scalar", lambda sc, tt=tt, par=par: sc.activation(
                    vaug[:, 129 * tt:129 * tt + 128],
                    pss[:, 1024 + 512 * par:1024 + 512 * par + 128],
                    AF.Identity), waits=[("mm", mv)], inc="ev")
        vone = S.op("vector", lambda v: v.memset(
            vaug[:].rearrange("p (a n) -> p a n", n=129)[:, :, 128:129], 1.0),
            waits=[("ev", S.cnt["ev"]), ("ve", prev_tail["ve"])], inc="ve")

        ebn = bcalc(S, pss[:, 1024:2048], pss[:, 0:1024], sc1, nB, q2, qtb, ktb,
                    128, SC, onesbf, ones32,
                    asm_deps=[("ev", S.cnt["ev"]), ("ve", vone)])

        for qg in range(QGN):
            eex = {}
            for kti in range(TTN):
                w = []
                if kti == 0:
                    w = [("ev", ebn), ("ve", S.cnt["ve"]), ("mm", prev_tail["mm"])]
                if kti < 4 and prev_e4f is not None:
                    w.append(("ev", prev_e4f[kti]))
                if kti >= 4:
                    w.append(("ev", eex[kti - 4]))
                smm = S.op("tensor", lambda t, qg=qg, kti=kti: t.matmul(
                    pss[:, 512 * (kti % 4):512 * (kti % 4) + 512],
                    ktb[:, 128 * kti:128 * kti + 128],
                    qtb[:, 512 * qg:512 * qg + 512], start=True, stop=True),
                    waits=w, inc="mm")
                eex[kti] = S.op("scalar", lambda sc, kti=kti: sc.activation(
                    ppr[:, 512 * kti:512 * kti + 512],
                    pss[:, 512 * (kti % 4):512 * (kti % 4) + 512],
                    AF.Exp, bias=nB[:, 0:1], scale=np.float32(SC)),
                    waits=[("mm", smm), ("ev", ebn)], inc="ev")
            prev_e4f = [eex[TTN - 4 + j] for j in range(4)]
            exp_all = S.cnt["ev"]
            for half in range(2):
                for j2 in range(2):
                    jt = 2 * half + j2
                    for kti in range(TTN):
                        S.op("tensor", lambda t, jt=jt, j2=j2, kti=kti: t.matmul(
                            psat[j2][0:128, 0:129],
                            ppr[:, 512 * kti + 128 * jt:512 * kti + 128 * jt + 128],
                            vaug[:, 129 * kti:129 * kti + 129],
                            start=(kti == 0), stop=(kti == TTN - 1)),
                            waits=[("ev", exp_all), ("ve", S.cnt["ve"])]
                            if (kti == 0 and j2 == 0) else (),
                            inc="mm" if kti == TTN - 1 else None)
                att_done = S.cnt["mm"]
                for j2 in range(2):
                    jt = 2 * half + j2
                    S.op("vector", lambda v, jt=jt, j2=j2: v.reciprocal(
                        sccol[:, jt:jt + 1], psat[j2][0:128, 128:129]),
                        waits=[("mm", att_done), ("ve", S.cnt["ve"])] if j2 == 0 else (),
                        inc="ve")
                vsc = S.cnt["ve"]
                for j2 in range(2):
                    jt = 2 * half + j2
                    S.op("vector", lambda v, jt=jt, j2=j2: v.tensor_scalar(
                        attsc[:, 128 * jt:128 * jt + 128], psat[j2][0:128, 0:128],
                        sccol[:, jt:jt + 1], None, ALU.mult),
                        waits=[("ve", vsc)] if j2 == 0 else (),
                        inc="ve" if j2 == 1 else None)
                vats = S.cnt["ve"]
                for j2 in range(2):
                    jt = 2 * half + j2
                    mt = S.op("tensor", lambda t, jt=jt: t.transpose(
                        psTs[jt % 2][:, :], attsc[:, 128 * jt:128 * jt + 128],
                        identbf[:]),
                        waits=[("ve", vats), ("ev", S.cnt["ev"])],
                        inc="mm")
                    S.op("scalar", lambda sc, jt=jt, qg=qg, h=h: sc.activation(
                        xbf[:, T * h + 512 * qg + 128 * jt:T * h + 512 * qg + 128 * jt + 128],
                        psTs[jt % 2][:, :], AF.Identity), waits=[("mm", mt)], inc="ev")
            prev_tail = dict(mm=S.cnt["mm"], ev=S.cnt["ev"], ve=S.cnt["ve"])

    S.fence_all()
    vres = None
    for e in range(4):
        mch = NCH >> e
        for ot in range(mch):
            eo = wtrans(owT, "l1f_ow", ot, 128 * mch, extra=[("ve", S.cnt["ve"])])
            for c in range(mch):
                w = ([("ev", eo)] + ([("ve", vres)] if vres else [])) if c == 0 else ()
                S.op("tensor", lambda t, e=e, c=c: t.matmul(
                    psp[:, 0:512], owT[:, 128 * c:128 * c + 128],
                    xbf[:, T * c + 512 * e:T * c + 512 * e + 512],
                    start=(c == 0), stop=False), waits=w)
            mo = S.op("tensor", lambda t, ot=ot: t.matmul(
                psp[:, 0:512], obrow[0:1, 128 * ot:128 * ot + 128],
                onesbf[0:1, 0:512], start=False, stop=True), inc="mm")
            vres = S.op("vector", lambda v, e=e, ot=ot: v.scalar_tensor_tensor(
                x32[:, T * ot + 512 * e:T * ot + 512 * e + 512],
                x32[:, T * ot + 512 * e:T * ot + 512 * e + 512], 2.0,
                psp[:, 0:512], ALU.mult, ALU.add),
                waits=[("mm", mo)], inc="ve")
        for ot in range(mch, NCH):
            vres = S.op("vector", lambda v, e=e, ot=ot: v.tensor_scalar(
                x32[:, T * ot + 512 * e:T * ot + 512 * e + 512],
                x32[:, T * ot + 512 * e:T * ot + 512 * e + 512], 2.0, None, ALU.mult),
                inc="ve")
    S.fence_all()
    S.emit()
    es.close()
    ln_boundary(st, "_l1f%d" % b)



def build_nc(nphase=99, custom_seq=None):
    nc = bass.Bass("TRN2", target_bir_lowering=False)
    xin = nc.dram_tensor("x_in", [BPC, T, D], F32, kind="ExternalInput")
    xout = nc.dram_tensor("x_out", [BPC, T, D], F32, kind="ExternalOutput")
    W = {}

    def din(name, shape):
        W[name] = nc.dram_tensor(name, shape, F32, kind="ExternalInput")

    din("l0a_wqkv", [D, 3 * D]); din("l0a_bqkv", [3 * D])
    din("l0a_wout", [D, D]); din("l0a_bout", [D])
    din("l0f_gw", [D, 8]); din("l0f_w1", [8, D, 512]); din("l0f_b1", [8, 512])
    din("l0f_w2", [8, 512, D]); din("l0f_b2", [8, D])
    din("l0f_ws1", [D, 512]); din("l0f_bs1", [512])
    din("l0f_ws2", [512, D]); din("l0f_bs2", [D])
    din("l0f_rs", [1, 1])
    din("l1a_wq", [D, D]); din("l1a_wqr", [D, 512]); din("l1a_wkr", [D, 32])
    din("l1a_wkvd", [D, 256]); din("l1a_wku", [256, D]); din("l1a_wvu", [256, D])
    din("l1a_wo", [D, D]); din("l1a_lobo", [1, 16])
    din("l1f_nw", [D]); din("l1f_nb", [D])
    din("l1f_qw", [D, D]); din("l1f_qb", [D]); din("l1f_kw", [D, D]); din("l1f_kb", [D])
    din("l1f_vw", [D, D]); din("l1f_vb", [D]); din("l1f_ow", [D, D]); din("l1f_ob", [D])

    CD = _mk_consts(nc)
    with ExitStack() as ctx:
        sb = lambda n, s, d: ctx.enter_context(nc.sbuf_tensor(n, s, d))
        x32 = sb("x32", [128, NCH * T], F32)
        xbf = sb("xbf", [128, NCH * T], BF16)
        arena = sb("arena", [128, NCH * T], BF16)
        identbf = sb("identbf_s", [128, 128], BF16)
        ident32 = sb("ident32_s", [128, 128], F32)
        onesbf = sb("onesbf_s", [128, 512], BF16)
        ones32 = sb("ones32_s", [128, 128], F32)
        maskc = sb("maskc_s", [128, 2048], BF16)
        pswap = sb("pswap_s", [128, 128], BF16)
        blk = ctx.enter_context(nc.Block())
        S = Sched(nc, ctx, blk)

        for dst, key in [(identbf, "identbf"), (ident32, "ident32"),
                         (onesbf, "onesbf"), (ones32, "ones32"),
                         (maskc, "maskc"), (pswap, "pswap")]:
            S.op("sync", lambda s, dst=dst, key=key: s.dma_start(
                out=dst[:], in_=CD[key].ap()), inc="wq", k=16)
        S.emit()

        st = dict(S=S, nc=nc, W=W, xin=xin, xout=xout, x32=x32, xbf=xbf,
                  arena=arena, identbf=identbf, ident32=ident32, onesbf=onesbf,
                  ones32=ones32, maskc=maskc, pswap=pswap, CD=CD)

        if custom_seq is not None:
            seq = [({"xload": phase_xload, "l0a": phase_l0a, "l0f": phase_l0f,
                     "l1a": phase_l1a, "l1f": phase_l1f, "store": phase_store}[nm], b)
                   for nm, b in custom_seq]
        else:
            seq = []
            for b in range(BPC):
                seq += [(phase_xload, b), (phase_l0a, b), (phase_l0f, b),
                        (phase_l1a, b), (phase_l1f, b), (phase_store, b)]
            seq = seq[:nphase]
        for fn, b in seq:
            fn(st, b)

        if S.cnt["out"]:
            S.op("sync", lambda s: None, waits=[("out", S.cnt["out"])])
        S.fence_all()
        S.emit()
    return nc



_IN_MAP = [
    ("l0a_wqkv", "L0_attn_W_qkv"), ("l0a_bqkv", "L0_attn_b_qkv"),
    ("l0a_wout", "L0_attn_W_out"), ("l0a_bout", "L0_attn_b_out"),
    ("l0f_gw", "L0_ffn_gate_W"), ("l0f_w1", "L0_ffn_W1"), ("l0f_b1", "L0_ffn_b1"),
    ("l0f_w2", "L0_ffn_W2"), ("l0f_b2", "L0_ffn_b2"),
    ("l0f_ws1", "L0_ffn_Ws1"), ("l0f_bs1", "L0_ffn_bs1"),
    ("l0f_ws2", "L0_ffn_Ws2"), ("l0f_bs2", "L0_ffn_bs2"),
    ("l1a_wq", "L1_attn_Wq"), ("l1a_wqr", "L1_attn_Wqr"), ("l1a_wkr", "L1_attn_Wkr"),
    ("l1a_wkvd", "L1_attn_Wkvd"), ("l1a_wku", "L1_attn_Wku"),
    ("l1a_wvu", "L1_attn_Wvu"), ("l1a_wo", "L1_attn_Wo"),
    ("l1f_nw", "L1_ffn_norm_w"), ("l1f_nb", "L1_ffn_norm_b"),
    ("l1f_qw", "L1_ffn_q_w"), ("l1f_qb", "L1_ffn_q_b"),
    ("l1f_kw", "L1_ffn_k_w"), ("l1f_kb", "L1_ffn_k_b"),
    ("l1f_vw", "L1_ffn_v_w"), ("l1f_vb", "L1_ffn_v_b"),
    ("l1f_ow", "L1_ffn_o_w"), ("l1f_ob", "L1_ffn_o_b"),
]

_CACHE = {}


def kernel(**inputs):
    n_cores = 8
    if "nc" not in _CACHE:
        _CACHE["nc"] = build_nc()
    nc = _CACHE["nc"]
    x = np.ascontiguousarray(np.asarray(inputs["L0_attn_x"], dtype=np.float32))
    wmap = {}
    for kname, iname in _IN_MAP:
        wmap[kname] = np.ascontiguousarray(np.asarray(inputs[iname], dtype=np.float32))
    wmap["l1a_lobo"] = np.ascontiguousarray(
        np.asarray(inputs["L1_attn_lobo_log"], dtype=np.float32).reshape(1, 16))
    wmap["l0f_rs"] = np.ascontiguousarray(
        np.asarray(inputs["L0_ffn_route_scale"], dtype=np.float32).reshape(1, 1))
    in_maps = []
    for c in range(n_cores):
        m = dict(wmap)
        m["x_in"] = np.ascontiguousarray(x[BPC * c:BPC * c + BPC])
        in_maps.append(m)
    res = run_bass_kernel_spmd(nc, in_maps, core_ids=list(range(n_cores)))
    out = np.concatenate([np.asarray(r["x_out"]) for r in res.results], axis=0)
    return out.astype(np.float32)


# version 29
# speedup vs baseline: 1.1873x; 1.1873x over previous
import math
import numpy as np
import ml_dtypes
import concourse.bass as bass
import concourse.mybir as mybir
from concourse.bass_utils import run_bass_kernel_spmd
from contextlib import ExitStack

F32 = mybir.dt.float32
BF16 = mybir.dt.bfloat16
AF = mybir.ActivationFunctionType
ALU = mybir.AluOpType
AX = mybir.AxisListType
bf16 = ml_dtypes.bfloat16

T = 2048
D = 1024
NCH = 8
TTN = 16
QGN = 4
BPC = 2
ENGS = ("tensor", "vector", "scalar", "sync", "gpsimd")


class Sched:
    def __init__(self, nc, ctx, blk):
        self.nc, self.ctx, self.blk = nc, ctx, blk
        self.cnt, self.sem = {}, {}
        for name in ("mm", "ve", "ev", "gv", "wq", "wg", "wg2", "wg3", "out"):
            self.sem[name] = ctx.enter_context(nc.semaphore("s_" + name))
            self.cnt[name] = 0
        self.ops = {e: [] for e in ENGS}

    def op(self, eng, build, waits=(), inc=None, k=1, chain=True):
        waits = list(waits)
        if inc in ("wq", "wg", "wg2", "wg3", "gv", "out"):
            waits.append((inc, self.cnt[inc]))
        if eng == "vector":
            if chain is True:
                if self.cnt["ve"]:
                    waits.append(("ve", self.cnt["ve"]))
            elif chain:
                waits.append(("ve", chain))
            inc = inc or "ve"
        elif eng == "scalar":
            if self.cnt["ev"]:
                waits.append(("ev", self.cnt["ev"]))
            inc = inc or "ev"
        after = None
        if inc is not None:
            self.cnt[inc] += k
            after = self.cnt[inc]
        self.ops[eng].append((tuple(waits), build, inc, k))
        return after

    def fence_all(self):
        waits = tuple((n, v) for n, v in self.cnt.items() if v > 0)
        for e in ENGS:
            self.ops[e].append((waits, None, None, 0))

    def emit(self):
        for e in ENGS:
            ops = self.ops[e]
            if not ops:
                continue
            self.ops[e] = []

            def runner(eng, ops=ops):
                for waits, build, inc, k in ops:
                    for sname, val in waits:
                        eng.wait_ge(self.sem[sname], val)
                    if build is not None:
                        ins = build(eng)
                        if inc is not None:
                            ins.then_inc(self.sem[inc], k)
            getattr(self.blk, e)(runner)


def _rcp_bf16(nc, v, out, in_):
    with nc.allow_low_precision(reason="bf16 reciprocal of softmax denominators; 0.4% rel on a well-conditioned scale"):
        return v.reciprocal(out, in_)


def _stt_bf16(nc, v, out, in0, scalar, in1):
    with nc.allow_low_precision(reason="bf16 softmax denominator accumulate; ~0.4% rel on a well-conditioned positive sum"):
        return v.scalar_tensor_tensor(out, in0, scalar, in1, ALU.mult, ALU.add)


def _mk_consts(nc):
    c = {}
    c["identbf"] = nc.inline_tensor(np.eye(128, dtype=bf16), name="identbf")
    c["ident32"] = nc.inline_tensor(np.eye(128, dtype=np.float32), name="ident32")
    c["onesbf"] = nc.inline_tensor(np.ones((128, 512), dtype=bf16), name="onesbf")
    c["ones32"] = nc.inline_tensor(np.ones((128, 128), dtype=np.float32), name="ones32")
    m = np.zeros((128, 4 * 512), dtype=bf16)
    ll = np.arange(128)[:, None]
    jj = np.arange(512)[None, :]
    for o in range(4):
        m[:, o * 512:(o + 1) * 512] = (jj >= ll + 128 * o).astype(bf16)
    c["maskc"] = nc.inline_tensor(m, name="maskc")
    mr = np.zeros((128, 8 * 512), dtype=bf16)
    for o in range(4):
        mr[:, 1024 * o:1024 * o + 512] = m[:, 512 * o:512 * o + 512]
        mr[:, 1024 * o + 512:1024 * o + 1024] = m[:, 512 * o:512 * o + 512]
    c["maskcR"] = nc.inline_tensor(mr, name="maskcR")
    half = 16
    inv = 1.0 / (10000.0 ** (np.arange(half, dtype=np.float64) / half))
    ang = np.arange(T, dtype=np.float64)[:, None] * inv[None, :]
    cosv = np.cos(ang).T.astype(np.float32)
    sinv = np.sin(ang).T.astype(np.float32)
    C = np.zeros((128, T), dtype=np.float32)
    Sm = np.zeros((128, T), dtype=np.float32)
    for p in range(4):
        C[32 * p:32 * p + 16] = cosv
        C[32 * p + 16:32 * p + 32] = cosv
        Sm[32 * p:32 * p + 16] = -sinv
        Sm[32 * p + 16:32 * p + 32] = sinv
    c["ropeC"] = nc.inline_tensor(C.astype(bf16), name="ropeC")
    c["ropeS"] = nc.inline_tensor(Sm.astype(bf16), name="ropeS")
    P = np.zeros((128, 128), dtype=bf16)
    for r in range(128):
        P[(r // 32) * 32 + (r % 32 + 16) % 32, r] = 1.0
    c["pswap"] = nc.inline_tensor(P, name="pswap")
    return c


_WDMA_ROT = ["wg", "wg2", "wg3"]


def wdma(S, dst, src_ap, waits=()):
    S._wrot = (getattr(S, "_wrot", 0) + 1) % 3
    sem = _WDMA_ROT[S._wrot]
    v = S.op("gpsimd", lambda g: g.dma_start(out=dst, in_=src_ap),
             waits=waits, inc=sem, k=16)
    return (sem, v)


def cc3(ap2d):
    return ap2d.rearrange("(c p) n -> p c n", p=128)


def sb3(buf, n):
    return buf[:].rearrange("p (c n) -> p c n", n=n)


def bias_col(ap1d):
    return ap1d.rearrange("(c a) -> a c", a=128)



def phase_xload(st, b):
    S, nc = st["S"], st["nc"]
    x32, xbf, ident32 = st["x32"], st["xbf"], st["ident32"]
    xin = st["xin"]
    S.fence_all()
    with nc.psum_tensor("pxl%d" % b, [128, 1024], F32) as pT, \
            nc.sbuf_tensor("xtok_l%d" % b, [128, 2 * D], F32) as xtok:
        ev = {}
        for j in range(TTN):
            w = [("ev", ev[(j - 2, NCH - 1)])] if j >= 2 else ()
            dj = S.op("sync", lambda s, j=j: s.dma_start(
                out=xtok[:, (j % 2) * D:(j % 2) * D + D],
                in_=xin.ap()[b, 128 * j:128 * j + 128, :]), waits=w, inc="wq", k=16)
            for c in range(NCH):
                k = 8 * j + c
                w2 = [("wq", dj)] if c == 0 else []
                if k >= 2:
                    w2.append(("ev", ev[((k - 2) // 8, (k - 2) % 8)]))
                mmk = S.op("tensor", lambda t, j=j, c=c, k=k: t.transpose(
                    pT[:, (k % 2) * 512:(k % 2) * 512 + 128],
                    xtok[:, (j % 2) * D + 128 * c:(j % 2) * D + 128 * c + 128],
                    ident32[:]), waits=w2, inc="mm")
                ev[(j, c)] = S.op("scalar", lambda sc, j=j, c=c, k=k: sc.activation(
                    x32[:, T * c + 128 * j:T * c + 128 * j + 128],
                    pT[:, (k % 2) * 512:(k % 2) * 512 + 128], AF.Identity),
                    waits=[("mm", mmk)], inc="ev")
        evl = S.cnt["ev"]
        for c in range(NCH):
            S.op("vector", lambda v, c=c: v.tensor_copy(
                xbf[:, T * c:T * c + T], x32[:, T * c:T * c + T]),
                waits=[("ev", evl)] if c == 0 else (), inc="ve")
        S.fence_all()
        S.emit()


def phase_store(st, b):
    S, nc = st["S"], st["nc"]
    x32, ident32 = st["x32"], st["ident32"]
    xout = st["xout"]
    S.fence_all()
    with nc.psum_tensor("pst%d" % b, [128, 1024], F32) as pT, \
            nc.sbuf_tensor("xtok_s%d" % b, [128, 2 * D], F32) as xtok:
        ev = {}
        dmas = {}
        for j in range(TTN):
            for c in range(NCH):
                k = 8 * j + c
                w = []
                if k >= 2:
                    w.append(("ev", ev[k - 2]))
                mmk = S.op("tensor", lambda t, j=j, c=c, k=k: t.transpose(
                    pT[:, (k % 2) * 512:(k % 2) * 512 + 128],
                    x32[:, T * c + 128 * j:T * c + 128 * j + 128],
                    ident32[:]), waits=w, inc="mm")
                w3 = [("mm", mmk)]
                if j >= 2 and c == 0:
                    w3.append(("out", dmas[j - 2]))
                ev[k] = S.op("scalar", lambda sc, j=j, c=c, k=k: sc.activation(
                    xtok[:, (j % 2) * D + 128 * c:(j % 2) * D + 128 * c + 128],
                    pT[:, (k % 2) * 512:(k % 2) * 512 + 128], AF.Identity),
                    waits=w3, inc="ev")
            dmas[j] = S.op("sync", lambda s, j=j: s.dma_start(
                out=xout.ap()[b, 128 * j:128 * j + 128, :],
                in_=xtok[:, (j % 2) * D:(j % 2) * D + D]),
                waits=[("ev", ev[8 * j + NCH - 1])], inc="out", k=16)
        S.fence_all()
        S.emit()


def ln_boundary(st, tag, affine=None):
    S, nc = st["S"], st["nc"]
    x32, xbf, onesbf, ones32 = st["x32"], st["xbf"], st["onesbf"], st["ones32"]
    modify_x = affine is None
    S.fence_all()
    es = ExitStack()
    sbt = lambda n, s, d: es.enter_context(nc.sbuf_tensor(n + tag, s, d))
    pst = lambda n, s, d: es.enter_context(nc.psum_tensor(n + tag, s, d))
    u2 = [sbt("u2%d" % i, [128, T], BF16) for i in range(2)]
    rowA = sbt("rowA", [1, T], F32)
    rowB = sbt("rowB", [1, T], F32)
    if not modify_x:
        t1 = sbt("t1", [128, 1024], BF16)
    psr = pst("psr", [128, T], F32)
    psm = pst("psm", [128, T], F32)
    psS = psr[0:1, :]

    for c in range(NCH):
        for qg in range(QGN):
            S.op("tensor", lambda t, c=c, qg=qg: t.matmul(
                psS[0:1, 512 * qg:512 * qg + 512], ones32[:, 0:1],
                x32[:, T * c + 512 * qg:T * c + 512 * qg + 512],
                start=(c == 0), stop=(c == NCH - 1)),
                inc="mm" if (c == NCH - 1 and qg == QGN - 1) else None)
    vA = S.op("vector", lambda v: v.tensor_scalar(
        rowA[0:1, :], psS[0:1, :], 1.0 / D, None, ALU.mult),
        waits=[("mm", S.cnt["mm"])], inc="ve", chain=False)
    sq_ev = {}
    for c in range(NCH):
        w = [("mm", mm_u2[c - 2])] if c >= 2 else []
        sq_ev[c] = S.op("scalar", lambda sc, c=c: sc.activation(
            u2[c % 2][:, :], x32[:, T * c:T * c + T], AF.Square), waits=w, inc="ev")
        if c == 0:
            mm_u2 = {}
        for qg in range(QGN):
            mm_u2[c] = S.op("tensor", lambda t, c=c, qg=qg: t.matmul(
                psS[0:1, 512 * qg:512 * qg + 512], onesbf[:, 0:1],
                u2[c % 2][:, 512 * qg:512 * qg + 512],
                start=(c == 0), stop=(c == NCH - 1)),
                waits=[("ev", sq_ev[c])] + ([("ve", vA)] if c == 0 else [])
                if qg == 0 else (),
                inc="mm" if qg == QGN - 1 else None)
    eQ = S.op("scalar", lambda sc: sc.activation(
        rowB[0:1, :], rowA[0:1, :], AF.Square), waits=[("ve", vA)], inc="ev")
    vV = S.op("vector", lambda v: v.scalar_tensor_tensor(
        rowB[0:1, :], psS[0:1, :], 1.0 / D, rowB[0:1, :],
        ALU.mult, ALU.subtract),
        waits=[("mm", S.cnt["mm"]), ("ev", eQ)], inc="ve", chain=False)
    er = S.op("scalar", lambda sc: sc.activation(
        rowB[0:1, :], rowB[0:1, :], AF.Sqrt, bias=np.float32(1e-5)),
        waits=[("ve", vV)], inc="ev")
    vR = S.op("vector", lambda v: v.reciprocal(rowB[0:1, :], rowB[0:1, :]),
              waits=[("ev", er)], inc="ve", chain=False)
    rows_ve = S.op("vector", lambda v: v.tensor_tensor(
        rowA[0:1, :], rowA[0:1, :], rowB[0:1, :], ALU.mult),
        inc="ve", chain=vR)

    for qg in range(QGN):
        S.op("tensor", lambda t, qg=qg: t.matmul(
            psr[:, 512 * qg:512 * qg + 512], ones32[0:1, 0:128],
            rowB[0:1, 512 * qg:512 * qg + 512],
            start=True, stop=True),
            waits=[("ve", rows_ve)] if qg == 0 else ())
        S.op("tensor", lambda t, qg=qg: t.matmul(
            psm[:, 512 * qg:512 * qg + 512], ones32[0:1, 0:128],
            rowA[0:1, 512 * qg:512 * qg + 512],
            start=True, stop=True), inc="mm" if qg == QGN - 1 else None)
    bdone = S.cnt["mm"]
    if modify_x:
        mids = {}
        for c in range(NCH):
            mids[c] = S.op("vector", lambda v, c=c: v.tensor_tensor(
                x32[:, T * c:T * c + T], x32[:, T * c:T * c + T], psr[:, :],
                ALU.mult), waits=[("mm", bdone)] if c == 0 else (),
                inc="ve", chain=False)
        for c in range(NCH):
            sub = S.op("vector", lambda v, c=c: v.tensor_tensor(
                x32[:, T * c:T * c + T], x32[:, T * c:T * c + T], psm[:, :],
                ALU.subtract), inc="ve", chain=mids[c])
            S.op("scalar", lambda sc, c=c: sc.activation(
                xbf[:, T * c:T * c + T], x32[:, T * c:T * c + T], AF.Identity),
                waits=[("ve", sub)], inc="ev")
    else:
        wcol, bcol, dst = affine
        prev = False
        prev_ea = None
        for c in range(NCH):
            for ch in range(2):
                sl = slice(T * c + 1024 * ch, T * c + 1024 * ch + 1024)
                bsl = slice(1024 * ch, 1024 * ch + 1024)
                m1 = S.op("vector", lambda v, sl=sl, bsl=bsl: v.tensor_tensor(
                    t1[:, :], x32[:, sl], psr[:, bsl], ALU.mult),
                    waits=[("mm", bdone)] if prev_ea is None
                    else [("ev", prev_ea)],
                    inc="ve", chain=prev)
                w2 = S.op("vector", lambda v, bsl=bsl: v.tensor_tensor(
                    t1[:, :], t1[:, :], psm[:, bsl], ALU.subtract),
                    inc="ve", chain=m1)
                prev_ea = S.op("scalar", lambda sc, sl=sl, c=c: sc.activation(
                    dst[:, sl], t1[:, :], AF.Identity,
                    bias=bcol[:, c:c + 1], scale=wcol[:, c:c + 1]),
                    waits=[("ve", w2)], inc="ev")
                prev = w2
    S.fence_all()
    S.emit()
    es.close()


def bcalc(S, psp, pss, sc1, nB, q2, qbuf, kbuf, rows, scale, onesbf, ones32,
          asm_deps=()):
    banks = [(psp, 0), (psp, 512), (pss, 0), (pss, 512)]
    red = {}
    for i in range(2):
        src = qbuf if i == 0 else kbuf
        sq = S.op("scalar", lambda sc, src=src, rows=rows: sc.activation(
            q2[0:rows, :], src[0:rows, :], AF.Square),
            waits=list(asm_deps) if i == 0 else [("mm", S.cnt["mm"])], inc="ev")
        for qg in range(QGN):
            buf, off = banks[qg]
            w = []
            if qg == 0:
                w = [("ev", sq)]
                if i == 1:
                    w.append(("ve", red[1]))
            S.op("tensor", lambda t, qg=qg, buf=buf, off=off, rows=rows: t.matmul(
                buf[:, off:off + 512], onesbf[0:rows, 0:128],
                q2[0:rows, 512 * qg:512 * qg + 512], start=True, stop=True),
                waits=w, inc="mm" if qg == 3 else None)
        mq = S.cnt["mm"]
        red[2 * i] = S.op("vector", lambda v, i=i: v.reduce_max(
            sc1[:, 2 * i:2 * i + 1], psp[:, 0:1024], axis=AX.X),
            waits=[("mm", mq)], inc="ve", chain=False)
        red[2 * i + 1] = S.op("vector", lambda v, i=i: v.reduce_max(
            sc1[:, 2 * i + 1:2 * i + 2], pss[:, 0:1024], axis=AX.X),
            inc="ve", chain=False)
    redq = S.op("vector", lambda v: v.reduce_max(
        sc1[:, 4:5], sc1[:, 0:2], axis=AX.X), inc="ve", chain=red[3])
    redk = S.op("vector", lambda v: v.reduce_max(
        sc1[:, 5:6], sc1[:, 2:4], axis=AX.X), inc="ve", chain=False)
    vb = S.op("vector", lambda v: v.tensor_tensor(
        sc1[:, 6:7], sc1[:, 4:5], sc1[:, 5:6], ALU.mult),
        inc="ve", chain=redk)
    sq2 = S.op("scalar", lambda sc: sc.activation(
        sc1[:, 7:8], sc1[:, 6:7], AF.Sqrt), waits=[("ve", vb)], inc="ev")
    return S.op("scalar", lambda sc, scale=scale: sc.activation(
        nB[:, 0:1], sc1[:, 7:8], AF.Identity, bias=np.float32(50.0),
        scale=np.float32(-scale)), waits=[("ev", sq2)], inc="ev")


def bcalc2(S, psp, pssb, sc1, nB, stage, qbuf, kbuf, scale, onesbf, ones32,
           asm_deps=()):
    banks = [(psp, 0), (psp, 512), (pssb, 0), (pssb, 512)]
    red = {}
    for i in range(2):
        src = qbuf if i == 0 else kbuf
        sq = S.op("scalar", lambda sc, src=src: sc.activation(
            stage[:, :], src[:, 0:2048], AF.Square),
            waits=list(asm_deps) if i == 0 else [("mm", S.cnt["mm"])], inc="ev")
        for qg in range(QGN):
            buf, off = banks[qg]
            w = []
            if qg == 0:
                w = [("ev", sq)]
                if i == 1:
                    w.append(("ve", red[1]))
            S.op("tensor", lambda t, qg=qg, buf=buf, off=off: t.matmul(
                buf[:, off:off + 512], onesbf[:, 0:128],
                stage[:, 512 * qg:512 * qg + 512], start=True, stop=True),
                waits=w, inc="mm" if qg == 3 else None)
        mq = S.cnt["mm"]
        red[2 * i] = S.op("vector", lambda v, i=i: v.reduce_max(
            sc1[:, 2 * i:2 * i + 1], psp[:, 0:1024], axis=AX.X),
            waits=[("mm", mq)], inc="ve", chain=False)
        red[2 * i + 1] = S.op("vector", lambda v, i=i: v.reduce_max(
            sc1[:, 2 * i + 1:2 * i + 2], pssb[:, 0:1024], axis=AX.X),
            inc="ve", chain=False)
    redq = S.op("vector", lambda v: v.reduce_max(
        sc1[:, 4:5], sc1[:, 0:2], axis=AX.X), inc="ve", chain=red[3])
    redk = S.op("vector", lambda v: v.reduce_max(
        sc1[:, 5:6], sc1[:, 2:4], axis=AX.X), inc="ve", chain=False)
    vb = S.op("vector", lambda v: v.tensor_tensor(
        sc1[:, 6:7], sc1[:, 4:5], sc1[:, 5:6], ALU.mult),
        inc="ve", chain=redk)
    sq2 = S.op("scalar", lambda sc: sc.activation(
        sc1[:, 7:8], sc1[:, 6:7], AF.Sqrt), waits=[("ve", vb)], inc="ev")
    return S.op("scalar", lambda sc, scale=scale: sc.activation(
        nB[:, 0:1], sc1[:, 7:8], AF.Identity, bias=np.float32(50.0),
        scale=np.float32(-scale)), waits=[("ev", sq2)], inc="ev")



def phase_l0a(st, b):
    S, nc, W = st["S"], st["nc"], st["W"]
    x32, xbf, arena = st["x32"], st["xbf"], st["arena"]
    onesbf, ones32, maskc = st["onesbf"], st["ones32"], st["maskc"]
    ident32 = st["ident32"]
    es = ExitStack()
    sbt = lambda n, s, d: es.enter_context(nc.sbuf_tensor(n + "_l0a%d" % b, s, d))
    pst = lambda n, s, d: es.enter_context(nc.psum_tensor(n + "_l0a%d" % b, s, d))

    vall = sbt("vall", [128, TTN * 1040], BF16)
    qt = sbt("qt", [128, T], BF16)
    kt = sbt("kt", [128, T], BF16)
    ppb = sbt("ppb", [128, 2048], BF16)
    atte = sbt("atte", [128, 1024], F32)
    atteB = sbt("atteB", [128, 512], BF16)
    rrowB = sbt("rrowB", [128, 1024], BF16)
    dcolF = sbt("dcolF", [128, 8], F32)
    nB = sbt("nB", [128, 1], F32)
    sc1 = sbt("sc1", [128, 8], F32)
    wqk = [sbt("wqkA%d" % i, [128, 1024], BF16) for i in range(2)]
    wqk2 = [sbt("wqkB%d" % i, [128, 1024], BF16) for i in range(2)]
    wv = sbt("wv", [128, NCH * 512], BF16)
    bq = sbt("bq", [128, 16], F32)
    bvrow = sbt("bvrow", [1, D], BF16)
    borow = sbt("borow", [1, D], BF16)

    psp = pst("psp", [128, 1024], F32)
    pssb = pst("pssb", [128, 2048], F32)
    psab = pst("psab", [128, 1024], F32)

    S.fence_all()
    S.op("sync", lambda s: s.dma_start(
        out=bq[:, :], in_=W["l0a_bqkv"].ap()[0:2048].rearrange("(a b) -> b a", b=128),
        allow_slow_non_contiguous=True), inc="wq", k=16)
    bv_k = wdma(S, bvrow[0:1, :], W["l0a_bqkv"].ap()[2048:3072].unsqueeze(0))
    bias_wg = wdma(S, borow[0:1, :], W["l0a_bout"].ap().unsqueeze(0))
    bias_wq = S.cnt["wq"]

    vev = {}
    for vg in range(2):
        wvd = wdma(S, sb3(wv, 512), cc3(W["l0a_wqkv"].ap()[:, 2048 + 512 * vg:2560 + 512 * vg]),
                   waits=[("mm", S.cnt["mm"])] if vg == 1 else ())
        for tt in range(TTN):
            par = tt % 2
            idx = vg * TTN + tt
            for c in range(NCH):
                w = []
                if c == 0:
                    w = [wvd, bv_k, bias_wg, ("wq", bias_wq)]
                    if idx >= 2:
                        w.append(("ev", vev[idx - 2]))
                S.op("tensor", lambda t, tt=tt, c=c, par=par: t.matmul(
                    psp[:, 512 * par:512 * par + 512],
                    xbf[:, T * c + 128 * tt:T * c + 128 * tt + 128],
                    wv[:, 512 * c:512 * c + 512], start=(c == 0), stop=False),
                    waits=w)
            mmv = S.op("tensor", lambda t, vg=vg, par=par: t.matmul(
                psp[:, 512 * par:512 * par + 512], onesbf[0:1, 0:128],
                bvrow[0:1, 512 * vg:512 * vg + 512], start=False, stop=True),
                inc="mm")
            dst = vall[:].rearrange("p (t h n) -> p t h n", t=TTN, n=65)
            vev[idx] = S.op(
                "scalar", lambda sc, tt=tt, vg=vg, par=par, dst=dst: sc.activation(
                    dst[:, tt, 8 * vg:8 * vg + 8, 0:64],
                    psp[:, 512 * par:512 * par + 512].rearrange("p (h n) -> p h n", n=64),
                    AF.Identity),
                waits=[("mm", mmv)], inc="ev")
    ones_ve = S.op("vector", lambda v: v.memset(
        vall[:].rearrange("p (a n) -> p a n", n=65)[:, :, 64:65], 1.0),
        waits=[("ev", S.cnt["ev"])], inc="ve")

    def qkcol(p, which):
        return cc3(W["l0a_wqkv"].ap()[:, 1024 * which + 128 * p:1024 * which + 128 * p + 128])

    wqd = [wdma(S, sb3(wqk[0], 128), qkcol(0, 0), waits=[("mm", S.cnt["mm"])]),
           wdma(S, sb3(wqk[1], 128), qkcol(0, 1))]
    for p in range(8):
        cur = wqk if p % 2 == 0 else wqk2
        nxt = wqk2 if p % 2 == 0 else wqk
        wq_ready = wqd
        if p < 7:
            wqd = [wdma(S, sb3(nxt[0], 128), qkcol(p + 1, 0), waits=[("mm", S.cnt["mm"])]),
                   wdma(S, sb3(nxt[1], 128), qkcol(p + 1, 1))]

        evq = {}
        for which, dstbuf, bcol in ((0, qt, p), (1, kt, 8 + p)):
            for qh in range(2):
                for qg2 in range(2):
                    qg = 2 * qh + qg2
                    for c in range(NCH):
                        w = []
                        if qh == 0 and qg2 == 0 and c == 0:
                            w = [wq_ready[which], ("ev", S.cnt["ev"]),
                                 ("ve", S.cnt["ve"])]
                        elif qg2 == 0 and c == 0 and qh == 1:
                            w = [("ev", evq[(which, 0)])]
                        S.op("tensor", lambda t, qg=qg, qg2=qg2, c=c, cb=cur[which]: t.matmul(
                            psp[:, 512 * qg2:512 * qg2 + 512],
                            cb[:, 128 * c:128 * c + 128],
                            xbf[:, T * c + 512 * qg:T * c + 512 * qg + 512],
                            start=(c == 0), stop=(c == NCH - 1)),
                            waits=w, inc="mm" if (qg2 == 1 and c == NCH - 1) else None)
                evq[(which, qh)] = S.op(
                    "scalar", lambda sc, dstbuf=dstbuf, bcol=bcol, qh=qh: sc.activation(
                        dstbuf[:, 1024 * qh:1024 * qh + 1024], psp[:, :], AF.Identity,
                        bias=bq[:, bcol:bcol + 1]),
                    waits=[("mm", S.cnt["mm"])], inc="ev")
        ebn = bcalc2(S, psp, pssb, sc1, nB, ppb, qt, kt, 0.125, onesbf, ones32,
                     asm_deps=[("ev", evq[(1, 1)])])

        hA, hB = 2 * p, 2 * p + 1
        prev_tail = dict(ev=S.cnt["ev"], ve=max(S.cnt["ve"], ones_ve),
                         gv=S.cnt["gv"], mask=0, e2=0, tt=S.cnt["ve"])
        for qg in range(QGN):
            KT = 4 * (qg + 1)
            expB = {}
            maskve = {}
            deps = {}

            def emit_att(kti, par, KT=None):
                dep = deps[kti]
                S.op("tensor", lambda t, kti=kti, par=par, hA=hA, KT=KT: t.matmul(
                    psab[0:65, 0:512],
                    vall[:, 1040 * kti + 65 * hA:1040 * kti + 65 * hA + 65],
                    ppb[:, 1024 * par:1024 * par + 512],
                    start=(kti == 0), stop=(kti == KT - 1)), waits=dep)
                return S.op("tensor", lambda t, kti=kti, par=par, hB=hB, KT=KT: t.matmul(
                    psab[0:65, 512:1024],
                    vall[:, 1040 * kti + 65 * hB:1040 * kti + 65 * hB + 65],
                    ppb[:, 1024 * par + 512:1024 * par + 1024],
                    start=(kti == 0), stop=(kti == KT - 1)), inc="mm")

            for kti in range(KT):
                par = kti % 2
                w = []
                if kti == 0:
                    w = [("ev", max(ebn, prev_tail["ev"]))]
                if kti >= 2:
                    w.append(("ev", expB[kti - 2]))
                S.op("tensor", lambda t, qg=qg, kti=kti, par=par: t.matmul(
                    pssb[:, 1024 * par:1024 * par + 512],
                    kt[0:64, 128 * kti:128 * kti + 128],
                    qt[0:64, 512 * qg:512 * qg + 512], start=True, stop=True,
                    tile_position=(0, 0)), waits=w)
                sB = S.op("tensor", lambda t, qg=qg, kti=kti, par=par: t.matmul(
                    pssb[:, 1024 * par + 512:1024 * par + 1024],
                    kt[64:128, 128 * kti:128 * kti + 128],
                    qt[64:128, 512 * qg:512 * qg + 512], start=True, stop=True,
                    tile_position=(64, 0)), inc="mm")
                ew = [("mm", sB), ("ev", ebn)]
                if kti == 0 and prev_tail.get("mask"):
                    ew.append(("ve", prev_tail["mask"]))
                if kti - 2 in maskve:
                    ew.append(("ve", maskve[kti - 2]))
                eB = S.op("scalar", lambda sc, kti=kti, par=par: sc.activation(
                    ppb[:, 1024 * par:1024 * par + 1024],
                    pssb[:, 1024 * par:1024 * par + 1024],
                    AF.Exp, bias=nB[:, 0:1], scale=np.float32(0.125)),
                    waits=ew, inc="ev")
                expB[kti] = eB
                dep = [("ev", eB)]
                if kti >= KT - 4:
                    o = kti - (KT - 4)
                    S.op("vector", lambda v, par=par, o=o: v.tensor_tensor(
                        ppb[:, 1024 * par:1024 * par + 512],
                        ppb[:, 1024 * par:1024 * par + 512],
                        maskc[:, 512 * o:512 * o + 512], ALU.mult),
                        waits=[("ev", eB)], inc="ve", chain=False)
                    mB = S.op("vector", lambda v, par=par, o=o: v.tensor_tensor(
                        ppb[:, 1024 * par + 512:1024 * par + 1024],
                        ppb[:, 1024 * par + 512:1024 * par + 1024],
                        maskc[:, 512 * o:512 * o + 512], ALU.mult),
                        inc="ve", chain=False)
                    maskve[kti] = mB
                    dep = [("ve", mB)]
                if kti == 0 and prev_tail.get("e2"):
                    dep = dep + [("ev", prev_tail["e2"])]
                deps[kti] = dep
                if kti >= 1:
                    am = emit_att(kti - 1, (kti - 1) % 2, KT)
            am = emit_att(KT - 1, (KT - 1) % 2, KT)
            e2 = S.op("scalar", lambda sc: sc.activation(
                atte[0:65, 0:1024], psab[0:65, 0:1024], AF.Identity),
                waits=[("mm", am), ("ve", prev_tail["tt"])], inc="ev")
            dts = {}
            for j in range(8):
                dts[j] = S.op("tensor", lambda t, j=j: t.transpose(
                    psp[:, j:j + 1], atte[64:65, 128 * j:128 * j + 128],
                    ident32[64:65, 64:65], tile_position=(64, 0)),
                    waits=[("ev", e2)] if j == 0 else (), inc="mm")
            r2 = S.op("vector", lambda v: v.reciprocal(
                dcolF[:, 0:8], psp[:, 0:8]),
                waits=[("mm", dts[7])], inc="ve", chain=False)
            for j in range(8):
                dts[j] = S.op("tensor", lambda t, j=j: t.transpose(
                    psp[0:1, 128 * j:128 * j + 128], dcolF[:, j:j + 1],
                    ident32[:]),
                    waits=[("ve", r2)] if j == 0 else (), inc="mm")
            erow = S.op("scalar", lambda sc: sc.activation(
                rrowB[0:1, 0:1024], psp[0:1, 0:1024], AF.Identity),
                waits=[("mm", dts[7])], inc="ev")
            S.op("tensor", lambda t: t.matmul(
                psp[0:64, 0:512], onesbf[0:1, 0:64], rrowB[0:1, 0:512],
                start=True, stop=True),
                waits=[("ev", erow)])
            b2m = S.op("tensor", lambda t: t.matmul(
                psp[0:64, 512:1024], onesbf[0:1, 0:64],
                rrowB[0:1, 512:1024], start=True, stop=True), inc="mm")
            S.op("vector", lambda v, p=p, qg=qg: v.tensor_tensor(
                arena[0:64, T * p + 512 * qg:T * p + 512 * qg + 512],
                atte[0:64, 0:512], psp[0:64, 0:512], ALU.mult),
                waits=[("mm", b2m)], inc="ve", chain=False)
            vB = S.op("vector", lambda v, p=p: v.tensor_tensor(
                atteB[0:64, :], atte[0:64, 512:1024],
                psp[0:64, 512:1024], ALU.mult),
                waits=[("gv", prev_tail["gv"])], inc="ve", chain=False)
            dB = S.op("sync", lambda g, p=p, qg=qg: g.dma_start(
                out=arena[64:128, T * p + 512 * qg:T * p + 512 * qg + 512],
                in_=atteB[0:64, :]), waits=[("ve", vB)], inc="gv", k=16)
            prev_tail = dict(ev=expB[KT - 1], ve=S.cnt["ve"], gv=dB,
                             mask=maskve[KT - 1], e2=e2, tt=S.cnt["ve"])

    S.fence_all()
    wo = [wqk[0], wqk2[0]]
    wod = wdma(S, sb3(wo[0], 128), cc3(W["l0a_wout"].ap()[:, 0:128]))
    vres = {}
    for ft in range(NCH):
        cur = wo[ft % 2]
        wcur = wod
        if ft < 7:
            wod = wdma(S, sb3(wo[(ft + 1) % 2], 128),
                       cc3(W["l0a_wout"].ap()[:, 128 * (ft + 1):128 * (ft + 1) + 128]),
                       waits=[("mm", S.cnt["mm"])])
        for qg in range(QGN):
            k = 4 * ft + qg
            par = k % 2
            w = []
            if qg == 0:
                w.append(wcur)
            if k >= 2:
                w.append(("ve", vres[k - 2]))
            for c in range(NCH):
                S.op("tensor", lambda t, qg=qg, c=c, cur=cur, par=par: t.matmul(
                    psp[:, 512 * par:512 * par + 512], cur[:, 128 * c:128 * c + 128],
                    arena[:, T * c + 512 * qg:T * c + 512 * qg + 512],
                    start=(c == 0), stop=False), waits=w if c == 0 else ())
            mo = S.op("tensor", lambda t, ft=ft, par=par: t.matmul(
                psp[:, 512 * par:512 * par + 512], borow[0:1, 128 * ft:128 * ft + 128],
                onesbf[0:1, 0:512], start=False, stop=True), inc="mm")
            vres[k] = S.op("vector", lambda v, ft=ft, qg=qg, par=par: v.tensor_tensor(
                x32[:, T * ft + 512 * qg:T * ft + 512 * qg + 512],
                psp[:, 512 * par:512 * par + 512],
                x32[:, T * ft + 512 * qg:T * ft + 512 * qg + 512], ALU.add),
                waits=[("mm", mo)], inc="ve")
    S.fence_all()
    S.emit()
    es.close()
    ln_boundary(st, "_l0a%d" % b)



def phase_l0f(st, b):
    S, nc, W = st["S"], st["nc"], st["W"]
    x32, xbf, arena = st["x32"], st["xbf"], st["arena"]
    onesbf, ones32, identbf = st["onesbf"], st["ones32"], st["identbf"]
    es = ExitStack()
    sbt = lambda n, s, d: es.enter_context(nc.sbuf_tensor(n + "_l0f%d" % b, s, d))
    pst = lambda n, s, d: es.enter_context(nc.psum_tensor(n + "_l0f%d" % b, s, d))

    gwb = sbt("gwb", [128, 64], BF16)
    rsc = sbt("rsc", [128, 1], F32)
    gateT = sbt("gateT", [8, T], BF16)
    grows = [sbt("grows%d" % i, [128, T], BF16) for i in range(2)]
    esg = ExitStack()
    sbg = lambda n, sh, d: esg.enter_context(nc.sbuf_tensor(n + "_l0fg%d" % b, sh, d))
    gs = sbg("gs", [128, 128], F32)
    m1s = sbg("m1s", [128, 16], F32)
    m2s = sbg("m2s", [128, 16], F32)
    d8 = sbg("d8", [128, 128], F32)
    ind1 = sbg("ind1", [128, 128], F32)
    xp8 = sbg("xp8", [128, 128], F32)
    ind2 = sbg("ind2", [128, 128], F32)
    gfull = xp8
    rden = sbg("rden", [128, 16], F32)
    gfb = sbg("gfb", [128, 128], BF16)
    psg = esg.enter_context(nc.psum_tensor("psg_l0f%d" % b, [128, 1024], F32))
    psgt = esg.enter_context(nc.psum_tensor("psgt_l0f%d" % b, [8, 2048], BF16))


    S.fence_all()
    S.op("sync", lambda s: s.dma_start(
        out=rsc[:, :], in_=W["l0f_rs"].ap().to_broadcast((128, 1))), inc="wq", k=16)
    ld0 = wdma(S, sb3(gwb, 8), cc3(W["l0f_gw"].ap()))
    wq0 = S.cnt["wq"]

    gev = {}
    for tt in range(TTN):
        par = tt % 2
        for c in range(NCH):
            w = []
            if tt == 0 and c == 0:
                w = [ld0, ("wq", wq0)]
            if c == 0 and tt >= 2:
                w.append(("ev", gev[tt - 2]))
            S.op("tensor", lambda t, tt=tt, c=c, par=par: t.matmul(
                psg[:, 512 * par:512 * par + 8],
                xbf[:, T * c + 128 * tt:T * c + 128 * tt + 128],
                gwb[:, 8 * c:8 * c + 8], start=(c == 0), stop=(c == NCH - 1)),
                waits=w, inc="mm" if c == NCH - 1 else None)
        gev[tt] = S.op("scalar", lambda sc, tt=tt, par=par: sc.activation(
            gs[:, 8 * tt:8 * tt + 8], psg[:, 512 * par:512 * par + 8], AF.Sigmoid),
            waits=[("mm", S.cnt["mm"])], inc="ev")
    S.op("vector", lambda v: v.tensor_scalar(gs[:, :], gs[:, :], rsc[:, 0:1],
                                             None, ALU.mult),
         waits=[("ev", S.cnt["ev"]), ("wq", wq0)], inc="ve")
    for tt in range(TTN):
        S.op("vector", lambda v, tt=tt: v.reduce_max(
            m1s[:, tt:tt + 1], gs[:, 8 * tt:8 * tt + 8], axis=AX.X),
            inc="ve" if tt == TTN - 1 else None)
    g1 = S.cnt["ve"]
    for tt in range(TTN):
        S.op("vector", lambda v, tt=tt: v.tensor_scalar(
            d8[:, 8 * tt:8 * tt + 8], gs[:, 8 * tt:8 * tt + 8],
            m1s[:, tt:tt + 1], 1e15, ALU.subtract, ALU.mult),
            waits=[("ve", g1)] if tt == 0 else (),
            inc="ve" if tt == TTN - 1 else None)
    e1 = S.op("scalar", lambda sc: sc.activation(ind1[:, :], d8[:, :], AF.Exp),
              waits=[("ve", S.cnt["ve"])], inc="ev")
    S.op("vector", lambda v: v.scalar_tensor_tensor(
        xp8[:, :], ind1[:, :], -1e22, gs[:, :], ALU.mult, ALU.add),
        waits=[("ev", e1)], inc="ve")
    for tt in range(TTN):
        S.op("vector", lambda v, tt=tt: v.reduce_max(
            m2s[:, tt:tt + 1], xp8[:, 8 * tt:8 * tt + 8], axis=AX.X),
            inc="ve" if tt == TTN - 1 else None)
    g2 = S.cnt["ve"]
    for tt in range(TTN):
        S.op("vector", lambda v, tt=tt: v.tensor_scalar(
            d8[:, 8 * tt:8 * tt + 8], xp8[:, 8 * tt:8 * tt + 8],
            m2s[:, tt:tt + 1], 1e15, ALU.subtract, ALU.mult),
            waits=[("ve", g2)] if tt == 0 else (),
            inc="ve" if tt == TTN - 1 else None)
    e2 = S.op("scalar", lambda sc: sc.activation(ind2[:, :], d8[:, :], AF.Exp),
              waits=[("ve", S.cnt["ve"])], inc="ev")
    S.op("vector", lambda v: v.tensor_tensor(ind1[:, :], ind1[:, :], ind2[:, :],
                                             ALU.add),
         waits=[("ev", e2)], inc="ve")
    for tt in range(TTN):
        S.op("vector", lambda v, tt=tt: v.tensor_scalar(
            rden[:, tt:tt + 1], m1s[:, tt:tt + 1], m2s[:, tt:tt + 1], None, ALU.add),
            inc="ve" if tt == TTN - 1 else None)
    vpre = S.cnt["ve"]
    S.op("vector", lambda v: v.reciprocal(rden[:, :], rden[:, :]),
         waits=[("ve", vpre)], inc="ve")
    g3 = S.op("vector", lambda v: v.tensor_tensor(gfull[:, :], gs[:, :], ind1[:, :],
                                                  ALU.mult), inc="ve")
    for tt in range(TTN):
        S.op("vector", lambda v, tt=tt: v.tensor_scalar(
            gfull[:, 8 * tt:8 * tt + 8], gfull[:, 8 * tt:8 * tt + 8],
            rden[:, tt:tt + 1], None, ALU.mult),
            waits=[("ve", g3)] if tt == 0 else (),
            inc="ve" if tt == TTN - 1 else None)
    vgf = S.op("vector", lambda v: v.tensor_copy(gfb[:, :], gfull[:, :]),
               waits=[("ve", S.cnt["ve"])], inc="ve")
    gtev = {}
    for tt in range(TTN):
        par = tt % 2
        w = [("ve", vgf)] if tt == 0 else []
        if tt >= 2:
            w.append(("ev", gtev[tt - 2]))
        mt = S.op("tensor", lambda t, tt=tt, par=par: t.transpose(
            psgt[0:8, 1024 * par:1024 * par + 128], gfb[:, 8 * tt:8 * tt + 8],
            identbf[:]), waits=w, inc="mm")
        gtev[tt] = S.op("scalar", lambda sc, tt=tt, par=par: sc.activation(
            gateT[0:8, 128 * tt:128 * tt + 128], psgt[0:8, 1024 * par:1024 * par + 128],
            AF.Identity), waits=[("mm", mt)], inc="ev")
    grow_wg = None
    for e in range(8):
        grow_wg = S.op("gpsimd", lambda g, e=e: g.dma_start(
            out=grows[e // 4][32 * (e % 4):32 * (e % 4) + 1, :],
            in_=gateT[e:e + 1, :]),
            waits=[("ev", S.cnt["ev"])] if e == 0 else (), inc="wg", k=16)

    S.emit()
    esg.close()
    hext = sbt("hext", [128, 20 * 1024], BF16)
    w1b = [sbt("w1b%d" % i, [128, NCH * 256], BF16) for i in range(2)]
    w2b = [sbt("w2b%d" % i, [128, 4 * 128], BF16) for i in range(2)]
    b1c = sbt("b1c", [128, 4], F32)
    b2sb = sbt("b2sb", [8, D], BF16)
    bs2row = sbt("bs2row", [1, D], BF16)
    psh = pst("psh", [128, 1024], F32)
    psgb = pst("psgb", [128, 1024], F32)
    psy = pst("psy", [128, 1024], F32)

    def hslot(e, hc):
        slot = 4 * e + hc
        if slot < 16:
            return arena[:, 1024 * slot:1024 * slot + 1024]
        return hext[:, 1024 * (slot - 16):1024 * (slot - 16) + 1024]

    b2k = wdma(S, b2sb[:, :], W["l0f_b2"].ap())
    b2_wg = wdma(S, bs2row[0:1, :], W["l0f_bs2"].ap().unsqueeze(0))

    def w1src(e, half):
        wsl = W["l0f_w1"].ap()[e] if e < 8 else W["l0f_ws1"].ap()
        return cc3(wsl[:, 256 * half:256 * half + 256])

    for th in range(2):
        w1d_next = wdma(S, sb3(w1b[0], 256), w1src(0, 0),
                        waits=[("mm", S.cnt["mm"]), ("ve", S.cnt["ve"]),
                               ("ev", S.cnt["ev"])])
        gb = None
        hdone = {}
        for e in range(9):
            bsrc = bias_col(W["l0f_b1"].ap()[e]) if e < 8 else bias_col(W["l0f_bs1"].ap())
            b1d = S.op("sync", lambda s, bsrc=bsrc: s.dma_start(
                out=b1c[:, :], in_=bsrc, allow_slow_non_contiguous=True),
                waits=[("ev", S.cnt["ev"])], inc="wq", k=16)
            if e < 8:
                S.op("tensor", lambda t, e=e, th=th: t.matmul(
                    psgb[:, 0:512], onesbf[32 * (e % 4):32 * (e % 4) + 1, 0:128],
                    grows[e // 4][32 * (e % 4):32 * (e % 4) + 1,
                                  1024 * th:1024 * th + 512],
                    start=True, stop=True, tile_position=(32 * (e % 4), 0)),
                    waits=[("wg", grow_wg), ("ve", S.cnt["ve"])])
                gb = S.op("tensor", lambda t, e=e, th=th: t.matmul(
                    psgb[:, 512:1024], onesbf[32 * (e % 4):32 * (e % 4) + 1, 0:128],
                    grows[e // 4][32 * (e % 4):32 * (e % 4) + 1,
                                  1024 * th + 512:1024 * th + 1024],
                    start=True, stop=True, tile_position=(32 * (e % 4), 0)), inc="mm")
            for half in range(2):
                wcur = w1d_next
                cbuf = w1b[(2 * e + half) % 2]
                if not (e == 8 and half == 1):
                    ne, nh = (e, 1) if half == 0 else (e + 1, 0)
                    w1d_next = wdma(S, sb3(w1b[(2 * e + half + 1) % 2], 256),
                                    w1src(ne, nh), waits=[("mm", S.cnt["mm"])])
                for hc in (2 * half, 2 * half + 1):
                    for qg in range(2):
                        for c in range(NCH):
                            w = []
                            if qg == 0 and c == 0:
                                w = [wcur, ("wq", b1d), ("ev", S.cnt["ev"])]
                            S.op("tensor", lambda t, hc=hc, qg=qg, c=c, cbuf=cbuf,
                                 half=half, th=th: t.matmul(
                                psh[:, 512 * qg:512 * qg + 512],
                                cbuf[:, 256 * c + 128 * (hc % 2):256 * c + 128 * (hc % 2) + 128],
                                xbf[:, T * c + 1024 * th + 512 * qg:
                                    T * c + 1024 * th + 512 * qg + 512],
                                start=(c == 0), stop=(c == NCH - 1)),
                                waits=w, inc="mm" if (qg == 1 and c == NCH - 1) else None)
                    he = S.op("scalar", lambda sc, e=e, hc=hc: sc.activation(
                        hslot(e, hc), psh[:, :], AF.Gelu, bias=b1c[:, hc:hc + 1]),
                        waits=[("mm", S.cnt["mm"])], inc="ev")
                    if e < 8:
                        hdone[(e, hc)] = S.op("vector", lambda v, e=e, hc=hc: v.tensor_tensor(
                            hslot(e, hc), hslot(e, hc), psgb[:, :], ALU.mult),
                            waits=[("ev", he), ("mm", gb)], inc="ve")
                    else:
                        hdone[(e, hc)] = S.cnt["ve"]
        hready = dict(ve=S.cnt["ve"], ev=S.cnt["ev"])

        def w2src(e, ft):
            if e < 8:
                return cc3(W["l0f_w2"].ap()[e][:, 128 * ft:128 * ft + 128])
            return cc3(W["l0f_ws2"].ap()[:, 128 * ft:128 * ft + 128])

        w2d_next = wdma(S, sb3(w2b[0], 128), w2src(0, 0), waits=[("mm", S.cnt["mm"])])
        vres = {}
        for ft in range(NCH):
            for e in range(9):
                idx = ft * 9 + e
                wcur = w2d_next
                cbuf = w2b[idx % 2]
                pre_mm = S.cnt["mm"]
                for qg in range(2):
                    for hc in range(4):
                        w = []
                        if e == 0 and hc == 0:
                            w = [wcur, ("ve", hready["ve"]), ("ev", hready["ev"])]
                            if ft >= 1 and qg == 0:
                                w.append(("ve", vres[(ft - 1, 1)]))
                        elif hc == 0 and qg == 0:
                            w = [wcur]
                        last = (qg == 1 and hc == 3)
                        S.op("tensor", lambda t, e=e, hc=hc, qg=qg, cbuf=cbuf: t.matmul(
                            psy[:, 512 * qg:512 * qg + 512],
                            cbuf[:, 128 * hc:128 * hc + 128],
                            hslot(e, hc)[:, 512 * qg:512 * qg + 512],
                            start=(e == 0 and hc == 0), stop=False), waits=w,
                            inc="mm" if last else None)
                nft, ne = (ft, e + 1) if e < 8 else (ft + 1, 0)
                if nft < NCH:
                    w2d_next = wdma(S, sb3(w2b[(idx + 1) % 2], 128), w2src(ne, nft),
                                    waits=[("mm", pre_mm)])
            for qg in range(2):
                S.op("tensor", lambda t, ft=ft, th=th, qg=qg: t.matmul(
                    psy[:, 512 * qg:512 * qg + 512], b2sb[0:8, 128 * ft:128 * ft + 128],
                    gateT[0:8, 1024 * th + 512 * qg:1024 * th + 512 * qg + 512],
                    start=False, stop=False),
                    waits=[b2k] if (ft == 0 and qg == 0) else ())
                my = S.op("tensor", lambda t, ft=ft, qg=qg: t.matmul(
                    psy[:, 512 * qg:512 * qg + 512], bs2row[0:1, 128 * ft:128 * ft + 128],
                    onesbf[0:1, 0:512], start=False, stop=True),
                    waits=[b2_wg] if (ft == 0 and qg == 0) else (), inc="mm")
                vres[(ft, qg)] = S.op("vector", lambda v, ft=ft, th=th, qg=qg: v.tensor_tensor(
                    x32[:, T * ft + 1024 * th + 512 * qg:T * ft + 1024 * th + 512 * qg + 512],
                    psy[:, 512 * qg:512 * qg + 512],
                    x32[:, T * ft + 1024 * th + 512 * qg:T * ft + 1024 * th + 512 * qg + 512],
                    ALU.add), waits=[("mm", my)], inc="ve")
        S.fence_all()
    S.emit()
    es.close()
    ln_boundary(st, "_l0f%d" % b)



def phase_l1a(st, b):
    S, nc, W = st["S"], st["nc"], st["W"]
    x32, xbf, arena = st["x32"], st["xbf"], st["arena"]
    onesbf, ones32, maskc, identbf = st["onesbf"], st["ones32"], st["maskc"], st["identbf"]
    pswap, CD = st["pswap"], st["CD"]
    es = ExitStack()
    sbt = lambda n, s, d: es.enter_context(nc.sbuf_tensor(n + "_l1a%d" % b, s, d))
    pst = lambda n, s, d: es.enter_context(nc.psum_tensor(n + "_l1a%d" % b, s, d))
    SC = 1.0 / math.sqrt(96.0)
    ropeC = sbt("ropeC", [128, T], BF16)
    ropeS = sbt("ropeS", [128, T], BF16)

    latT = sbt("latT", [128, 2 * T], BF16)
    qrp = [sbt("qrp%d" % i, [128, T], BF16) for i in range(2)]
    qcp = sbt("qcp", [128, T], BF16)
    kcp = sbt("kcp", [128, T], BF16)
    qasm = [sbt("qasm%d" % i, [128, T], BF16) for i in range(2)]
    kasm = [sbt("kasm%d" % i, [128, T], BF16) for i in range(2)]
    vaug = [sbt("vaug%d" % i, [128, TTN * 65], BF16) for i in range(2)]
    mega = sbt("mega", [128, 16 * 512], BF16)
    ppr = mega
    q2full = nc.virtual_sbuf_tensor("q2_l1a%d" % b, [128, 16 * 512], BF16)
    nc.back_virtual_tensor_with(q2full, mega)
    q2 = q2full[:, 0:T]
    krT = q2full[:, T:2 * T]
    macc = sbt("macc", [128, 1536], BF16)
    pmaxs = sbt("pmaxs", [128, 4], BF16)
    scr = sbt("scr", [128, 512], BF16)
    dcol = sbt("dcol", [128, 4], BF16)
    atteS = sbt("atteS", [128, 512], BF16)
    scrB = atteS[64:128, :]
    nB = sbt("nB", [128, 1], F32)
    ccol = sbt("ccol", [128, 16], F32)
    lrow = sbt("lrow", [1, 16], F32)
    sc1 = sbt("sc1", [128, 8], F32)
    wtv = nc.virtual_sbuf_tensor("wtv_l1a%d" % b, [128, 16 * 512], BF16)
    nc.back_virtual_tensor_with(wtv, mega)
    wt = [wtv[:, 4096:5120], wtv[:, 5120:6144]]
    wkv = wtv[:, 6144:6656]

    pss = pst("pss", [128, 2048], F32)
    psp = pss[:, 0:1024]
    psmt = pst("psmt", [128, 1024], BF16)
    psatF = pst("psatF", [128, 512], F32)
    prow = pst("prow", [128, 512], BF16)
    psbc = pst("psbc", [128, 512], F32)

    S.fence_all()
    S.op("sync", lambda s: s.dma_start(out=ropeC[:], in_=CD["ropeC"].ap()), inc="wq", k=16)
    rop_wq = S.op("sync", lambda s: s.dma_start(out=ropeS[:], in_=CD["ropeS"].ap()),
                  inc="wq", k=16)
    ldl = S.op("sync", lambda s: s.dma_start(out=lrow[0:1, :], in_=W["l1a_lobo"].ap()),
               inc="wq", k=16)
    el = S.op("scalar", lambda sc: sc.activation(lrow[0:1, :], lrow[0:1, :], AF.Exp),
              waits=[("wq", ldl)], inc="ev")
    ml = S.op("tensor", lambda t: t.matmul(psp[:, 0:16], ones32[0:1, 0:128], lrow[0:1, :],
                                           start=True, stop=True),
              waits=[("ev", el)], inc="mm")
    ecc = S.op("scalar", lambda sc: sc.activation(ccol[:, :], psp[:, 0:16], AF.Identity),
               waits=[("mm", ml)], inc="ev")

    def proj_fm(dst, wbuf, nchunks, src, srcT, evwaits, tag):
        evs = []
        for qh in range(2):
            for qg2 in range(2):
                qg = 2 * qh + qg2
                for c in range(nchunks):
                    w = []
                    if qh == 0 and qg2 == 0 and c == 0:
                        w = list(evwaits)
                    elif qh == 1 and qg2 == 0 and c == 0:
                        w = [("ev", evs[0])]
                    S.op("tensor", lambda t, qg=qg, qg2=qg2, c=c: t.matmul(
                        psp[:, 512 * qg2:512 * qg2 + 512],
                        wbuf[:, 128 * c:128 * c + 128],
                        src[:, srcT * c + 512 * qg:srcT * c + 512 * qg + 512],
                        start=(c == 0), stop=(c == nchunks - 1)),
                        waits=w, inc="mm" if (qg2 == 1 and c == nchunks - 1) else None)
            evs.append(S.op("scalar", lambda sc, qh=qh: sc.activation(
                dst[:, 1024 * qh:1024 * qh + 1024], psp[:, :], AF.Identity),
                waits=[("mm", S.cnt["mm"])], inc="ev"))
        return evs[-1]

    for lc in range(2):
        wd = wdma(S, sb3(wt[lc], 128), cc3(W["l1a_wkvd"].ap()[:, 128 * lc:128 * lc + 128]),
                  waits=[("mm", S.cnt["mm"])] if lc == 0 else ())
        proj_fm(_slice2(latT, T * lc, T),
                wt[lc], NCH, xbf, T,
                [wd, ("ev", max(ecc, S.cnt["ev"]))], "lat%d" % lc)
    wd = wdma(S, sb3(wt[0], 32)[:, 0:8, :], cc3(W["l1a_wkr"].ap()),
              waits=[("ev", S.cnt["ev"])])
    krev = {}
    for qg in range(QGN):
        for c in range(NCH):
            w = []
            if qg == 0 and c == 0:
                w = [wd]
            elif qg >= 2 and c == 0:
                w = [("ev", krev[qg - 2])]
            S.op("tensor", lambda t, qg=qg, c=c: t.matmul(
                psp[0:32, 512 * (qg % 2):512 * (qg % 2) + 512], wt[0][:, 32 * c:32 * c + 32],
                xbf[:, T * c + 512 * qg:T * c + 512 * qg + 512],
                start=(c == 0), stop=(c == NCH - 1)), waits=w,
                inc="mm" if c == NCH - 1 else None)
        krev[qg] = S.op("scalar", lambda sc, qg=qg: sc.activation(
            krT[0:32, 512 * qg:512 * qg + 512], psp[0:32, 512 * (qg % 2):512 * (qg % 2) + 512],
            AF.Identity), waits=[("mm", S.cnt["mm"])], inc="ev")
    ekr = S.cnt["ev"]

    def rope2(srcbuf, rows, evwait):
        lastve = None
        for qg in range(QGN):
            half = qg % 2
            mm1 = S.op("tensor", lambda t, qg=qg, half=half: t.matmul(
                psp[0:rows, 512 * half:512 * half + 512], pswap[0:rows, 0:rows],
                srcbuf[0:rows, 512 * qg:512 * qg + 512], start=True, stop=True),
                waits=([("ev", evwait), ("wq", rop_wq)] if qg == 0 else []) +
                      ([("ve", lastve)] if qg >= 2 else []), inc="mm")
            S.op("vector", lambda v, qg=qg, half=half: v.tensor_tensor(
                q2[0:rows, 512 * half:512 * half + 512],
                psp[0:rows, 512 * half:512 * half + 512],
                ropeS[0:rows, 512 * qg:512 * qg + 512], ALU.mult),
                waits=[("mm", mm1)], inc="ve")
            S.op("vector", lambda v, qg=qg: v.tensor_tensor(
                srcbuf[0:rows, 512 * qg:512 * qg + 512],
                srcbuf[0:rows, 512 * qg:512 * qg + 512],
                ropeC[0:rows, 512 * qg:512 * qg + 512], ALU.mult), inc="ve")
            lastve = S.op("vector", lambda v, qg=qg, half=half: v.tensor_tensor(
                srcbuf[0:rows, 512 * qg:512 * qg + 512],
                srcbuf[0:rows, 512 * qg:512 * qg + 512],
                q2[0:rows, 512 * half:512 * half + 512], ALU.add), inc="ve")
        return lastve

    kr_ve = rope2(krT, 32, ekr)
    krd0 = S.op("sync", lambda g: g.dma_start(
        out=kasm[0][64:96, :], in_=krT[0:32, :]), waits=[("ve", kr_ve)],
        inc="gv", k=16)
    krd1 = S.op("sync", lambda g: g.dma_start(
        out=kasm[1][64:96, :], in_=krT[0:32, :]), inc="gv", k=16)

    qr_ve = {}

    def make_qr_pack(pk, buf):
        wd = wdma(S, sb3(wt[1], 128), cc3(W["l1a_wqr"].ap()[:, 128 * pk:128 * pk + 128]),
                  waits=([("mm", prev_att), ("ve", prev_vm)] if prev_vm is not None
                         else [("mm", S.cnt["mm"]), ("ve", S.cnt["ve"])]))
        ev = proj_fm(buf, wt[1], NCH, xbf, T,
                     [wd, ("ve", S.cnt["ve"]), ("ev", S.cnt["ev"])], "qr")
        return rope2(buf, 128, ev)

    prev_eex = None
    prev_e4 = None
    prev_att = 0
    prev_vm = None
    qr_ve[0] = make_qr_pack(0, qrp[0])
    prev_vma = None
    prev_KT = 4
    prev_evat = 0
    prev_tt = 0
    prev_trans_mm = 0
    prev_odma = 0
    for h in range(16):
        pr = h // 2
        pk = h // 4
        am = h % 2
        if h % 2 == 0:
            wv_pre = ([("mm", prev_att), ("ve", prev_vm)] if prev_vm is not None
                      else [("mm", S.cnt["mm"]), ("ve", S.cnt["ve"])])
            wd = wdma(S, sb3(wt[0], 128), cc3(W["l1a_wq"].ap()[:, 128 * pr:128 * pr + 128]),
                      waits=list(wv_pre))
            eqc = proj_fm(qcp, wt[0], NCH, xbf, T,
                          [wd, ("ev", S.cnt["ev"])], "qc")
            wd2 = wdma(S, sb3(wkv, 128)[:, 0:2, :],
                       cc3(W["l1a_wku"].ap()[:, 128 * pr:128 * pr + 128]),
                       waits=list(wv_pre))
            ekc = proj_fm(kcp, wkv, 2, latT, T,
                          [wd2, ("ev", S.cnt["ev"])], "kc")
            wd3 = wdma(S, sb3(wkv, 128)[:, 2:4, :],
                       cc3(W["l1a_wvu"].ap()[:, 128 * pr:128 * pr + 128]),
                       waits=list(wv_pre))
            vv = {}
            for tt in range(TTN):
                par = tt % 2
                w = [wd3, ("ev", S.cnt["ev"])] if tt == 0 else []
                if tt >= 2:
                    w.append(("ev", vv[tt - 2]))
                for lc in range(2):
                    S.op("tensor", lambda t, tt=tt, lc=lc, par=par: t.matmul(
                        psp[:, 512 * par:512 * par + 128],
                        latT[:, T * lc + 128 * tt:T * lc + 128 * tt + 128],
                        wkv[:, 256 + 128 * lc:256 + 128 * lc + 128],
                        start=(lc == 0), stop=(lc == 1)),
                        waits=w if lc == 0 else (), inc="mm" if lc == 1 else None)
                mmv = S.cnt["mm"]
                S.op("scalar", lambda sc, tt=tt, par=par: sc.activation(
                    vaug[0][:, 65 * tt:65 * tt + 64], psp[:, 512 * par:512 * par + 64],
                    AF.Identity), waits=[("mm", mmv)], inc="ev")
                vv[tt] = S.op("scalar", lambda sc, tt=tt, par=par: sc.activation(
                    vaug[1][:, 65 * tt:65 * tt + 64],
                    psp[:, 512 * par + 64:512 * par + 128], AF.Identity), inc="ev")
            S.op("vector", lambda v: v.memset(
                vaug[0][:].rearrange("p (a n) -> p a n", n=65)[:, :, 64:65], 1.0),
                waits=[("ev", S.cnt["ev"])], inc="ve")
            S.op("vector", lambda v: v.memset(
                vaug[1][:].rearrange("p (a n) -> p a n", n=65)[:, :, 64:65], 1.0),
                inc="ve")
        if h % 4 == 0 and pk + 1 < 4:
            qr_ve[pk + 1] = make_qr_pack(pk + 1, qrp[(pk + 1) % 2])

        asm_deps = []
        if h % 2 == 0:
            a1 = S.op("sync", lambda g, am=am: g.dma_start(
                out=qasm[am][0:64, :], in_=qcp[0:64, :]),
                waits=[("ev", S.cnt["ev"]), ("mm", S.cnt["mm"])], inc="gv", k=16)
            a2 = S.op("sync", lambda g, am=am: g.dma_start(
                out=kasm[am][0:64, :], in_=kcp[0:64, :]), inc="gv", k=16)
            asm_deps.append(("gv", a2))
        else:
            a1 = S.op("sync", lambda g, am=am: g.dma_start(
                out=qasm[am][0:64, :], in_=qcp[64:128, :]),
                waits=[("ev", S.cnt["ev"]), ("mm", S.cnt["mm"]), ("ve", S.cnt["ve"])],
                inc="gv", k=16)
            a2 = S.op("sync", lambda g, am=am: g.dma_start(
                out=kasm[am][0:64, :], in_=kcp[64:128, :]), inc="gv", k=16)
            asm_deps.append(("gv", a2))
        a3 = S.op("sync", lambda g, am=am, pk=pk, h=h: g.dma_start(
            out=qasm[am][64:96, :], in_=qrp[pk % 2][32 * (h % 4):32 * (h % 4) + 32, :]),
            waits=[("ve", qr_ve[pk])], inc="gv", k=16)
        asm_deps.append(("gv", a3))

        ebn = bcalc(S, pss[:, 1024:2048], pss[:, 0:1024], sc1, nB, q2,
                    qasm[am], kasm[am], 96, SC, onesbf, ones32,
                    asm_deps=asm_deps + [("mm", S.cnt["mm"]), ("ve", S.cnt["ve"])])

        for qg in range(QGN):
            KT = 4 * (qg + 1)
            eex = {}
            for kti in range(KT):
                w = []
                if kti == 0:
                    w = [("ev", ebn)]
                    if prev_vm is not None:
                        w.append(("ve", prev_vm if prev_KT == 4 else prev_vma))
                if kti < 4 and prev_e4 is not None:
                    w.append(("ev", prev_e4[kti]))
                if prev_vm is not None and kti == prev_KT - 4 and kti > 0:
                    w.append(("ve", prev_vm))
                if kti >= 4:
                    w.append(("ev", eex[kti - 4]))
                smm = S.op("tensor", lambda t, am=am, qg=qg, kti=kti: t.matmul(
                    pss[:, 512 * (kti % 4):512 * (kti % 4) + 512],
                    kasm[am][0:96, 128 * kti:128 * kti + 128],
                    qasm[am][0:96, 512 * qg:512 * qg + 512], start=True, stop=True),
                    waits=w, inc="mm")
                eex[kti] = S.op("scalar", lambda sc, kti=kti: sc.activation(
                    ppr[:, 512 * kti:512 * kti + 512],
                    pss[:, 512 * (kti % 4):512 * (kti % 4) + 512],
                    AF.Exp, bias=nB[:, 0:1], scale=np.float32(SC)),
                    waits=[("mm", smm), ("ev", ebn)], inc="ev")
            prev_eex = eex[KT - 1]
            prev_e4 = [eex[KT - 4 + j] for j in range(4)]
            ch0 = (prev_vm if prev_vm is not None else False)
            ND = KT - 4
            if ND >= 8:
                mid = ND // 2
                va1 = S.op("vector", lambda v, mid=mid: v.reduce_max(
                    macc[:, 0:512],
                    ppr[:, 0:512 * mid].rearrange("p (t q) -> p q t", t=mid),
                    axis=AX.X),
                    waits=[("ev", eex[mid - 1]), ("mm", prev_trans_mm)],
                    inc="ve", chain=ch0)
                va2 = S.op("vector", lambda v, mid=mid, ND=ND: v.reduce_max(
                    macc[:, 512:1024],
                    ppr[:, 512 * mid:512 * ND].rearrange(
                        "p (t q) -> p q t", t=ND - mid),
                    axis=AX.X),
                    waits=[("ev", eex[ND - 1])], inc="ve", chain=False)
                vma = S.op("vector", lambda v: v.reduce_max(
                    macc[:, 1024:1536],
                    macc[:, 0:1024].rearrange("p (b q) -> p q b", b=2),
                    axis=AX.X), inc="ve", chain=va2)
            elif ND > 0:
                vma = S.op("vector", lambda v, ND=ND: v.reduce_max(
                    macc[:, 1024:1536],
                    ppr[:, 0:512 * ND].rearrange("p (t q) -> p q t", t=ND),
                    axis=AX.X),
                    waits=[("ev", eex[ND - 1]), ("mm", prev_trans_mm)],
                    inc="ve", chain=ch0)
            else:
                vma = S.op("vector", lambda v: v.memset(macc[:, 1024:1536], 0.0),
                           waits=[("mm", prev_trans_mm)],
                           inc="ve", chain=ch0)
            mve = S.op("vector", lambda v, KT=KT: v.tensor_tensor(
                ppr[:, 512 * (KT - 4):512 * KT], ppr[:, 512 * (KT - 4):512 * KT],
                maskc[:, :], ALU.mult), waits=[("ev", prev_eex)], inc="ve",
                chain=False)
            for kti in range(KT):
                w = []
                if kti == KT - 4:
                    w.append(("ve", mve))
                elif kti < KT - 4:
                    w.append(("ev", eex[kti]))
                if kti == 0:
                    w.append(("ev", prev_evat))
                S.op("tensor", lambda t, kti=kti, h=h, KT=KT: t.matmul(
                    psatF[0:65, 0:512],
                    vaug[h % 2][:, 65 * kti:65 * kti + 65],
                    ppr[:, 512 * kti:512 * kti + 512],
                    start=(kti == 0), stop=(kti == KT - 1)),
                    waits=w, inc="mm" if kti == KT - 1 else None)
            att_done = S.cnt["mm"]
            ev1 = S.op("scalar", lambda sc: sc.activation(
                scr[64:65, 0:512], psatF[64:65, 0:512], AF.Identity),
                waits=[("mm", att_done), ("ve", prev_tt)], inc="ev")
            evat = S.op("scalar", lambda sc: sc.activation(
                atteS[0:64, 0:512], psatF[0:64, 0:512], AF.Identity),
                waits=[("gv", prev_odma)], inc="ev")
            vm = S.op("vector", lambda v, KT=KT: v.reduce_max(
                macc[:, 512:1024],
                ppr[:, 512 * (KT - 4):512 * KT].rearrange("p (t q) -> p q t", t=4),
                axis=AX.X), inc="ve", chain=mve)
            for jt in range(4):
                mt = S.op("tensor", lambda t, jt=jt: t.transpose(
                    psmt[:, 256 * jt:256 * jt + 128],
                    macc[:, 1024 + 128 * jt:1024 + 128 * jt + 128], identbf[:]),
                    waits=[("ve", vm)] if jt == 0 else (), inc="mm")
                mt = S.op("tensor", lambda t, jt=jt: t.transpose(
                    psmt[:, 256 * jt + 128:256 * jt + 256],
                    macc[:, 512 + 128 * jt:512 + 128 * jt + 128], identbf[:]),
                    inc="mm")
            mtr = S.op("vector", lambda v: v.reduce_max(
                pmaxs[:, 0:4], psmt[:, :].rearrange("p (b c) -> p b c", c=256),
                axis=AX.X), waits=[("mm", mt)], inc="ve", chain=False)
            dts = {}
            for j in range(4):
                dts[j] = S.op("tensor", lambda t, j=j: t.transpose(
                    psmt[:, 2 * j:2 * j + 1], scr[64:65, 128 * j:128 * j + 128],
                    identbf[64:65, 64:65], tile_position=(64, 0)),
                    waits=([("ve", mtr), ("ev", ev1)] if j == 0 else ()),
                    inc="mm")
            stt = S.op("vector", lambda v, h=h: _stt_bf16(
                nc, v,
                dcol[:, 0:4].rearrange("p (a b) -> p a b", b=1),
                pmaxs[:, 0:4].rearrange("p (a b) -> p a b", b=1),
                ccol[:, h:h + 1],
                psmt[:, 0:8].rearrange("p (a b) -> p a b", b=2)[:, :, 0:1]),
                waits=[("mm", dts[3])], inc="ve", chain=False)
            rcp = S.op("vector", lambda v: _rcp_bf16(
                nc, v, dcol[:, 0:4], dcol[:, 0:4]),
                inc="ve", chain=stt)
            prt = {}
            for jt in range(4):
                prt[jt] = S.op("tensor", lambda t, jt=jt: t.transpose(
                    prow[64:65, 128 * jt:128 * jt + 128], dcol[:, jt:jt + 1],
                    identbf[:], tile_position=(0, 64)),
                    waits=[("ve", rcp)] if jt == 0 else (), inc="mm")
            prev_trans_mm = S.cnt["mm"]
            erow = S.op("scalar", lambda sc: sc.activation(
                scrB[0:1, 0:512], prow[64:65, 0:512], AF.Identity),
                waits=[("mm", prt[3])], inc="ev")
            bc = S.op("tensor", lambda t: t.matmul(
                psbc[0:64, 0:512], onesbf[64:65, 0:64], scrB[0:1, 0:512],
                start=True, stop=True), waits=[("ev", erow)], inc="mm")
            acol = T * (h // 2) + 512 * qg
            if h % 2 == 0:
                tt = S.op("vector", lambda v, acol=acol: v.tensor_tensor(
                    arena[0:64, acol:acol + 512], atteS[0:64, 0:512],
                    psbc[0:64, 0:512], ALU.mult),
                    waits=[("mm", bc), ("ev", evat)], inc="ve", chain=False)
            else:
                tt = S.op("vector", lambda v: v.tensor_tensor(
                    atteS[0:64, 0:512], atteS[0:64, 0:512],
                    psbc[0:64, 0:512], ALU.mult),
                    waits=[("mm", bc), ("ev", evat)], inc="ve", chain=False)
                prev_odma = S.op("sync", lambda g, acol=acol: g.dma_start(
                    out=arena[64:128, acol:acol + 512], in_=atteS[0:64, 0:512]),
                    waits=[("ve", tt)], inc="gv", k=16)
            prev_tt = tt
            prev_evat = evat
            prev_att = att_done
            prev_vm = vm
            prev_vma = vma
            prev_KT = KT

    S.fence_all()
    wo = [wt[0], wt[1]]
    wod = wdma(S, sb3(wo[0], 128), cc3(W["l1a_wo"].ap()[:, 0:128]))
    vres = {}
    for ft in range(NCH):
        cur = wo[ft % 2]
        wcur = wod
        if ft < 7:
            wod = wdma(S, sb3(wo[(ft + 1) % 2], 128),
                       cc3(W["l1a_wo"].ap()[:, 128 * (ft + 1):128 * (ft + 1) + 128]),
                       waits=[("mm", S.cnt["mm"])])
        for qg in range(QGN):
            k = 4 * ft + qg
            par = k % 2
            w = []
            if qg == 0:
                w.append(wcur)
            if k >= 2:
                w.append(("ve", vres[k - 2]))
            for c in range(NCH):
                S.op("tensor", lambda t, qg=qg, c=c, cur=cur, par=par: t.matmul(
                    psp[:, 512 * par:512 * par + 512], cur[:, 128 * c:128 * c + 128],
                    arena[:, T * c + 512 * qg:T * c + 512 * qg + 512],
                    start=(c == 0), stop=(c == NCH - 1)),
                    waits=w if c == 0 else (),
                    inc="mm" if c == NCH - 1 else None)
            mo = S.cnt["mm"]
            vres[k] = S.op("vector", lambda v, ft=ft, qg=qg, par=par: v.tensor_tensor(
                x32[:, T * ft + 512 * qg:T * ft + 512 * qg + 512],
                psp[:, 512 * par:512 * par + 512],
                x32[:, T * ft + 512 * qg:T * ft + 512 * qg + 512], ALU.add),
                waits=[("mm", mo)], inc="ve")
    S.fence_all()
    S.emit()
    es.close()
    ln_boundary(st, "_l1a%d" % b)


def _slice2(buf, off, n):
    return buf[:, off:off + n]



def phase_l1f(st, b):
    S, nc, W = st["S"], st["nc"], st["W"]
    x32, xbf, arena = st["x32"], st["xbf"], st["arena"]
    onesbf, ones32, identbf = st["onesbf"], st["ones32"], st["identbf"]
    es = ExitStack()
    sbt = lambda n, s, d: es.enter_context(nc.sbuf_tensor(n + "_l1f%d" % b, s, d))
    SC = 1.0 / 32.0

    nwc = sbt("nwc", [128, 8], F32)
    nbc = sbt("nbc", [128, 8], F32)
    qbc = sbt("qbc", [128, 8], F32)
    kbc = sbt("kbc", [128, 8], F32)
    vbrow = sbt("vbrow", [1, D], BF16)
    obrow = sbt("obrow", [1, D], BF16)
    wraw = sbt("wraw", [128, 1024], BF16)
    qwT = sbt("qwT", [128, 1024], BF16)
    kwT = sbt("kwT", [128, 1024], BF16)
    vwT = sbt("vwT", [128, 1024], BF16)
    owT = sbt("owT", [128, 1024], BF16)
    qtb = sbt("qtb", [128, T], BF16)
    ktb = sbt("ktb", [128, T], BF16)
    vaug = sbt("vaug", [128, TTN * 129], BF16)
    ppr = sbt("ppr", [128, 16 * 512], BF16)
    attsc = sbt("attsc", [128, 4 * 128], BF16)
    sccol = sbt("sccol", [128, 4], F32)
    nB = sbt("nB", [128, 1], F32)
    q2 = sbt("q2", [128, T], BF16)
    sc1 = sbt("sc1", [128, 8], F32)

    S.fence_all()
    for dst, src in ((nwc, "l1f_nw"), (nbc, "l1f_nb"), (qbc, "l1f_qb"),
                     (kbc, "l1f_kb")):
        S.op("sync", lambda s, dst=dst, src=src: s.dma_start(
            out=dst[:, :], in_=bias_col(W[src].ap()),
            allow_slow_non_contiguous=True), inc="wq", k=16)
    vb_k = wdma(S, vbrow[0:1, :], W["l1f_vb"].ap().unsqueeze(0))
    ldw = wdma(S, obrow[0:1, :], W["l1f_ob"].ap().unsqueeze(0))
    wq0 = S.cnt["wq"]
    S.op("sync", lambda s: None, waits=[("wq", wq0), ldw])
    ln_boundary(st, "_l1fn%d" % b, affine=(nwc, nbc, arena))

    pst = lambda n, s, d: es.enter_context(nc.psum_tensor(n + "_l1f%d" % b, s, d))
    pss = pst("pss", [128, 2048], F32)
    psp = pss[:, 0:1024]
    psat = [pst("psat%d" % i, [128, 144], F32) for i in range(2)]
    psTs = [pst("psT%d" % i, [128, 128], BF16) for i in range(2)]

    wtr_state = dict(mm=0, ev=0)

    def wtrans(dst, wname, rowblk, mcols, extra=()):
        wd = wdma(S, wraw[:, 0:mcols],
                  W[wname].ap()[128 * rowblk:128 * rowblk + 128, 0:mcols],
                  waits=list(extra) + [("mm", wtr_state["mm"])])
        ev0 = S.cnt["ev"]
        evs = None
        for j in range(mcols // 128):
            w = [wd] if j == 0 else []
            if j < 2:
                w.append(("ev", ev0))
            else:
                w.append(("ev", evw[j - 2]))
            mt = S.op("tensor", lambda t, j=j: t.transpose(
                psTs[j % 2][:, :], wraw[:, 128 * j:128 * j + 128],
                identbf[:]), waits=w, inc="mm")
            if j == 0:
                evw = {}
            evw[j] = S.op("scalar", lambda sc, j=j, dst=dst: sc.activation(
                dst[:, 128 * j:128 * j + 128], psTs[j % 2][:, :], AF.Identity),
                waits=[("mm", mt)], inc="ev")
            evs = evw[j]
        wtr_state["mm"] = S.cnt["mm"]
        wtr_state["ev"] = evs
        return evs

    prev_tail = dict(mm=S.cnt["mm"], ve=S.cnt["ve"], ev=S.cnt["ev"], wg=S.cnt["wg"])
    prev_e4f = None
    for h in range(8):
        wtrans(qwT, "l1f_qw", h, 1024)
        wtrans(kwT, "l1f_kw", h, 1024)
        e3 = wtrans(vwT, "l1f_vw", h, 1024)
        qkev = dict(q=prev_tail["ev"], k=prev_tail["ev"])
        for e in range(4):
            mch = NCH >> e
            for c in range(mch):
                if e == 0 and c == 0:
                    w = [("ev", e3), ("ve", prev_tail["ve"])]
                elif c == 0:
                    w = [("ev", qkev["q"])]
                else:
                    w = ()
                S.op("tensor", lambda t, e=e, c=c, mch=mch: t.matmul(
                    psp[:, 0:512], qwT[:, 128 * c:128 * c + 128],
                    arena[:, T * c + 512 * e:T * c + 512 * e + 512],
                    start=(c == 0), stop=(c == mch - 1)), waits=w,
                    inc="mm" if c == mch - 1 else None)
            qkev["q"] = S.op("scalar", lambda sc, e=e, h=h: sc.activation(
                qtb[:, 512 * e:512 * e + 512], psp[:, 0:512], AF.Identity,
                bias=qbc[:, h:h + 1]), waits=[("mm", S.cnt["mm"])], inc="ev")
            for c in range(mch):
                S.op("tensor", lambda t, e=e, c=c, mch=mch: t.matmul(
                    psp[:, 512:1024], kwT[:, 128 * c:128 * c + 128],
                    arena[:, T * c + 512 * e:T * c + 512 * e + 512],
                    start=(c == 0), stop=(c == mch - 1)),
                    waits=[("ev", qkev["k"])] if c == 0 else (),
                    inc="mm" if c == mch - 1 else None)
            qkev["k"] = S.op("scalar", lambda sc, e=e, h=h: sc.activation(
                ktb[:, 512 * e:512 * e + 512], psp[:, 512:1024], AF.Identity,
                bias=kbc[:, h:h + 1]), waits=[("mm", S.cnt["mm"])], inc="ev")
            for j in range(4):
                tt = 4 * e + j
                par = tt % 2
                w = []
                if tt >= 2:
                    w.append(("ev", vtt[tt - 2]))
                for c in range(mch):
                    S.op("tensor", lambda t, c=c, tt=tt, par=par: t.matmul(
                        pss[:, 1024 + 512 * par:1024 + 512 * par + 128],
                        arena[:, T * c + 128 * tt:T * c + 128 * tt + 128],
                        vwT[:, 128 * c:128 * c + 128],
                        start=(c == 0), stop=False), waits=w if c == 0 else ())
                mv = S.op("tensor", lambda t, h=h, par=par, tt=tt: t.matmul(
                    pss[:, 1024 + 512 * par:1024 + 512 * par + 128], onesbf[0:1, 0:128],
                    vbrow[0:1, 128 * h:128 * h + 128], start=False, stop=True),
                    waits=[("wq", wq0), vb_k] if tt == 0 else (), inc="mm")
                if tt == 0:
                    vtt = {}
                vtt[tt] = S.op("scalar", lambda sc, tt=tt, par=par: sc.activation(
                    vaug[:, 129 * tt:129 * tt + 128],
                    pss[:, 1024 + 512 * par:1024 + 512 * par + 128],
                    AF.Identity), waits=[("mm", mv)], inc="ev")
        vone = S.op("vector", lambda v: v.memset(
            vaug[:].rearrange("p (a n) -> p a n", n=129)[:, :, 128:129], 1.0),
            waits=[("ev", S.cnt["ev"]), ("ve", prev_tail["ve"])], inc="ve")

        ebn = bcalc(S, pss[:, 1024:2048], pss[:, 0:1024], sc1, nB, q2, qtb, ktb,
                    128, SC, onesbf, ones32,
                    asm_deps=[("ev", S.cnt["ev"]), ("ve", vone)])

        for qg in range(QGN):
            eex = {}
            for kti in range(TTN):
                w = []
                if kti == 0:
                    w = [("ev", ebn), ("ve", S.cnt["ve"]), ("mm", prev_tail["mm"])]
                if kti < 4 and prev_e4f is not None:
                    w.append(("ev", prev_e4f[kti]))
                if kti >= 4:
                    w.append(("ev", eex[kti - 4]))
                smm = S.op("tensor", lambda t, qg=qg, kti=kti: t.matmul(
                    pss[:, 512 * (kti % 4):512 * (kti % 4) + 512],
                    ktb[:, 128 * kti:128 * kti + 128],
                    qtb[:, 512 * qg:512 * qg + 512], start=True, stop=True),
                    waits=w, inc="mm")
                eex[kti] = S.op("scalar", lambda sc, kti=kti: sc.activation(
                    ppr[:, 512 * kti:512 * kti + 512],
                    pss[:, 512 * (kti % 4):512 * (kti % 4) + 512],
                    AF.Exp, bias=nB[:, 0:1], scale=np.float32(SC)),
                    waits=[("mm", smm), ("ev", ebn)], inc="ev")
            prev_e4f = [eex[TTN - 4 + j] for j in range(4)]
            exp_all = S.cnt["ev"]
            for half in range(2):
                for j2 in range(2):
                    jt = 2 * half + j2
                    for kti in range(TTN):
                        S.op("tensor", lambda t, jt=jt, j2=j2, kti=kti: t.matmul(
                            psat[j2][0:128, 0:129],
                            ppr[:, 512 * kti + 128 * jt:512 * kti + 128 * jt + 128],
                            vaug[:, 129 * kti:129 * kti + 129],
                            start=(kti == 0), stop=(kti == TTN - 1)),
                            waits=[("ev", exp_all), ("ve", S.cnt["ve"])]
                            if (kti == 0 and j2 == 0) else (),
                            inc="mm" if kti == TTN - 1 else None)
                att_done = S.cnt["mm"]
                for j2 in range(2):
                    jt = 2 * half + j2
                    S.op("vector", lambda v, jt=jt, j2=j2: v.reciprocal(
                        sccol[:, jt:jt + 1], psat[j2][0:128, 128:129]),
                        waits=[("mm", att_done), ("ve", S.cnt["ve"])] if j2 == 0 else (),
                        inc="ve")
                vsc = S.cnt["ve"]
                for j2 in range(2):
                    jt = 2 * half + j2
                    S.op("vector", lambda v, jt=jt, j2=j2: v.tensor_scalar(
                        attsc[:, 128 * jt:128 * jt + 128], psat[j2][0:128, 0:128],
                        sccol[:, jt:jt + 1], None, ALU.mult),
                        waits=[("ve", vsc)] if j2 == 0 else (),
                        inc="ve" if j2 == 1 else None)
                vats = S.cnt["ve"]
                for j2 in range(2):
                    jt = 2 * half + j2
                    mt = S.op("tensor", lambda t, jt=jt: t.transpose(
                        psTs[jt % 2][:, :], attsc[:, 128 * jt:128 * jt + 128],
                        identbf[:]),
                        waits=[("ve", vats), ("ev", S.cnt["ev"])],
                        inc="mm")
                    S.op("scalar", lambda sc, jt=jt, qg=qg, h=h: sc.activation(
                        xbf[:, T * h + 512 * qg + 128 * jt:T * h + 512 * qg + 128 * jt + 128],
                        psTs[jt % 2][:, :], AF.Identity), waits=[("mm", mt)], inc="ev")
            prev_tail = dict(mm=S.cnt["mm"], ev=S.cnt["ev"], ve=S.cnt["ve"])

    S.fence_all()
    vres = None
    for e in range(4):
        mch = NCH >> e
        for ot in range(mch):
            eo = wtrans(owT, "l1f_ow", ot, 128 * mch, extra=[("ve", S.cnt["ve"])])
            for c in range(mch):
                w = ([("ev", eo)] + ([("ve", vres)] if vres else [])) if c == 0 else ()
                S.op("tensor", lambda t, e=e, c=c: t.matmul(
                    psp[:, 0:512], owT[:, 128 * c:128 * c + 128],
                    xbf[:, T * c + 512 * e:T * c + 512 * e + 512],
                    start=(c == 0), stop=False), waits=w)
            mo = S.op("tensor", lambda t, ot=ot: t.matmul(
                psp[:, 0:512], obrow[0:1, 128 * ot:128 * ot + 128],
                onesbf[0:1, 0:512], start=False, stop=True), inc="mm")
            vres = S.op("vector", lambda v, e=e, ot=ot: v.scalar_tensor_tensor(
                x32[:, T * ot + 512 * e:T * ot + 512 * e + 512],
                x32[:, T * ot + 512 * e:T * ot + 512 * e + 512], 2.0,
                psp[:, 0:512], ALU.mult, ALU.add),
                waits=[("mm", mo)], inc="ve")
        for ot in range(mch, NCH):
            vres = S.op("vector", lambda v, e=e, ot=ot: v.tensor_scalar(
                x32[:, T * ot + 512 * e:T * ot + 512 * e + 512],
                x32[:, T * ot + 512 * e:T * ot + 512 * e + 512], 2.0, None, ALU.mult),
                inc="ve")
    S.fence_all()
    S.emit()
    es.close()
    ln_boundary(st, "_l1f%d" % b)



def build_nc(nphase=99, custom_seq=None):
    nc = bass.Bass("TRN2", target_bir_lowering=False)
    xin = nc.dram_tensor("x_in", [BPC, T, D], F32, kind="ExternalInput")
    xout = nc.dram_tensor("x_out", [BPC, T, D], F32, kind="ExternalOutput")
    W = {}

    def din(name, shape):
        W[name] = nc.dram_tensor(name, shape, F32, kind="ExternalInput")

    din("l0a_wqkv", [D, 3 * D]); din("l0a_bqkv", [3 * D])
    din("l0a_wout", [D, D]); din("l0a_bout", [D])
    din("l0f_gw", [D, 8]); din("l0f_w1", [8, D, 512]); din("l0f_b1", [8, 512])
    din("l0f_w2", [8, 512, D]); din("l0f_b2", [8, D])
    din("l0f_ws1", [D, 512]); din("l0f_bs1", [512])
    din("l0f_ws2", [512, D]); din("l0f_bs2", [D])
    din("l0f_rs", [1, 1])
    din("l1a_wq", [D, D]); din("l1a_wqr", [D, 512]); din("l1a_wkr", [D, 32])
    din("l1a_wkvd", [D, 256]); din("l1a_wku", [256, D]); din("l1a_wvu", [256, D])
    din("l1a_wo", [D, D]); din("l1a_lobo", [1, 16])
    din("l1f_nw", [D]); din("l1f_nb", [D])
    din("l1f_qw", [D, D]); din("l1f_qb", [D]); din("l1f_kw", [D, D]); din("l1f_kb", [D])
    din("l1f_vw", [D, D]); din("l1f_vb", [D]); din("l1f_ow", [D, D]); din("l1f_ob", [D])

    CD = _mk_consts(nc)
    with ExitStack() as ctx:
        sb = lambda n, s, d: ctx.enter_context(nc.sbuf_tensor(n, s, d))
        x32 = sb("x32", [128, NCH * T], F32)
        xbf = sb("xbf", [128, NCH * T], BF16)
        arena = sb("arena", [128, NCH * T], BF16)
        identbf = sb("identbf_s", [128, 128], BF16)
        ident32 = sb("ident32_s", [128, 128], F32)
        onesbf = sb("onesbf_s", [128, 512], BF16)
        ones32 = sb("ones32_s", [128, 128], F32)
        maskc = sb("maskc_s", [128, 2048], BF16)
        pswap = sb("pswap_s", [128, 128], BF16)
        blk = ctx.enter_context(nc.Block())
        S = Sched(nc, ctx, blk)

        for dst, key in [(identbf, "identbf"), (ident32, "ident32"),
                         (onesbf, "onesbf"), (ones32, "ones32"),
                         (maskc, "maskc"), (pswap, "pswap")]:
            S.op("sync", lambda s, dst=dst, key=key: s.dma_start(
                out=dst[:], in_=CD[key].ap()), inc="wq", k=16)
        S.emit()

        st = dict(S=S, nc=nc, W=W, xin=xin, xout=xout, x32=x32, xbf=xbf,
                  arena=arena, identbf=identbf, ident32=ident32, onesbf=onesbf,
                  ones32=ones32, maskc=maskc, pswap=pswap, CD=CD)

        if custom_seq is not None:
            seq = [({"xload": phase_xload, "l0a": phase_l0a, "l0f": phase_l0f,
                     "l1a": phase_l1a, "l1f": phase_l1f, "store": phase_store}[nm], b)
                   for nm, b in custom_seq]
        else:
            seq = []
            for b in range(BPC):
                seq += [(phase_xload, b), (phase_l0a, b), (phase_l0f, b),
                        (phase_l1a, b), (phase_l1f, b), (phase_store, b)]
            seq = seq[:nphase]
        for fn, b in seq:
            fn(st, b)

        if S.cnt["out"]:
            S.op("sync", lambda s: None, waits=[("out", S.cnt["out"])])
        S.fence_all()
        S.emit()
    return nc



_IN_MAP = [
    ("l0a_wqkv", "L0_attn_W_qkv"), ("l0a_bqkv", "L0_attn_b_qkv"),
    ("l0a_wout", "L0_attn_W_out"), ("l0a_bout", "L0_attn_b_out"),
    ("l0f_gw", "L0_ffn_gate_W"), ("l0f_w1", "L0_ffn_W1"), ("l0f_b1", "L0_ffn_b1"),
    ("l0f_w2", "L0_ffn_W2"), ("l0f_b2", "L0_ffn_b2"),
    ("l0f_ws1", "L0_ffn_Ws1"), ("l0f_bs1", "L0_ffn_bs1"),
    ("l0f_ws2", "L0_ffn_Ws2"), ("l0f_bs2", "L0_ffn_bs2"),
    ("l1a_wq", "L1_attn_Wq"), ("l1a_wqr", "L1_attn_Wqr"), ("l1a_wkr", "L1_attn_Wkr"),
    ("l1a_wkvd", "L1_attn_Wkvd"), ("l1a_wku", "L1_attn_Wku"),
    ("l1a_wvu", "L1_attn_Wvu"), ("l1a_wo", "L1_attn_Wo"),
    ("l1f_nw", "L1_ffn_norm_w"), ("l1f_nb", "L1_ffn_norm_b"),
    ("l1f_qw", "L1_ffn_q_w"), ("l1f_qb", "L1_ffn_q_b"),
    ("l1f_kw", "L1_ffn_k_w"), ("l1f_kb", "L1_ffn_k_b"),
    ("l1f_vw", "L1_ffn_v_w"), ("l1f_vb", "L1_ffn_v_b"),
    ("l1f_ow", "L1_ffn_o_w"), ("l1f_ob", "L1_ffn_o_b"),
]

_CACHE = {}


def kernel(**inputs):
    n_cores = 8
    if "nc" not in _CACHE:
        _CACHE["nc"] = build_nc()
    nc = _CACHE["nc"]
    x = np.ascontiguousarray(np.asarray(inputs["L0_attn_x"], dtype=np.float32))
    wmap = {}
    for kname, iname in _IN_MAP:
        wmap[kname] = np.ascontiguousarray(np.asarray(inputs[iname], dtype=np.float32))
    wmap["l1a_lobo"] = np.ascontiguousarray(
        np.asarray(inputs["L1_attn_lobo_log"], dtype=np.float32).reshape(1, 16))
    wmap["l0f_rs"] = np.ascontiguousarray(
        np.asarray(inputs["L0_ffn_route_scale"], dtype=np.float32).reshape(1, 1))
    in_maps = []
    for c in range(n_cores):
        m = dict(wmap)
        m["x_in"] = np.ascontiguousarray(x[BPC * c:BPC * c + BPC])
        in_maps.append(m)
    res = run_bass_kernel_spmd(nc, in_maps, core_ids=list(range(n_cores)))
    out = np.concatenate([np.asarray(r["x_out"]) for r in res.results], axis=0)
    return out.astype(np.float32)


# version 30
# speedup vs baseline: 1.1944x; 1.0059x over previous
import math
import numpy as np
import ml_dtypes
import concourse.bass as bass
import concourse.mybir as mybir
from concourse.bass_utils import run_bass_kernel_spmd
from contextlib import ExitStack

F32 = mybir.dt.float32
BF16 = mybir.dt.bfloat16
AF = mybir.ActivationFunctionType
ALU = mybir.AluOpType
AX = mybir.AxisListType
bf16 = ml_dtypes.bfloat16

T = 2048
D = 1024
NCH = 8
TTN = 16
QGN = 4
BPC = 2
ENGS = ("tensor", "vector", "scalar", "sync", "gpsimd")


class Sched:
    def __init__(self, nc, ctx, blk):
        self.nc, self.ctx, self.blk = nc, ctx, blk
        self.cnt, self.sem = {}, {}
        for name in ("mm", "ve", "ev", "gv", "wq", "wg", "wg2", "wg3", "out"):
            self.sem[name] = ctx.enter_context(nc.semaphore("s_" + name))
            self.cnt[name] = 0
        self.ops = {e: [] for e in ENGS}

    def op(self, eng, build, waits=(), inc=None, k=1, chain=True):
        waits = list(waits)
        if inc in ("wq", "wg", "wg2", "wg3", "gv", "out"):
            waits.append((inc, self.cnt[inc]))
        if eng == "vector":
            if chain is True:
                if self.cnt["ve"]:
                    waits.append(("ve", self.cnt["ve"]))
            elif chain:
                waits.append(("ve", chain))
            inc = inc or "ve"
        elif eng == "scalar":
            if self.cnt["ev"]:
                waits.append(("ev", self.cnt["ev"]))
            inc = inc or "ev"
        after = None
        if inc is not None:
            self.cnt[inc] += k
            after = self.cnt[inc]
        self.ops[eng].append((tuple(waits), build, inc, k))
        return after

    def fence_all(self):
        waits = tuple((n, v) for n, v in self.cnt.items() if v > 0)
        for e in ENGS:
            self.ops[e].append((waits, None, None, 0))

    def emit(self):
        for e in ENGS:
            ops = self.ops[e]
            if not ops:
                continue
            self.ops[e] = []

            def runner(eng, ops=ops):
                for waits, build, inc, k in ops:
                    for sname, val in waits:
                        eng.wait_ge(self.sem[sname], val)
                    if build is not None:
                        ins = build(eng)
                        if inc is not None:
                            ins.then_inc(self.sem[inc], k)
            getattr(self.blk, e)(runner)


def _rcp_bf16(nc, v, out, in_):
    with nc.allow_low_precision(reason="bf16 reciprocal of softmax denominators; 0.4% rel on a well-conditioned scale"):
        return v.reciprocal(out, in_)


def _stt_bf16(nc, v, out, in0, scalar, in1):
    with nc.allow_low_precision(reason="bf16 softmax denominator accumulate; ~0.4% rel on a well-conditioned positive sum"):
        return v.scalar_tensor_tensor(out, in0, scalar, in1, ALU.mult, ALU.add)


def _mk_consts(nc):
    c = {}
    c["identbf"] = nc.inline_tensor(np.eye(128, dtype=bf16), name="identbf")
    c["ident32"] = nc.inline_tensor(np.eye(128, dtype=np.float32), name="ident32")
    c["onesbf"] = nc.inline_tensor(np.ones((128, 512), dtype=bf16), name="onesbf")
    c["ones32"] = nc.inline_tensor(np.ones((128, 128), dtype=np.float32), name="ones32")
    m = np.zeros((128, 4 * 512), dtype=bf16)
    ll = np.arange(128)[:, None]
    jj = np.arange(512)[None, :]
    for o in range(4):
        m[:, o * 512:(o + 1) * 512] = (jj >= ll + 128 * o).astype(bf16)
    c["maskc"] = nc.inline_tensor(m, name="maskc")
    mr = np.zeros((128, 8 * 512), dtype=bf16)
    for o in range(4):
        mr[:, 1024 * o:1024 * o + 512] = m[:, 512 * o:512 * o + 512]
        mr[:, 1024 * o + 512:1024 * o + 1024] = m[:, 512 * o:512 * o + 512]
    c["maskcR"] = nc.inline_tensor(mr, name="maskcR")
    half = 16
    inv = 1.0 / (10000.0 ** (np.arange(half, dtype=np.float64) / half))
    ang = np.arange(T, dtype=np.float64)[:, None] * inv[None, :]
    cosv = np.cos(ang).T.astype(np.float32)
    sinv = np.sin(ang).T.astype(np.float32)
    C = np.zeros((128, T), dtype=np.float32)
    Sm = np.zeros((128, T), dtype=np.float32)
    for p in range(4):
        C[32 * p:32 * p + 16] = cosv
        C[32 * p + 16:32 * p + 32] = cosv
        Sm[32 * p:32 * p + 16] = -sinv
        Sm[32 * p + 16:32 * p + 32] = sinv
    c["ropeC"] = nc.inline_tensor(C.astype(bf16), name="ropeC")
    c["ropeS"] = nc.inline_tensor(Sm.astype(bf16), name="ropeS")
    P = np.zeros((128, 128), dtype=bf16)
    for r in range(128):
        P[(r // 32) * 32 + (r % 32 + 16) % 32, r] = 1.0
    c["pswap"] = nc.inline_tensor(P, name="pswap")
    return c


_WDMA_ROT = ["wg", "wg2", "wg3"]


def wdma(S, dst, src_ap, waits=()):
    S._wrot = (getattr(S, "_wrot", 0) + 1) % 3
    sem = _WDMA_ROT[S._wrot]
    v = S.op("gpsimd", lambda g: g.dma_start(out=dst, in_=src_ap),
             waits=waits, inc=sem, k=16)
    return (sem, v)


def cc3(ap2d):
    return ap2d.rearrange("(c p) n -> p c n", p=128)


def sb3(buf, n):
    return buf[:].rearrange("p (c n) -> p c n", n=n)


def bias_col(ap1d):
    return ap1d.rearrange("(c a) -> a c", a=128)



def phase_xload(st, b):
    S, nc = st["S"], st["nc"]
    x32, xbf, ident32 = st["x32"], st["xbf"], st["ident32"]
    xin = st["xin"]
    S.fence_all()
    with nc.psum_tensor("pxl%d" % b, [128, 1024], F32) as pT, \
            nc.sbuf_tensor("xtok_l%d" % b, [128, 2 * D], F32) as xtok:
        ev = {}
        for j in range(TTN):
            w = [("ev", ev[(j - 2, NCH - 1)])] if j >= 2 else ()
            dj = S.op("sync", lambda s, j=j: s.dma_start(
                out=xtok[:, (j % 2) * D:(j % 2) * D + D],
                in_=xin.ap()[b, 128 * j:128 * j + 128, :]), waits=w, inc="wq", k=16)
            for c in range(NCH):
                k = 8 * j + c
                w2 = [("wq", dj)] if c == 0 else []
                if k >= 2:
                    w2.append(("ev", ev[((k - 2) // 8, (k - 2) % 8)]))
                mmk = S.op("tensor", lambda t, j=j, c=c, k=k: t.transpose(
                    pT[:, (k % 2) * 512:(k % 2) * 512 + 128],
                    xtok[:, (j % 2) * D + 128 * c:(j % 2) * D + 128 * c + 128],
                    ident32[:]), waits=w2, inc="mm")
                ev[(j, c)] = S.op("scalar", lambda sc, j=j, c=c, k=k: sc.activation(
                    x32[:, T * c + 128 * j:T * c + 128 * j + 128],
                    pT[:, (k % 2) * 512:(k % 2) * 512 + 128], AF.Identity),
                    waits=[("mm", mmk)], inc="ev")
        evl = S.cnt["ev"]
        for c in range(NCH):
            S.op("vector", lambda v, c=c: v.tensor_copy(
                xbf[:, T * c:T * c + T], x32[:, T * c:T * c + T]),
                waits=[("ev", evl)] if c == 0 else (), inc="ve")
        S.fence_all()
        S.emit()


def phase_store(st, b):
    S, nc = st["S"], st["nc"]
    x32, ident32 = st["x32"], st["ident32"]
    xout = st["xout"]
    S.fence_all()
    with nc.psum_tensor("pst%d" % b, [128, 1024], F32) as pT, \
            nc.sbuf_tensor("xtok_s%d" % b, [128, 2 * D], F32) as xtok:
        ev = {}
        dmas = {}
        for j in range(TTN):
            for c in range(NCH):
                k = 8 * j + c
                w = []
                if k >= 2:
                    w.append(("ev", ev[k - 2]))
                mmk = S.op("tensor", lambda t, j=j, c=c, k=k: t.transpose(
                    pT[:, (k % 2) * 512:(k % 2) * 512 + 128],
                    x32[:, T * c + 128 * j:T * c + 128 * j + 128],
                    ident32[:]), waits=w, inc="mm")
                w3 = [("mm", mmk)]
                if j >= 2 and c == 0:
                    w3.append(("out", dmas[j - 2]))
                ev[k] = S.op("scalar", lambda sc, j=j, c=c, k=k: sc.activation(
                    xtok[:, (j % 2) * D + 128 * c:(j % 2) * D + 128 * c + 128],
                    pT[:, (k % 2) * 512:(k % 2) * 512 + 128], AF.Identity),
                    waits=w3, inc="ev")
            dmas[j] = S.op("sync", lambda s, j=j: s.dma_start(
                out=xout.ap()[b, 128 * j:128 * j + 128, :],
                in_=xtok[:, (j % 2) * D:(j % 2) * D + D]),
                waits=[("ev", ev[8 * j + NCH - 1])], inc="out", k=16)
        S.fence_all()
        S.emit()


def ln_boundary(st, tag, affine=None):
    S, nc = st["S"], st["nc"]
    x32, xbf, onesbf, ones32 = st["x32"], st["xbf"], st["onesbf"], st["ones32"]
    modify_x = affine is None
    S.fence_all()
    es = ExitStack()
    sbt = lambda n, s, d: es.enter_context(nc.sbuf_tensor(n + tag, s, d))
    pst = lambda n, s, d: es.enter_context(nc.psum_tensor(n + tag, s, d))
    u2 = [sbt("u2%d" % i, [128, T], BF16) for i in range(2)]
    rowA = sbt("rowA", [1, T], F32)
    rowB = sbt("rowB", [1, T], F32)
    if not modify_x:
        t1 = sbt("t1", [128, 1024], BF16)
    psr = pst("psr", [128, T], F32)
    psm = pst("psm", [128, T], F32)
    psS = psr[0:1, :]

    for c in range(NCH):
        for qg in range(QGN):
            S.op("tensor", lambda t, c=c, qg=qg: t.matmul(
                psS[0:1, 512 * qg:512 * qg + 512], ones32[:, 0:1],
                x32[:, T * c + 512 * qg:T * c + 512 * qg + 512],
                start=(c == 0), stop=(c == NCH - 1)),
                inc="mm" if (c == NCH - 1 and qg == QGN - 1) else None)
    vA = S.op("vector", lambda v: v.tensor_scalar(
        rowA[0:1, :], psS[0:1, :], 1.0 / D, None, ALU.mult),
        waits=[("mm", S.cnt["mm"])], inc="ve", chain=False)
    sq_ev = {}
    for c in range(NCH):
        w = [("mm", mm_u2[c - 2])] if c >= 2 else []
        sq_ev[c] = S.op("scalar", lambda sc, c=c: sc.activation(
            u2[c % 2][:, :], x32[:, T * c:T * c + T], AF.Square), waits=w, inc="ev")
        if c == 0:
            mm_u2 = {}
        for qg in range(QGN):
            mm_u2[c] = S.op("tensor", lambda t, c=c, qg=qg: t.matmul(
                psS[0:1, 512 * qg:512 * qg + 512], onesbf[:, 0:1],
                u2[c % 2][:, 512 * qg:512 * qg + 512],
                start=(c == 0), stop=(c == NCH - 1)),
                waits=[("ev", sq_ev[c])] + ([("ve", vA)] if c == 0 else [])
                if qg == 0 else (),
                inc="mm" if qg == QGN - 1 else None)
    eQ = S.op("scalar", lambda sc: sc.activation(
        rowB[0:1, :], rowA[0:1, :], AF.Square), waits=[("ve", vA)], inc="ev")
    vV = S.op("vector", lambda v: v.scalar_tensor_tensor(
        rowB[0:1, :], psS[0:1, :], 1.0 / D, rowB[0:1, :],
        ALU.mult, ALU.subtract),
        waits=[("mm", S.cnt["mm"]), ("ev", eQ)], inc="ve", chain=False)
    er = S.op("scalar", lambda sc: sc.activation(
        rowB[0:1, :], rowB[0:1, :], AF.Sqrt, bias=np.float32(1e-5)),
        waits=[("ve", vV)], inc="ev")
    vR = S.op("vector", lambda v: v.reciprocal(rowB[0:1, :], rowB[0:1, :]),
              waits=[("ev", er)], inc="ve", chain=False)
    rows_ve = S.op("vector", lambda v: v.tensor_tensor(
        rowA[0:1, :], rowA[0:1, :], rowB[0:1, :], ALU.mult),
        inc="ve", chain=vR)

    for qg in range(QGN):
        S.op("tensor", lambda t, qg=qg: t.matmul(
            psr[:, 512 * qg:512 * qg + 512], ones32[0:1, 0:128],
            rowB[0:1, 512 * qg:512 * qg + 512],
            start=True, stop=True),
            waits=[("ve", rows_ve)] if qg == 0 else ())
        S.op("tensor", lambda t, qg=qg: t.matmul(
            psm[:, 512 * qg:512 * qg + 512], ones32[0:1, 0:128],
            rowA[0:1, 512 * qg:512 * qg + 512],
            start=True, stop=True), inc="mm" if qg == QGN - 1 else None)
    bdone = S.cnt["mm"]
    if modify_x:
        mids = {}
        for c in range(NCH):
            mids[c] = S.op("vector", lambda v, c=c: v.tensor_tensor(
                x32[:, T * c:T * c + T], x32[:, T * c:T * c + T], psr[:, :],
                ALU.mult), waits=[("mm", bdone)] if c == 0 else (),
                inc="ve", chain=False)
        for c in range(NCH):
            sub = S.op("vector", lambda v, c=c: v.tensor_tensor(
                x32[:, T * c:T * c + T], x32[:, T * c:T * c + T], psm[:, :],
                ALU.subtract), inc="ve", chain=mids[c])
            S.op("scalar", lambda sc, c=c: sc.activation(
                xbf[:, T * c:T * c + T], x32[:, T * c:T * c + T], AF.Identity),
                waits=[("ve", sub)], inc="ev")
    else:
        wcol, bcol, dst = affine
        prev = False
        prev_ea = None
        for c in range(NCH):
            for ch in range(2):
                sl = slice(T * c + 1024 * ch, T * c + 1024 * ch + 1024)
                bsl = slice(1024 * ch, 1024 * ch + 1024)
                m1 = S.op("vector", lambda v, sl=sl, bsl=bsl: v.tensor_tensor(
                    t1[:, :], x32[:, sl], psr[:, bsl], ALU.mult),
                    waits=[("mm", bdone)] if prev_ea is None
                    else [("ev", prev_ea)],
                    inc="ve", chain=prev)
                w2 = S.op("vector", lambda v, bsl=bsl: v.tensor_tensor(
                    t1[:, :], t1[:, :], psm[:, bsl], ALU.subtract),
                    inc="ve", chain=m1)
                prev_ea = S.op("scalar", lambda sc, sl=sl, c=c: sc.activation(
                    dst[:, sl], t1[:, :], AF.Identity,
                    bias=bcol[:, c:c + 1], scale=wcol[:, c:c + 1]),
                    waits=[("ve", w2)], inc="ev")
                prev = w2
    S.fence_all()
    S.emit()
    es.close()


def bcalc(S, psp, pss, sc1, nB, q2, qbuf, kbuf, rows, scale, onesbf, ones32,
          asm_deps=()):
    banks = [(psp, 0), (psp, 512), (pss, 0), (pss, 512)]
    red = {}
    for i in range(2):
        src = qbuf if i == 0 else kbuf
        sq = S.op("scalar", lambda sc, src=src, rows=rows: sc.activation(
            q2[0:rows, :], src[0:rows, :], AF.Square),
            waits=list(asm_deps) if i == 0 else [("mm", S.cnt["mm"])], inc="ev")
        for qg in range(QGN):
            buf, off = banks[qg]
            w = []
            if qg == 0:
                w = [("ev", sq)]
                if i == 1:
                    w.append(("ve", red[1]))
            S.op("tensor", lambda t, qg=qg, buf=buf, off=off, rows=rows: t.matmul(
                buf[:, off:off + 512], onesbf[0:rows, 0:128],
                q2[0:rows, 512 * qg:512 * qg + 512], start=True, stop=True),
                waits=w, inc="mm" if qg == 3 else None)
        mq = S.cnt["mm"]
        red[2 * i] = S.op("vector", lambda v, i=i: v.reduce_max(
            sc1[:, 2 * i:2 * i + 1], psp[:, 0:1024], axis=AX.X),
            waits=[("mm", mq)], inc="ve", chain=False)
        red[2 * i + 1] = S.op("vector", lambda v, i=i: v.reduce_max(
            sc1[:, 2 * i + 1:2 * i + 2], pss[:, 0:1024], axis=AX.X),
            inc="ve", chain=False)
    redq = S.op("vector", lambda v: v.reduce_max(
        sc1[:, 4:5], sc1[:, 0:2], axis=AX.X), inc="ve", chain=red[3])
    redk = S.op("vector", lambda v: v.reduce_max(
        sc1[:, 5:6], sc1[:, 2:4], axis=AX.X), inc="ve", chain=False)
    vb = S.op("vector", lambda v: v.tensor_tensor(
        sc1[:, 6:7], sc1[:, 4:5], sc1[:, 5:6], ALU.mult),
        inc="ve", chain=redk)
    sq2 = S.op("scalar", lambda sc: sc.activation(
        sc1[:, 7:8], sc1[:, 6:7], AF.Sqrt), waits=[("ve", vb)], inc="ev")
    return S.op("scalar", lambda sc, scale=scale: sc.activation(
        nB[:, 0:1], sc1[:, 7:8], AF.Identity, bias=np.float32(50.0),
        scale=np.float32(-scale)), waits=[("ev", sq2)], inc="ev")


def bcalc2(S, psp, pssb, sc1, nB, stage, qbuf, kbuf, scale, onesbf, ones32,
           asm_deps=()):
    banks = [(psp, 0), (psp, 512), (pssb, 0), (pssb, 512)]
    red = {}
    for i in range(2):
        src = qbuf if i == 0 else kbuf
        sq = S.op("scalar", lambda sc, src=src: sc.activation(
            stage[:, :], src[:, 0:2048], AF.Square),
            waits=list(asm_deps) if i == 0 else [("mm", S.cnt["mm"])], inc="ev")
        for qg in range(QGN):
            buf, off = banks[qg]
            w = []
            if qg == 0:
                w = [("ev", sq)]
                if i == 1:
                    w.append(("ve", red[1]))
            S.op("tensor", lambda t, qg=qg, buf=buf, off=off: t.matmul(
                buf[:, off:off + 512], onesbf[:, 0:128],
                stage[:, 512 * qg:512 * qg + 512], start=True, stop=True),
                waits=w, inc="mm" if qg == 3 else None)
        mq = S.cnt["mm"]
        red[2 * i] = S.op("vector", lambda v, i=i: v.reduce_max(
            sc1[:, 2 * i:2 * i + 1], psp[:, 0:1024], axis=AX.X),
            waits=[("mm", mq)], inc="ve", chain=False)
        red[2 * i + 1] = S.op("vector", lambda v, i=i: v.reduce_max(
            sc1[:, 2 * i + 1:2 * i + 2], pssb[:, 0:1024], axis=AX.X),
            inc="ve", chain=False)
    redq = S.op("vector", lambda v: v.reduce_max(
        sc1[:, 4:5], sc1[:, 0:2], axis=AX.X), inc="ve", chain=red[3])
    redk = S.op("vector", lambda v: v.reduce_max(
        sc1[:, 5:6], sc1[:, 2:4], axis=AX.X), inc="ve", chain=False)
    vb = S.op("vector", lambda v: v.tensor_tensor(
        sc1[:, 6:7], sc1[:, 4:5], sc1[:, 5:6], ALU.mult),
        inc="ve", chain=redk)
    sq2 = S.op("scalar", lambda sc: sc.activation(
        sc1[:, 7:8], sc1[:, 6:7], AF.Sqrt), waits=[("ve", vb)], inc="ev")
    return S.op("scalar", lambda sc, scale=scale: sc.activation(
        nB[:, 0:1], sc1[:, 7:8], AF.Identity, bias=np.float32(50.0),
        scale=np.float32(-scale)), waits=[("ev", sq2)], inc="ev")



def phase_l0a(st, b):
    S, nc, W = st["S"], st["nc"], st["W"]
    x32, xbf, arena = st["x32"], st["xbf"], st["arena"]
    onesbf, ones32, maskc = st["onesbf"], st["ones32"], st["maskc"]
    ident32 = st["ident32"]
    es = ExitStack()
    sbt = lambda n, s, d: es.enter_context(nc.sbuf_tensor(n + "_l0a%d" % b, s, d))
    pst = lambda n, s, d: es.enter_context(nc.psum_tensor(n + "_l0a%d" % b, s, d))

    vall = sbt("vall", [128, TTN * 1040], BF16)
    qt = sbt("qt", [128, T], BF16)
    kt = sbt("kt", [128, T], BF16)
    ppb = sbt("ppb", [128, 2048], BF16)
    atte = sbt("atte", [128, 1024], F32)
    atteB = sbt("atteB", [128, 512], BF16)
    rrowB = sbt("rrowB", [128, 1024], BF16)
    dcolF = sbt("dcolF", [128, 8], F32)
    nB = sbt("nB", [128, 1], F32)
    sc1 = sbt("sc1", [128, 8], F32)
    wqk = [sbt("wqkA%d" % i, [128, 1024], BF16) for i in range(2)]
    wqk2 = [sbt("wqkB%d" % i, [128, 1024], BF16) for i in range(2)]
    wv = sbt("wv", [128, NCH * 512], BF16)
    bq = sbt("bq", [128, 16], F32)
    bvrow = sbt("bvrow", [1, D], BF16)
    borow = sbt("borow", [1, D], BF16)

    psp = pst("psp", [128, 1024], F32)
    pssb = pst("pssb", [128, 2048], F32)
    psab = pst("psab", [128, 1024], F32)

    S.fence_all()
    S.op("sync", lambda s: s.dma_start(
        out=bq[:, :], in_=W["l0a_bqkv"].ap()[0:2048].rearrange("(a b) -> b a", b=128),
        allow_slow_non_contiguous=True), inc="wq", k=16)
    bv_k = wdma(S, bvrow[0:1, :], W["l0a_bqkv"].ap()[2048:3072].unsqueeze(0))
    bias_wg = wdma(S, borow[0:1, :], W["l0a_bout"].ap().unsqueeze(0))
    bias_wq = S.cnt["wq"]

    vev = {}
    for vg in range(2):
        wvd = wdma(S, sb3(wv, 512), cc3(W["l0a_wqkv"].ap()[:, 2048 + 512 * vg:2560 + 512 * vg]),
                   waits=[("mm", S.cnt["mm"])] if vg == 1 else ())
        for tt in range(TTN):
            par = tt % 2
            idx = vg * TTN + tt
            for c in range(NCH):
                w = []
                if c == 0:
                    w = [wvd, bv_k, bias_wg, ("wq", bias_wq)]
                    if idx >= 2:
                        w.append(("ev", vev[idx - 2]))
                S.op("tensor", lambda t, tt=tt, c=c, par=par: t.matmul(
                    psp[:, 512 * par:512 * par + 512],
                    xbf[:, T * c + 128 * tt:T * c + 128 * tt + 128],
                    wv[:, 512 * c:512 * c + 512], start=(c == 0), stop=False),
                    waits=w)
            mmv = S.op("tensor", lambda t, vg=vg, par=par: t.matmul(
                psp[:, 512 * par:512 * par + 512], onesbf[0:1, 0:128],
                bvrow[0:1, 512 * vg:512 * vg + 512], start=False, stop=True),
                inc="mm")
            dst = vall[:].rearrange("p (t h n) -> p t h n", t=TTN, n=65)
            vev[idx] = S.op(
                "scalar", lambda sc, tt=tt, vg=vg, par=par, dst=dst: sc.activation(
                    dst[:, tt, 8 * vg:8 * vg + 8, 0:64],
                    psp[:, 512 * par:512 * par + 512].rearrange("p (h n) -> p h n", n=64),
                    AF.Identity),
                waits=[("mm", mmv)], inc="ev")
    ones_ve = S.op("vector", lambda v: v.memset(
        vall[:].rearrange("p (a n) -> p a n", n=65)[:, :, 64:65], 1.0),
        waits=[("ev", S.cnt["ev"])], inc="ve")

    def qkcol(p, which):
        return cc3(W["l0a_wqkv"].ap()[:, 1024 * which + 128 * p:1024 * which + 128 * p + 128])

    wqd = [wdma(S, sb3(wqk[0], 128), qkcol(0, 0), waits=[("mm", S.cnt["mm"])]),
           wdma(S, sb3(wqk[1], 128), qkcol(0, 1))]
    for p in range(8):
        cur = wqk if p % 2 == 0 else wqk2
        nxt = wqk2 if p % 2 == 0 else wqk
        wq_ready = wqd
        if p < 7:
            wqd = [wdma(S, sb3(nxt[0], 128), qkcol(p + 1, 0), waits=[("mm", S.cnt["mm"])]),
                   wdma(S, sb3(nxt[1], 128), qkcol(p + 1, 1))]

        evq = {}
        for which, dstbuf, bcol in ((0, qt, p), (1, kt, 8 + p)):
            for qh in range(2):
                for qg2 in range(2):
                    qg = 2 * qh + qg2
                    for c in range(NCH):
                        w = []
                        if qh == 0 and qg2 == 0 and c == 0:
                            w = [wq_ready[which], ("ev", S.cnt["ev"]),
                                 ("ve", S.cnt["ve"])]
                        elif qg2 == 0 and c == 0 and qh == 1:
                            w = [("ev", evq[(which, 0)])]
                        S.op("tensor", lambda t, qg=qg, qg2=qg2, c=c, cb=cur[which]: t.matmul(
                            psp[:, 512 * qg2:512 * qg2 + 512],
                            cb[:, 128 * c:128 * c + 128],
                            xbf[:, T * c + 512 * qg:T * c + 512 * qg + 512],
                            start=(c == 0), stop=(c == NCH - 1)),
                            waits=w, inc="mm" if (qg2 == 1 and c == NCH - 1) else None)
                evq[(which, qh)] = S.op(
                    "scalar", lambda sc, dstbuf=dstbuf, bcol=bcol, qh=qh: sc.activation(
                        dstbuf[:, 1024 * qh:1024 * qh + 1024], psp[:, :], AF.Identity,
                        bias=bq[:, bcol:bcol + 1]),
                    waits=[("mm", S.cnt["mm"])], inc="ev")
        ebn = bcalc2(S, psp, pssb, sc1, nB, ppb, qt, kt, 0.125, onesbf, ones32,
                     asm_deps=[("ev", evq[(1, 1)])])

        hA, hB = 2 * p, 2 * p + 1
        prev_tail = dict(ev=S.cnt["ev"], ve=max(S.cnt["ve"], ones_ve),
                         gv=S.cnt["gv"], mask=0, e2=0, tt=S.cnt["ve"])
        for qg in range(QGN):
            KT = 4 * (qg + 1)
            expB = {}
            maskve = {}
            deps = {}

            def emit_att(kti, par, KT=None):
                dep = deps[kti]
                S.op("tensor", lambda t, kti=kti, par=par, hA=hA, KT=KT: t.matmul(
                    psab[0:65, 0:512],
                    vall[:, 1040 * kti + 65 * hA:1040 * kti + 65 * hA + 65],
                    ppb[:, 1024 * par:1024 * par + 512],
                    start=(kti == 0), stop=(kti == KT - 1)), waits=dep)
                return S.op("tensor", lambda t, kti=kti, par=par, hB=hB, KT=KT: t.matmul(
                    psab[0:65, 512:1024],
                    vall[:, 1040 * kti + 65 * hB:1040 * kti + 65 * hB + 65],
                    ppb[:, 1024 * par + 512:1024 * par + 1024],
                    start=(kti == 0), stop=(kti == KT - 1)), inc="mm")

            for kti in range(KT):
                par = kti % 2
                w = []
                if kti == 0:
                    w = [("ev", max(ebn, prev_tail["ev"]))]
                if kti >= 2:
                    w.append(("ev", expB[kti - 2]))
                S.op("tensor", lambda t, qg=qg, kti=kti, par=par: t.matmul(
                    pssb[:, 1024 * par:1024 * par + 512],
                    kt[0:64, 128 * kti:128 * kti + 128],
                    qt[0:64, 512 * qg:512 * qg + 512], start=True, stop=True,
                    tile_position=(0, 0)), waits=w)
                sB = S.op("tensor", lambda t, qg=qg, kti=kti, par=par: t.matmul(
                    pssb[:, 1024 * par + 512:1024 * par + 1024],
                    kt[64:128, 128 * kti:128 * kti + 128],
                    qt[64:128, 512 * qg:512 * qg + 512], start=True, stop=True,
                    tile_position=(64, 0)), inc="mm")
                ew = [("mm", sB), ("ev", ebn)]
                if kti == 0 and prev_tail.get("mask"):
                    ew.append(("ve", prev_tail["mask"]))
                if kti - 2 in maskve:
                    ew.append(("ve", maskve[kti - 2]))
                eB = S.op("scalar", lambda sc, kti=kti, par=par: sc.activation(
                    ppb[:, 1024 * par:1024 * par + 1024],
                    pssb[:, 1024 * par:1024 * par + 1024],
                    AF.Exp, bias=nB[:, 0:1], scale=np.float32(0.125)),
                    waits=ew, inc="ev")
                expB[kti] = eB
                dep = [("ev", eB)]
                if kti >= KT - 4:
                    o = kti - (KT - 4)
                    S.op("vector", lambda v, par=par, o=o: v.tensor_tensor(
                        ppb[:, 1024 * par:1024 * par + 512],
                        ppb[:, 1024 * par:1024 * par + 512],
                        maskc[:, 512 * o:512 * o + 512], ALU.mult),
                        waits=[("ev", eB)], inc="ve", chain=False)
                    mB = S.op("vector", lambda v, par=par, o=o: v.tensor_tensor(
                        ppb[:, 1024 * par + 512:1024 * par + 1024],
                        ppb[:, 1024 * par + 512:1024 * par + 1024],
                        maskc[:, 512 * o:512 * o + 512], ALU.mult),
                        inc="ve", chain=False)
                    maskve[kti] = mB
                    dep = [("ve", mB)]
                if kti == 0 and prev_tail.get("e2"):
                    dep = dep + [("ev", prev_tail["e2"])]
                deps[kti] = dep
                if kti >= 1:
                    am = emit_att(kti - 1, (kti - 1) % 2, KT)
            am = emit_att(KT - 1, (KT - 1) % 2, KT)
            e2 = S.op("scalar", lambda sc: sc.activation(
                atte[0:65, 0:1024], psab[0:65, 0:1024], AF.Identity),
                waits=[("mm", am), ("ve", prev_tail["tt"])], inc="ev")
            dts = {}
            for j in range(8):
                dts[j] = S.op("tensor", lambda t, j=j: t.transpose(
                    psp[:, j:j + 1], atte[64:65, 128 * j:128 * j + 128],
                    ident32[64:65, 64:65], tile_position=(64, 0)),
                    waits=[("ev", e2)] if j == 0 else (), inc="mm")
            r2 = S.op("vector", lambda v: v.reciprocal(
                dcolF[:, 0:8], psp[:, 0:8]),
                waits=[("mm", dts[7])], inc="ve", chain=False)
            for j in range(8):
                dts[j] = S.op("tensor", lambda t, j=j: t.transpose(
                    psp[0:1, 128 * j:128 * j + 128], dcolF[:, j:j + 1],
                    ident32[:]),
                    waits=[("ve", r2)] if j == 0 else (), inc="mm")
            erow = S.op("scalar", lambda sc: sc.activation(
                rrowB[0:1, 0:1024], psp[0:1, 0:1024], AF.Identity),
                waits=[("mm", dts[7])], inc="ev")
            S.op("tensor", lambda t: t.matmul(
                psp[0:64, 0:512], onesbf[0:1, 0:64], rrowB[0:1, 0:512],
                start=True, stop=True),
                waits=[("ev", erow)])
            b2m = S.op("tensor", lambda t: t.matmul(
                psp[0:64, 512:1024], onesbf[0:1, 0:64],
                rrowB[0:1, 512:1024], start=True, stop=True), inc="mm")
            S.op("vector", lambda v, p=p, qg=qg: v.tensor_tensor(
                arena[0:64, T * p + 512 * qg:T * p + 512 * qg + 512],
                atte[0:64, 0:512], psp[0:64, 0:512], ALU.mult),
                waits=[("mm", b2m)], inc="ve", chain=False)
            vB = S.op("vector", lambda v, p=p: v.tensor_tensor(
                atteB[0:64, :], atte[0:64, 512:1024],
                psp[0:64, 512:1024], ALU.mult),
                waits=[("gv", prev_tail["gv"])], inc="ve", chain=False)
            dB = S.op("sync", lambda g, p=p, qg=qg: g.dma_start(
                out=arena[64:128, T * p + 512 * qg:T * p + 512 * qg + 512],
                in_=atteB[0:64, :]), waits=[("ve", vB)], inc="gv", k=16)
            prev_tail = dict(ev=expB[KT - 1], ve=S.cnt["ve"], gv=dB,
                             mask=maskve[KT - 1], e2=e2, tt=S.cnt["ve"])

    S.fence_all()
    wo = [wqk[0], wqk2[0]]
    wod = wdma(S, sb3(wo[0], 128), cc3(W["l0a_wout"].ap()[:, 0:128]))
    vres = {}
    for ft in range(NCH):
        cur = wo[ft % 2]
        wcur = wod
        if ft < 7:
            wod = wdma(S, sb3(wo[(ft + 1) % 2], 128),
                       cc3(W["l0a_wout"].ap()[:, 128 * (ft + 1):128 * (ft + 1) + 128]),
                       waits=[("mm", S.cnt["mm"])])
        for qg in range(QGN):
            k = 4 * ft + qg
            par = k % 2
            w = []
            if qg == 0:
                w.append(wcur)
            if k >= 2:
                w.append(("ve", vres[k - 2]))
            for c in range(NCH):
                S.op("tensor", lambda t, qg=qg, c=c, cur=cur, par=par: t.matmul(
                    psp[:, 512 * par:512 * par + 512], cur[:, 128 * c:128 * c + 128],
                    arena[:, T * c + 512 * qg:T * c + 512 * qg + 512],
                    start=(c == 0), stop=False), waits=w if c == 0 else ())
            mo = S.op("tensor", lambda t, ft=ft, par=par: t.matmul(
                psp[:, 512 * par:512 * par + 512], borow[0:1, 128 * ft:128 * ft + 128],
                onesbf[0:1, 0:512], start=False, stop=True), inc="mm")
            vres[k] = S.op("vector", lambda v, ft=ft, qg=qg, par=par: v.tensor_tensor(
                x32[:, T * ft + 512 * qg:T * ft + 512 * qg + 512],
                psp[:, 512 * par:512 * par + 512],
                x32[:, T * ft + 512 * qg:T * ft + 512 * qg + 512], ALU.add),
                waits=[("mm", mo)], inc="ve")
    S.fence_all()
    S.emit()
    es.close()
    ln_boundary(st, "_l0a%d" % b)



def phase_l0f(st, b):
    S, nc, W = st["S"], st["nc"], st["W"]
    x32, xbf, arena = st["x32"], st["xbf"], st["arena"]
    onesbf, ones32, identbf = st["onesbf"], st["ones32"], st["identbf"]
    es = ExitStack()
    sbt = lambda n, s, d: es.enter_context(nc.sbuf_tensor(n + "_l0f%d" % b, s, d))
    pst = lambda n, s, d: es.enter_context(nc.psum_tensor(n + "_l0f%d" % b, s, d))

    gwb = sbt("gwb", [128, 64], BF16)
    rsc = sbt("rsc", [128, 1], F32)
    gateT = sbt("gateT", [8, T], BF16)
    grows = [sbt("grows%d" % i, [128, T], BF16) for i in range(2)]
    esg = ExitStack()
    sbg = lambda n, sh, d: esg.enter_context(nc.sbuf_tensor(n + "_l0fg%d" % b, sh, d))
    gs = sbg("gs", [128, 128], F32)
    m1s = sbg("m1s", [128, 16], F32)
    m2s = sbg("m2s", [128, 16], F32)
    d8 = sbg("d8", [128, 128], F32)
    ind1 = sbg("ind1", [128, 128], F32)
    xp8 = sbg("xp8", [128, 128], F32)
    ind2 = sbg("ind2", [128, 128], F32)
    gfull = xp8
    rden = sbg("rden", [128, 16], F32)
    gfb = sbg("gfb", [128, 128], BF16)
    psg = esg.enter_context(nc.psum_tensor("psg_l0f%d" % b, [128, 1024], F32))
    psgt = esg.enter_context(nc.psum_tensor("psgt_l0f%d" % b, [8, 2048], BF16))


    S.fence_all()
    S.op("sync", lambda s: s.dma_start(
        out=rsc[:, :], in_=W["l0f_rs"].ap().to_broadcast((128, 1))), inc="wq", k=16)
    ld0 = wdma(S, sb3(gwb, 8), cc3(W["l0f_gw"].ap()))
    wq0 = S.cnt["wq"]

    gev = {}
    for tt in range(TTN):
        par = tt % 2
        for c in range(NCH):
            w = []
            if tt == 0 and c == 0:
                w = [ld0, ("wq", wq0)]
            if c == 0 and tt >= 2:
                w.append(("ev", gev[tt - 2]))
            S.op("tensor", lambda t, tt=tt, c=c, par=par: t.matmul(
                psg[:, 512 * par:512 * par + 8],
                xbf[:, T * c + 128 * tt:T * c + 128 * tt + 128],
                gwb[:, 8 * c:8 * c + 8], start=(c == 0), stop=(c == NCH - 1)),
                waits=w, inc="mm" if c == NCH - 1 else None)
        gev[tt] = S.op("scalar", lambda sc, tt=tt, par=par: sc.activation(
            gs[:, 8 * tt:8 * tt + 8], psg[:, 512 * par:512 * par + 8], AF.Sigmoid),
            waits=[("mm", S.cnt["mm"])], inc="ev")
    S.op("vector", lambda v: v.tensor_scalar(gs[:, :], gs[:, :], rsc[:, 0:1],
                                             None, ALU.mult),
         waits=[("ev", S.cnt["ev"]), ("wq", wq0)], inc="ve")
    for tt in range(TTN):
        S.op("vector", lambda v, tt=tt: v.reduce_max(
            m1s[:, tt:tt + 1], gs[:, 8 * tt:8 * tt + 8], axis=AX.X),
            inc="ve" if tt == TTN - 1 else None)
    g1 = S.cnt["ve"]
    for tt in range(TTN):
        S.op("vector", lambda v, tt=tt: v.tensor_scalar(
            d8[:, 8 * tt:8 * tt + 8], gs[:, 8 * tt:8 * tt + 8],
            m1s[:, tt:tt + 1], 1e15, ALU.subtract, ALU.mult),
            waits=[("ve", g1)] if tt == 0 else (),
            inc="ve" if tt == TTN - 1 else None)
    e1 = S.op("scalar", lambda sc: sc.activation(ind1[:, :], d8[:, :], AF.Exp),
              waits=[("ve", S.cnt["ve"])], inc="ev")
    S.op("vector", lambda v: v.scalar_tensor_tensor(
        xp8[:, :], ind1[:, :], -1e22, gs[:, :], ALU.mult, ALU.add),
        waits=[("ev", e1)], inc="ve")
    for tt in range(TTN):
        S.op("vector", lambda v, tt=tt: v.reduce_max(
            m2s[:, tt:tt + 1], xp8[:, 8 * tt:8 * tt + 8], axis=AX.X),
            inc="ve" if tt == TTN - 1 else None)
    g2 = S.cnt["ve"]
    for tt in range(TTN):
        S.op("vector", lambda v, tt=tt: v.tensor_scalar(
            d8[:, 8 * tt:8 * tt + 8], xp8[:, 8 * tt:8 * tt + 8],
            m2s[:, tt:tt + 1], 1e15, ALU.subtract, ALU.mult),
            waits=[("ve", g2)] if tt == 0 else (),
            inc="ve" if tt == TTN - 1 else None)
    e2 = S.op("scalar", lambda sc: sc.activation(ind2[:, :], d8[:, :], AF.Exp),
              waits=[("ve", S.cnt["ve"])], inc="ev")
    S.op("vector", lambda v: v.tensor_tensor(ind1[:, :], ind1[:, :], ind2[:, :],
                                             ALU.add),
         waits=[("ev", e2)], inc="ve")
    for tt in range(TTN):
        S.op("vector", lambda v, tt=tt: v.tensor_scalar(
            rden[:, tt:tt + 1], m1s[:, tt:tt + 1], m2s[:, tt:tt + 1], None, ALU.add),
            inc="ve" if tt == TTN - 1 else None)
    vpre = S.cnt["ve"]
    S.op("vector", lambda v: v.reciprocal(rden[:, :], rden[:, :]),
         waits=[("ve", vpre)], inc="ve")
    g3 = S.op("vector", lambda v: v.tensor_tensor(gfull[:, :], gs[:, :], ind1[:, :],
                                                  ALU.mult), inc="ve")
    for tt in range(TTN):
        S.op("vector", lambda v, tt=tt: v.tensor_scalar(
            gfull[:, 8 * tt:8 * tt + 8], gfull[:, 8 * tt:8 * tt + 8],
            rden[:, tt:tt + 1], None, ALU.mult),
            waits=[("ve", g3)] if tt == 0 else (),
            inc="ve" if tt == TTN - 1 else None)
    vgf = S.op("vector", lambda v: v.tensor_copy(gfb[:, :], gfull[:, :]),
               waits=[("ve", S.cnt["ve"])], inc="ve")
    gtev = {}
    for tt in range(TTN):
        par = tt % 2
        w = [("ve", vgf)] if tt == 0 else []
        if tt >= 2:
            w.append(("ev", gtev[tt - 2]))
        mt = S.op("tensor", lambda t, tt=tt, par=par: t.transpose(
            psgt[0:8, 1024 * par:1024 * par + 128], gfb[:, 8 * tt:8 * tt + 8],
            identbf[:]), waits=w, inc="mm")
        gtev[tt] = S.op("scalar", lambda sc, tt=tt, par=par: sc.activation(
            gateT[0:8, 128 * tt:128 * tt + 128], psgt[0:8, 1024 * par:1024 * par + 128],
            AF.Identity), waits=[("mm", mt)], inc="ev")
    grow_wg = None
    for e in range(8):
        grow_wg = S.op("gpsimd", lambda g, e=e: g.dma_start(
            out=grows[e // 4][32 * (e % 4):32 * (e % 4) + 1, :],
            in_=gateT[e:e + 1, :]),
            waits=[("ev", S.cnt["ev"])] if e == 0 else (), inc="wg", k=16)

    S.emit()
    esg.close()
    hext = sbt("hext", [128, 20 * 1024], BF16)
    w1b = [sbt("w1b%d" % i, [128, NCH * 256], BF16) for i in range(2)]
    w2b = [sbt("w2b%d" % i, [128, 4 * 128], BF16) for i in range(2)]
    b1c = sbt("b1c", [128, 4], F32)
    b2sb = sbt("b2sb", [8, D], BF16)
    bs2row = sbt("bs2row", [1, D], BF16)
    psh = pst("psh", [128, 1024], F32)
    psgb = pst("psgb", [128, 1024], F32)
    psy = pst("psy", [128, 1024], F32)

    def hslot(e, hc):
        slot = 4 * e + hc
        if slot < 16:
            return arena[:, 1024 * slot:1024 * slot + 1024]
        return hext[:, 1024 * (slot - 16):1024 * (slot - 16) + 1024]

    b2k = wdma(S, b2sb[:, :], W["l0f_b2"].ap())
    b2_wg = wdma(S, bs2row[0:1, :], W["l0f_bs2"].ap().unsqueeze(0))

    def w1src(e, half):
        wsl = W["l0f_w1"].ap()[e] if e < 8 else W["l0f_ws1"].ap()
        return cc3(wsl[:, 256 * half:256 * half + 256])

    for th in range(2):
        w1d_next = wdma(S, sb3(w1b[0], 256), w1src(0, 0),
                        waits=[("mm", S.cnt["mm"]), ("ve", S.cnt["ve"]),
                               ("ev", S.cnt["ev"])])
        gb = None
        hdone = {}
        for e in range(9):
            bsrc = bias_col(W["l0f_b1"].ap()[e]) if e < 8 else bias_col(W["l0f_bs1"].ap())
            b1d = S.op("sync", lambda s, bsrc=bsrc: s.dma_start(
                out=b1c[:, :], in_=bsrc, allow_slow_non_contiguous=True),
                waits=[("ev", S.cnt["ev"])], inc="wq", k=16)
            if e < 8:
                S.op("tensor", lambda t, e=e, th=th: t.matmul(
                    psgb[:, 0:512], onesbf[32 * (e % 4):32 * (e % 4) + 1, 0:128],
                    grows[e // 4][32 * (e % 4):32 * (e % 4) + 1,
                                  1024 * th:1024 * th + 512],
                    start=True, stop=True, tile_position=(32 * (e % 4), 0)),
                    waits=[("wg", grow_wg), ("ve", S.cnt["ve"])])
                gb = S.op("tensor", lambda t, e=e, th=th: t.matmul(
                    psgb[:, 512:1024], onesbf[32 * (e % 4):32 * (e % 4) + 1, 0:128],
                    grows[e // 4][32 * (e % 4):32 * (e % 4) + 1,
                                  1024 * th + 512:1024 * th + 1024],
                    start=True, stop=True, tile_position=(32 * (e % 4), 0)), inc="mm")
            for half in range(2):
                wcur = w1d_next
                cbuf = w1b[(2 * e + half) % 2]
                if not (e == 8 and half == 1):
                    ne, nh = (e, 1) if half == 0 else (e + 1, 0)
                    w1d_next = wdma(S, sb3(w1b[(2 * e + half + 1) % 2], 256),
                                    w1src(ne, nh), waits=[("mm", S.cnt["mm"])])
                for hc in (2 * half, 2 * half + 1):
                    for qg in range(2):
                        for c in range(NCH):
                            w = []
                            if qg == 0 and c == 0:
                                w = [wcur, ("wq", b1d), ("ev", S.cnt["ev"])]
                            S.op("tensor", lambda t, hc=hc, qg=qg, c=c, cbuf=cbuf,
                                 half=half, th=th: t.matmul(
                                psh[:, 512 * qg:512 * qg + 512],
                                cbuf[:, 256 * c + 128 * (hc % 2):256 * c + 128 * (hc % 2) + 128],
                                xbf[:, T * c + 1024 * th + 512 * qg:
                                    T * c + 1024 * th + 512 * qg + 512],
                                start=(c == 0), stop=(c == NCH - 1)),
                                waits=w, inc="mm" if (qg == 1 and c == NCH - 1) else None)
                    he = S.op("scalar", lambda sc, e=e, hc=hc: sc.activation(
                        hslot(e, hc), psh[:, :], AF.Gelu, bias=b1c[:, hc:hc + 1]),
                        waits=[("mm", S.cnt["mm"])], inc="ev")
                    if e < 8:
                        hdone[(e, hc)] = S.op("vector", lambda v, e=e, hc=hc: v.tensor_tensor(
                            hslot(e, hc), hslot(e, hc), psgb[:, :], ALU.mult),
                            waits=[("ev", he), ("mm", gb)], inc="ve")
                    else:
                        hdone[(e, hc)] = S.cnt["ve"]
        hready = dict(ve=S.cnt["ve"], ev=S.cnt["ev"])

        def w2src(e, ft):
            if e < 8:
                return cc3(W["l0f_w2"].ap()[e][:, 128 * ft:128 * ft + 128])
            return cc3(W["l0f_ws2"].ap()[:, 128 * ft:128 * ft + 128])

        w2d_next = wdma(S, sb3(w2b[0], 128), w2src(0, 0), waits=[("mm", S.cnt["mm"])])
        vres = {}
        for ft in range(NCH):
            for e in range(9):
                idx = ft * 9 + e
                wcur = w2d_next
                cbuf = w2b[idx % 2]
                pre_mm = S.cnt["mm"]
                for qg in range(2):
                    for hc in range(4):
                        w = []
                        if e == 0 and hc == 0:
                            w = [wcur, ("ve", hready["ve"]), ("ev", hready["ev"])]
                            if ft >= 1 and qg == 0:
                                w.append(("ve", vres[(ft - 1, 1)]))
                        elif hc == 0 and qg == 0:
                            w = [wcur]
                        last = (qg == 1 and hc == 3)
                        S.op("tensor", lambda t, e=e, hc=hc, qg=qg, cbuf=cbuf: t.matmul(
                            psy[:, 512 * qg:512 * qg + 512],
                            cbuf[:, 128 * hc:128 * hc + 128],
                            hslot(e, hc)[:, 512 * qg:512 * qg + 512],
                            start=(e == 0 and hc == 0), stop=False), waits=w,
                            inc="mm" if last else None)
                nft, ne = (ft, e + 1) if e < 8 else (ft + 1, 0)
                if nft < NCH:
                    w2d_next = wdma(S, sb3(w2b[(idx + 1) % 2], 128), w2src(ne, nft),
                                    waits=[("mm", pre_mm)])
            for qg in range(2):
                S.op("tensor", lambda t, ft=ft, th=th, qg=qg: t.matmul(
                    psy[:, 512 * qg:512 * qg + 512], b2sb[0:8, 128 * ft:128 * ft + 128],
                    gateT[0:8, 1024 * th + 512 * qg:1024 * th + 512 * qg + 512],
                    start=False, stop=False),
                    waits=[b2k] if (ft == 0 and qg == 0) else ())
                my = S.op("tensor", lambda t, ft=ft, qg=qg: t.matmul(
                    psy[:, 512 * qg:512 * qg + 512], bs2row[0:1, 128 * ft:128 * ft + 128],
                    onesbf[0:1, 0:512], start=False, stop=True),
                    waits=[b2_wg] if (ft == 0 and qg == 0) else (), inc="mm")
                vres[(ft, qg)] = S.op("vector", lambda v, ft=ft, th=th, qg=qg: v.tensor_tensor(
                    x32[:, T * ft + 1024 * th + 512 * qg:T * ft + 1024 * th + 512 * qg + 512],
                    psy[:, 512 * qg:512 * qg + 512],
                    x32[:, T * ft + 1024 * th + 512 * qg:T * ft + 1024 * th + 512 * qg + 512],
                    ALU.add), waits=[("mm", my)], inc="ve")
        S.fence_all()
    S.emit()
    es.close()
    ln_boundary(st, "_l0f%d" % b)



def phase_l1a(st, b):
    S, nc, W = st["S"], st["nc"], st["W"]
    x32, xbf, arena = st["x32"], st["xbf"], st["arena"]
    onesbf, ones32, maskc, identbf = st["onesbf"], st["ones32"], st["maskc"], st["identbf"]
    pswap, CD = st["pswap"], st["CD"]
    es = ExitStack()
    sbt = lambda n, s, d: es.enter_context(nc.sbuf_tensor(n + "_l1a%d" % b, s, d))
    pst = lambda n, s, d: es.enter_context(nc.psum_tensor(n + "_l1a%d" % b, s, d))
    SC = 1.0 / math.sqrt(96.0)
    ropeC = sbt("ropeC", [128, T], BF16)
    ropeS = sbt("ropeS", [128, T], BF16)

    latT = sbt("latT", [128, 2 * T], BF16)
    qrp = [sbt("qrp%d" % i, [128, T], BF16) for i in range(2)]
    qcp = sbt("qcp", [128, T], BF16)
    kcp = sbt("kcp", [128, T], BF16)
    qasm = [sbt("qasm%d" % i, [128, T], BF16) for i in range(2)]
    kasm = [sbt("kasm%d" % i, [128, T], BF16) for i in range(2)]
    vaug = [sbt("vaug%d" % i, [128, TTN * 65], BF16) for i in range(2)]
    mega = sbt("mega", [128, 16 * 512], BF16)
    ppr = mega
    q2full = nc.virtual_sbuf_tensor("q2_l1a%d" % b, [128, 16 * 512], BF16)
    nc.back_virtual_tensor_with(q2full, mega)
    q2 = q2full[:, 0:T]
    krT = q2full[:, T:2 * T]
    macc = sbt("macc", [128, 1536], BF16)
    pmaxs = sbt("pmaxs", [128, 4], BF16)
    scr = sbt("scr", [128, 512], BF16)
    dcol = sbt("dcol", [128, 4], BF16)
    atteS = sbt("atteS", [128, 512], BF16)
    scrB = atteS[64:128, :]
    nB = sbt("nB", [128, 1], F32)
    ccol = sbt("ccol", [128, 16], F32)
    lrow = sbt("lrow", [1, 16], F32)
    sc1 = sbt("sc1", [128, 8], F32)
    wtv = nc.virtual_sbuf_tensor("wtv_l1a%d" % b, [128, 16 * 512], BF16)
    nc.back_virtual_tensor_with(wtv, mega)
    wt = [wtv[:, 4096:5120], wtv[:, 5120:6144]]
    wkv = wtv[:, 6144:6656]

    pss = pst("pss", [128, 2048], F32)
    psp = pss[:, 0:1024]
    psmt = pst("psmt", [128, 1024], BF16)
    psatF = pst("psatF", [128, 512], F32)
    prow = pst("prow", [128, 512], BF16)
    psbc = pst("psbc", [128, 512], F32)

    S.fence_all()
    S.op("sync", lambda s: s.dma_start(out=ropeC[:], in_=CD["ropeC"].ap()), inc="wq", k=16)
    rop_wq = S.op("sync", lambda s: s.dma_start(out=ropeS[:], in_=CD["ropeS"].ap()),
                  inc="wq", k=16)
    ldl = S.op("sync", lambda s: s.dma_start(out=lrow[0:1, :], in_=W["l1a_lobo"].ap()),
               inc="wq", k=16)
    el = S.op("scalar", lambda sc: sc.activation(lrow[0:1, :], lrow[0:1, :], AF.Exp),
              waits=[("wq", ldl)], inc="ev")
    ml = S.op("tensor", lambda t: t.matmul(psp[:, 0:16], ones32[0:1, 0:128], lrow[0:1, :],
                                           start=True, stop=True),
              waits=[("ev", el)], inc="mm")
    ecc = S.op("scalar", lambda sc: sc.activation(ccol[:, :], psp[:, 0:16], AF.Identity),
               waits=[("mm", ml)], inc="ev")

    def proj_fm(dst, wbuf, nchunks, src, srcT, evwaits, tag):
        evs = []
        for qh in range(2):
            for qg2 in range(2):
                qg = 2 * qh + qg2
                for c in range(nchunks):
                    w = []
                    if qh == 0 and qg2 == 0 and c == 0:
                        w = list(evwaits)
                    elif qh == 1 and qg2 == 0 and c == 0:
                        w = [("ev", evs[0])]
                    S.op("tensor", lambda t, qg=qg, qg2=qg2, c=c: t.matmul(
                        psp[:, 512 * qg2:512 * qg2 + 512],
                        wbuf[:, 128 * c:128 * c + 128],
                        src[:, srcT * c + 512 * qg:srcT * c + 512 * qg + 512],
                        start=(c == 0), stop=(c == nchunks - 1)),
                        waits=w, inc="mm" if (qg2 == 1 and c == nchunks - 1) else None)
            evs.append(S.op("scalar", lambda sc, qh=qh: sc.activation(
                dst[:, 1024 * qh:1024 * qh + 1024], psp[:, :], AF.Identity),
                waits=[("mm", S.cnt["mm"])], inc="ev"))
        return evs[-1]

    for lc in range(2):
        wd = wdma(S, sb3(wt[lc], 128), cc3(W["l1a_wkvd"].ap()[:, 128 * lc:128 * lc + 128]),
                  waits=[("mm", S.cnt["mm"])] if lc == 0 else ())
        proj_fm(_slice2(latT, T * lc, T),
                wt[lc], NCH, xbf, T,
                [wd, ("ev", max(ecc, S.cnt["ev"]))], "lat%d" % lc)
    wd = wdma(S, sb3(wt[0], 32)[:, 0:8, :], cc3(W["l1a_wkr"].ap()),
              waits=[("ev", S.cnt["ev"])])
    krev = {}
    for qg in range(QGN):
        for c in range(NCH):
            w = []
            if qg == 0 and c == 0:
                w = [wd]
            elif qg >= 2 and c == 0:
                w = [("ev", krev[qg - 2])]
            S.op("tensor", lambda t, qg=qg, c=c: t.matmul(
                psp[0:32, 512 * (qg % 2):512 * (qg % 2) + 512], wt[0][:, 32 * c:32 * c + 32],
                xbf[:, T * c + 512 * qg:T * c + 512 * qg + 512],
                start=(c == 0), stop=(c == NCH - 1)), waits=w,
                inc="mm" if c == NCH - 1 else None)
        krev[qg] = S.op("scalar", lambda sc, qg=qg: sc.activation(
            krT[0:32, 512 * qg:512 * qg + 512], psp[0:32, 512 * (qg % 2):512 * (qg % 2) + 512],
            AF.Identity), waits=[("mm", S.cnt["mm"])], inc="ev")
    ekr = S.cnt["ev"]

    def rope2(srcbuf, rows, evwait):
        lastve = None
        for qg in range(QGN):
            half = qg % 2
            mm1 = S.op("tensor", lambda t, qg=qg, half=half: t.matmul(
                psp[0:rows, 512 * half:512 * half + 512], pswap[0:rows, 0:rows],
                srcbuf[0:rows, 512 * qg:512 * qg + 512], start=True, stop=True),
                waits=([("ev", evwait), ("wq", rop_wq)] if qg == 0 else []) +
                      ([("ve", lastve)] if qg >= 2 else []), inc="mm")
            S.op("vector", lambda v, qg=qg, half=half: v.tensor_tensor(
                q2[0:rows, 512 * half:512 * half + 512],
                psp[0:rows, 512 * half:512 * half + 512],
                ropeS[0:rows, 512 * qg:512 * qg + 512], ALU.mult),
                waits=[("mm", mm1)], inc="ve")
            S.op("vector", lambda v, qg=qg: v.tensor_tensor(
                srcbuf[0:rows, 512 * qg:512 * qg + 512],
                srcbuf[0:rows, 512 * qg:512 * qg + 512],
                ropeC[0:rows, 512 * qg:512 * qg + 512], ALU.mult), inc="ve")
            lastve = S.op("vector", lambda v, qg=qg, half=half: v.tensor_tensor(
                srcbuf[0:rows, 512 * qg:512 * qg + 512],
                srcbuf[0:rows, 512 * qg:512 * qg + 512],
                q2[0:rows, 512 * half:512 * half + 512], ALU.add), inc="ve")
        return lastve

    kr_ve = rope2(krT, 32, ekr)
    krd0 = S.op("sync", lambda g: g.dma_start(
        out=kasm[0][64:96, :], in_=krT[0:32, :]), waits=[("ve", kr_ve)],
        inc="gv", k=16)
    krd1 = S.op("sync", lambda g: g.dma_start(
        out=kasm[1][64:96, :], in_=krT[0:32, :]), inc="gv", k=16)

    qr_ve = {}

    def make_qr_pack(pk, buf):
        wd = wdma(S, sb3(wt[1], 128), cc3(W["l1a_wqr"].ap()[:, 128 * pk:128 * pk + 128]),
                  waits=[("mm", S.cnt["mm"]), ("ve", S.cnt["ve"])])
        ev = proj_fm(buf, wt[1], NCH, xbf, T,
                     [wd, ("ve", S.cnt["ve"]), ("ev", S.cnt["ev"])], "qr")
        return rope2(buf, 128, ev)

    qr_ve[0] = make_qr_pack(0, qrp[0])

    prev_eex = None
    prev_e4 = None
    prev_att = 0
    prev_vm = None
    prev_vma = None
    prev_KT = 4
    prev_evat = 0
    prev_tt = 0
    prev_trans_mm = 0
    prev_odma = 0
    for h in range(16):
        pr = h // 2
        pk = h // 4
        am = h % 2
        if h % 2 == 0:
            wv_pre = ([("mm", prev_att), ("ve", prev_vm)] if prev_vm is not None
                      else [("mm", S.cnt["mm"]), ("ve", S.cnt["ve"])])
            wd = wdma(S, sb3(wt[0], 128), cc3(W["l1a_wq"].ap()[:, 128 * pr:128 * pr + 128]),
                      waits=list(wv_pre))
            eqc = proj_fm(qcp, wt[0], NCH, xbf, T,
                          [wd, ("ev", S.cnt["ev"])], "qc")
            wd2 = wdma(S, sb3(wkv, 128)[:, 0:2, :],
                       cc3(W["l1a_wku"].ap()[:, 128 * pr:128 * pr + 128]),
                       waits=list(wv_pre))
            ekc = proj_fm(kcp, wkv, 2, latT, T,
                          [wd2, ("ev", S.cnt["ev"])], "kc")
            wd3 = wdma(S, sb3(wkv, 128)[:, 2:4, :],
                       cc3(W["l1a_wvu"].ap()[:, 128 * pr:128 * pr + 128]),
                       waits=list(wv_pre))
            vv = {}
            for tt in range(TTN):
                par = tt % 2
                w = [wd3, ("ev", S.cnt["ev"])] if tt == 0 else []
                if tt >= 2:
                    w.append(("ev", vv[tt - 2]))
                for lc in range(2):
                    S.op("tensor", lambda t, tt=tt, lc=lc, par=par: t.matmul(
                        psp[:, 512 * par:512 * par + 128],
                        latT[:, T * lc + 128 * tt:T * lc + 128 * tt + 128],
                        wkv[:, 256 + 128 * lc:256 + 128 * lc + 128],
                        start=(lc == 0), stop=(lc == 1)),
                        waits=w if lc == 0 else (), inc="mm" if lc == 1 else None)
                mmv = S.cnt["mm"]
                S.op("scalar", lambda sc, tt=tt, par=par: sc.activation(
                    vaug[0][:, 65 * tt:65 * tt + 64], psp[:, 512 * par:512 * par + 64],
                    AF.Identity), waits=[("mm", mmv)], inc="ev")
                vv[tt] = S.op("scalar", lambda sc, tt=tt, par=par: sc.activation(
                    vaug[1][:, 65 * tt:65 * tt + 64],
                    psp[:, 512 * par + 64:512 * par + 128], AF.Identity), inc="ev")
            S.op("vector", lambda v: v.memset(
                vaug[0][:].rearrange("p (a n) -> p a n", n=65)[:, :, 64:65], 1.0),
                waits=[("ev", S.cnt["ev"])], inc="ve")
            S.op("vector", lambda v: v.memset(
                vaug[1][:].rearrange("p (a n) -> p a n", n=65)[:, :, 64:65], 1.0),
                inc="ve")
        if h % 4 == 0 and pk + 1 < 4:
            qr_ve[pk + 1] = make_qr_pack(pk + 1, qrp[(pk + 1) % 2])

        asm_deps = []
        if h % 2 == 0:
            a1 = S.op("sync", lambda g, am=am: g.dma_start(
                out=qasm[am][0:64, :], in_=qcp[0:64, :]),
                waits=[("ev", S.cnt["ev"]), ("mm", S.cnt["mm"])], inc="gv", k=16)
            a2 = S.op("sync", lambda g, am=am: g.dma_start(
                out=kasm[am][0:64, :], in_=kcp[0:64, :]), inc="gv", k=16)
            asm_deps.append(("gv", a2))
        else:
            a1 = S.op("sync", lambda g, am=am: g.dma_start(
                out=qasm[am][0:64, :], in_=qcp[64:128, :]),
                waits=[("ev", S.cnt["ev"]), ("mm", S.cnt["mm"]), ("ve", S.cnt["ve"])],
                inc="gv", k=16)
            a2 = S.op("sync", lambda g, am=am: g.dma_start(
                out=kasm[am][0:64, :], in_=kcp[64:128, :]), inc="gv", k=16)
            asm_deps.append(("gv", a2))
        a3 = S.op("sync", lambda g, am=am, pk=pk, h=h: g.dma_start(
            out=qasm[am][64:96, :], in_=qrp[pk % 2][32 * (h % 4):32 * (h % 4) + 32, :]),
            waits=[("ve", qr_ve[pk])], inc="gv", k=16)
        asm_deps.append(("gv", a3))

        ebn = bcalc(S, pss[:, 1024:2048], pss[:, 0:1024], sc1, nB, q2,
                    qasm[am], kasm[am], 96, SC, onesbf, ones32,
                    asm_deps=asm_deps + [("mm", S.cnt["mm"]), ("ve", S.cnt["ve"])])

        for qg in range(QGN):
            KT = 4 * (qg + 1)
            eex = {}
            for kti in range(KT):
                w = []
                if kti == 0:
                    w = [("ev", ebn)]
                    if prev_vm is not None:
                        w.append(("ve", prev_vm if prev_KT == 4 else prev_vma))
                if kti < 4 and prev_e4 is not None:
                    w.append(("ev", prev_e4[kti]))
                if prev_vm is not None and kti == prev_KT - 4 and kti > 0:
                    w.append(("ve", prev_vm))
                if kti >= 4:
                    w.append(("ev", eex[kti - 4]))
                smm = S.op("tensor", lambda t, am=am, qg=qg, kti=kti: t.matmul(
                    pss[:, 512 * (kti % 4):512 * (kti % 4) + 512],
                    kasm[am][0:96, 128 * kti:128 * kti + 128],
                    qasm[am][0:96, 512 * qg:512 * qg + 512], start=True, stop=True),
                    waits=w, inc="mm")
                eex[kti] = S.op("scalar", lambda sc, kti=kti: sc.activation(
                    ppr[:, 512 * kti:512 * kti + 512],
                    pss[:, 512 * (kti % 4):512 * (kti % 4) + 512],
                    AF.Exp, bias=nB[:, 0:1], scale=np.float32(SC)),
                    waits=[("mm", smm), ("ev", ebn)], inc="ev")
            prev_eex = eex[KT - 1]
            prev_e4 = [eex[KT - 4 + j] for j in range(4)]
            ch0 = (prev_vm if prev_vm is not None else False)
            ND = KT - 4
            if ND >= 8:
                mid = ND // 2
                va1 = S.op("vector", lambda v, mid=mid: v.reduce_max(
                    macc[:, 0:512],
                    ppr[:, 0:512 * mid].rearrange("p (t q) -> p q t", t=mid),
                    axis=AX.X),
                    waits=[("ev", eex[mid - 1]), ("mm", prev_trans_mm)],
                    inc="ve", chain=ch0)
                va2 = S.op("vector", lambda v, mid=mid, ND=ND: v.reduce_max(
                    macc[:, 512:1024],
                    ppr[:, 512 * mid:512 * ND].rearrange(
                        "p (t q) -> p q t", t=ND - mid),
                    axis=AX.X),
                    waits=[("ev", eex[ND - 1])], inc="ve", chain=False)
                vma = S.op("vector", lambda v: v.reduce_max(
                    macc[:, 1024:1536],
                    macc[:, 0:1024].rearrange("p (b q) -> p q b", b=2),
                    axis=AX.X), inc="ve", chain=va2)
            elif ND > 0:
                vma = S.op("vector", lambda v, ND=ND: v.reduce_max(
                    macc[:, 1024:1536],
                    ppr[:, 0:512 * ND].rearrange("p (t q) -> p q t", t=ND),
                    axis=AX.X),
                    waits=[("ev", eex[ND - 1]), ("mm", prev_trans_mm)],
                    inc="ve", chain=ch0)
            else:
                vma = S.op("vector", lambda v: v.memset(macc[:, 1024:1536], 0.0),
                           waits=[("mm", prev_trans_mm)],
                           inc="ve", chain=ch0)
            mve = S.op("vector", lambda v, KT=KT: v.tensor_tensor(
                ppr[:, 512 * (KT - 4):512 * KT], ppr[:, 512 * (KT - 4):512 * KT],
                maskc[:, :], ALU.mult), waits=[("ev", prev_eex)], inc="ve",
                chain=False)
            for kti in range(KT):
                w = []
                if kti == KT - 4:
                    w.append(("ve", mve))
                elif kti < KT - 4:
                    w.append(("ev", eex[kti]))
                if kti == 0:
                    w.append(("ev", prev_evat))
                S.op("tensor", lambda t, kti=kti, h=h, KT=KT: t.matmul(
                    psatF[0:65, 0:512],
                    vaug[h % 2][:, 65 * kti:65 * kti + 65],
                    ppr[:, 512 * kti:512 * kti + 512],
                    start=(kti == 0), stop=(kti == KT - 1)),
                    waits=w, inc="mm" if kti == KT - 1 else None)
            att_done = S.cnt["mm"]
            ev1 = S.op("scalar", lambda sc: sc.activation(
                scr[64:65, 0:512], psatF[64:65, 0:512], AF.Identity),
                waits=[("mm", att_done), ("ve", prev_tt)], inc="ev")
            evat = S.op("scalar", lambda sc: sc.activation(
                atteS[0:64, 0:512], psatF[0:64, 0:512], AF.Identity),
                waits=[("gv", prev_odma)], inc="ev")
            vm = S.op("vector", lambda v, KT=KT: v.reduce_max(
                macc[:, 512:1024],
                ppr[:, 512 * (KT - 4):512 * KT].rearrange("p (t q) -> p q t", t=4),
                axis=AX.X), inc="ve", chain=mve)
            for jt in range(4):
                mt = S.op("tensor", lambda t, jt=jt: t.transpose(
                    psmt[:, 256 * jt:256 * jt + 128],
                    macc[:, 1024 + 128 * jt:1024 + 128 * jt + 128], identbf[:]),
                    waits=[("ve", vm)] if jt == 0 else (), inc="mm")
                mt = S.op("tensor", lambda t, jt=jt: t.transpose(
                    psmt[:, 256 * jt + 128:256 * jt + 256],
                    macc[:, 512 + 128 * jt:512 + 128 * jt + 128], identbf[:]),
                    inc="mm")
            mtr = S.op("vector", lambda v: v.reduce_max(
                pmaxs[:, 0:4], psmt[:, :].rearrange("p (b c) -> p b c", c=256),
                axis=AX.X), waits=[("mm", mt)], inc="ve", chain=False)
            dts = {}
            for j in range(4):
                dts[j] = S.op("tensor", lambda t, j=j: t.transpose(
                    psmt[:, 2 * j:2 * j + 1], scr[64:65, 128 * j:128 * j + 128],
                    identbf[64:65, 64:65], tile_position=(64, 0)),
                    waits=([("ve", mtr), ("ev", ev1)] if j == 0 else ()),
                    inc="mm")
            stt = S.op("vector", lambda v, h=h: _stt_bf16(
                nc, v,
                dcol[:, 0:4].rearrange("p (a b) -> p a b", b=1),
                pmaxs[:, 0:4].rearrange("p (a b) -> p a b", b=1),
                ccol[:, h:h + 1],
                psmt[:, 0:8].rearrange("p (a b) -> p a b", b=2)[:, :, 0:1]),
                waits=[("mm", dts[3])], inc="ve", chain=False)
            rcp = S.op("vector", lambda v: _rcp_bf16(
                nc, v, dcol[:, 0:4], dcol[:, 0:4]),
                inc="ve", chain=stt)
            prt = {}
            for jt in range(4):
                prt[jt] = S.op("tensor", lambda t, jt=jt: t.transpose(
                    prow[64:65, 128 * jt:128 * jt + 128], dcol[:, jt:jt + 1],
                    identbf[:], tile_position=(0, 64)),
                    waits=[("ve", rcp)] if jt == 0 else (), inc="mm")
            prev_trans_mm = S.cnt["mm"]
            erow = S.op("scalar", lambda sc: sc.activation(
                scrB[0:1, 0:512], prow[64:65, 0:512], AF.Identity),
                waits=[("mm", prt[3])], inc="ev")
            bc = S.op("tensor", lambda t: t.matmul(
                psbc[0:64, 0:512], onesbf[64:65, 0:64], scrB[0:1, 0:512],
                start=True, stop=True), waits=[("ev", erow)], inc="mm")
            acol = T * (h // 2) + 512 * qg
            if h % 2 == 0:
                tt = S.op("vector", lambda v, acol=acol: v.tensor_tensor(
                    arena[0:64, acol:acol + 512], atteS[0:64, 0:512],
                    psbc[0:64, 0:512], ALU.mult),
                    waits=[("mm", bc), ("ev", evat)], inc="ve", chain=False)
            else:
                tt = S.op("vector", lambda v: v.tensor_tensor(
                    atteS[0:64, 0:512], atteS[0:64, 0:512],
                    psbc[0:64, 0:512], ALU.mult),
                    waits=[("mm", bc), ("ev", evat)], inc="ve", chain=False)
                prev_odma = S.op("sync", lambda g, acol=acol: g.dma_start(
                    out=arena[64:128, acol:acol + 512], in_=atteS[0:64, 0:512]),
                    waits=[("ve", tt)], inc="gv", k=16)
            prev_tt = tt
            prev_evat = evat
            prev_att = att_done
            prev_vm = vm
            prev_vma = vma
            prev_KT = KT

    S.fence_all()
    wo = [wt[0], wt[1]]
    wod = wdma(S, sb3(wo[0], 128), cc3(W["l1a_wo"].ap()[:, 0:128]))
    vres = {}
    for ft in range(NCH):
        cur = wo[ft % 2]
        wcur = wod
        if ft < 7:
            wod = wdma(S, sb3(wo[(ft + 1) % 2], 128),
                       cc3(W["l1a_wo"].ap()[:, 128 * (ft + 1):128 * (ft + 1) + 128]),
                       waits=[("mm", S.cnt["mm"])])
        for qg in range(QGN):
            k = 4 * ft + qg
            par = k % 2
            w = []
            if qg == 0:
                w.append(wcur)
            if k >= 2:
                w.append(("ve", vres[k - 2]))
            for c in range(NCH):
                S.op("tensor", lambda t, qg=qg, c=c, cur=cur, par=par: t.matmul(
                    psp[:, 512 * par:512 * par + 512], cur[:, 128 * c:128 * c + 128],
                    arena[:, T * c + 512 * qg:T * c + 512 * qg + 512],
                    start=(c == 0), stop=(c == NCH - 1)),
                    waits=w if c == 0 else (),
                    inc="mm" if c == NCH - 1 else None)
            mo = S.cnt["mm"]
            vres[k] = S.op("vector", lambda v, ft=ft, qg=qg, par=par: v.tensor_tensor(
                x32[:, T * ft + 512 * qg:T * ft + 512 * qg + 512],
                psp[:, 512 * par:512 * par + 512],
                x32[:, T * ft + 512 * qg:T * ft + 512 * qg + 512], ALU.add),
                waits=[("mm", mo)], inc="ve")
    S.fence_all()
    S.emit()
    es.close()
    ln_boundary(st, "_l1a%d" % b)


def _slice2(buf, off, n):
    return buf[:, off:off + n]



def phase_l1f(st, b):
    S, nc, W = st["S"], st["nc"], st["W"]
    x32, xbf, arena = st["x32"], st["xbf"], st["arena"]
    onesbf, ones32, identbf = st["onesbf"], st["ones32"], st["identbf"]
    es = ExitStack()
    sbt = lambda n, s, d: es.enter_context(nc.sbuf_tensor(n + "_l1f%d" % b, s, d))
    SC = 1.0 / 32.0

    nwc = sbt("nwc", [128, 8], F32)
    nbc = sbt("nbc", [128, 8], F32)
    qbc = sbt("qbc", [128, 8], F32)
    kbc = sbt("kbc", [128, 8], F32)
    vbrow = sbt("vbrow", [1, D], BF16)
    obrow = sbt("obrow", [1, D], BF16)
    wraw = sbt("wraw", [128, 1024], BF16)
    qwT = sbt("qwT", [128, 1024], BF16)
    kwT = sbt("kwT", [128, 1024], BF16)
    vwT = sbt("vwT", [128, 1024], BF16)
    owT = sbt("owT", [128, 1024], BF16)
    qtb = sbt("qtb", [128, T], BF16)
    ktb = sbt("ktb", [128, T], BF16)
    vaug = sbt("vaug", [128, TTN * 129], BF16)
    ppr = sbt("ppr", [128, 16 * 512], BF16)
    attsc = sbt("attsc", [128, 4 * 128], BF16)
    sccol = sbt("sccol", [128, 4], F32)
    nB = sbt("nB", [128, 1], F32)
    q2 = sbt("q2", [128, T], BF16)
    sc1 = sbt("sc1", [128, 8], F32)

    S.fence_all()
    for dst, src in ((nwc, "l1f_nw"), (nbc, "l1f_nb"), (qbc, "l1f_qb"),
                     (kbc, "l1f_kb")):
        S.op("sync", lambda s, dst=dst, src=src: s.dma_start(
            out=dst[:, :], in_=bias_col(W[src].ap()),
            allow_slow_non_contiguous=True), inc="wq", k=16)
    vb_k = wdma(S, vbrow[0:1, :], W["l1f_vb"].ap().unsqueeze(0))
    ldw = wdma(S, obrow[0:1, :], W["l1f_ob"].ap().unsqueeze(0))
    wq0 = S.cnt["wq"]
    S.op("sync", lambda s: None, waits=[("wq", wq0), ldw])
    ln_boundary(st, "_l1fn%d" % b, affine=(nwc, nbc, arena))

    pst = lambda n, s, d: es.enter_context(nc.psum_tensor(n + "_l1f%d" % b, s, d))
    pss = pst("pss", [128, 2048], F32)
    psp = pss[:, 0:1024]
    psat = [pst("psat%d" % i, [128, 144], F32) for i in range(2)]
    psTs = [pst("psT%d" % i, [128, 128], BF16) for i in range(2)]

    wtr_state = dict(mm=0, ev=0)

    def wtrans(dst, wname, rowblk, mcols, extra=()):
        wd = wdma(S, wraw[:, 0:mcols],
                  W[wname].ap()[128 * rowblk:128 * rowblk + 128, 0:mcols],
                  waits=list(extra) + [("mm", wtr_state["mm"])])
        ev0 = S.cnt["ev"]
        evs = None
        for j in range(mcols // 128):
            w = [wd] if j == 0 else []
            if j < 2:
                w.append(("ev", ev0))
            else:
                w.append(("ev", evw[j - 2]))
            mt = S.op("tensor", lambda t, j=j: t.transpose(
                psTs[j % 2][:, :], wraw[:, 128 * j:128 * j + 128],
                identbf[:]), waits=w, inc="mm")
            if j == 0:
                evw = {}
            evw[j] = S.op("scalar", lambda sc, j=j, dst=dst: sc.activation(
                dst[:, 128 * j:128 * j + 128], psTs[j % 2][:, :], AF.Identity),
                waits=[("mm", mt)], inc="ev")
            evs = evw[j]
        wtr_state["mm"] = S.cnt["mm"]
        wtr_state["ev"] = evs
        return evs

    prev_tail = dict(mm=S.cnt["mm"], ve=S.cnt["ve"], ev=S.cnt["ev"], wg=S.cnt["wg"])
    prev_e4f = None
    for h in range(8):
        wtrans(qwT, "l1f_qw", h, 1024)
        wtrans(kwT, "l1f_kw", h, 1024)
        e3 = wtrans(vwT, "l1f_vw", h, 1024)
        qkev = dict(q=prev_tail["ev"], k=prev_tail["ev"])
        for e in range(4):
            mch = NCH >> e
            for c in range(mch):
                if e == 0 and c == 0:
                    w = [("ev", e3), ("ve", prev_tail["ve"])]
                elif c == 0:
                    w = [("ev", qkev["q"])]
                else:
                    w = ()
                S.op("tensor", lambda t, e=e, c=c, mch=mch: t.matmul(
                    psp[:, 0:512], qwT[:, 128 * c:128 * c + 128],
                    arena[:, T * c + 512 * e:T * c + 512 * e + 512],
                    start=(c == 0), stop=(c == mch - 1)), waits=w,
                    inc="mm" if c == mch - 1 else None)
            qkev["q"] = S.op("scalar", lambda sc, e=e, h=h: sc.activation(
                qtb[:, 512 * e:512 * e + 512], psp[:, 0:512], AF.Identity,
                bias=qbc[:, h:h + 1]), waits=[("mm", S.cnt["mm"])], inc="ev")
            for c in range(mch):
                S.op("tensor", lambda t, e=e, c=c, mch=mch: t.matmul(
                    psp[:, 512:1024], kwT[:, 128 * c:128 * c + 128],
                    arena[:, T * c + 512 * e:T * c + 512 * e + 512],
                    start=(c == 0), stop=(c == mch - 1)),
                    waits=[("ev", qkev["k"])] if c == 0 else (),
                    inc="mm" if c == mch - 1 else None)
            qkev["k"] = S.op("scalar", lambda sc, e=e, h=h: sc.activation(
                ktb[:, 512 * e:512 * e + 512], psp[:, 512:1024], AF.Identity,
                bias=kbc[:, h:h + 1]), waits=[("mm", S.cnt["mm"])], inc="ev")
            for j in range(4):
                tt = 4 * e + j
                par = tt % 2
                w = []
                if tt >= 2:
                    w.append(("ev", vtt[tt - 2]))
                for c in range(mch):
                    S.op("tensor", lambda t, c=c, tt=tt, par=par: t.matmul(
                        pss[:, 1024 + 512 * par:1024 + 512 * par + 128],
                        arena[:, T * c + 128 * tt:T * c + 128 * tt + 128],
                        vwT[:, 128 * c:128 * c + 128],
                        start=(c == 0), stop=False), waits=w if c == 0 else ())
                mv = S.op("tensor", lambda t, h=h, par=par, tt=tt: t.matmul(
                    pss[:, 1024 + 512 * par:1024 + 512 * par + 128], onesbf[0:1, 0:128],
                    vbrow[0:1, 128 * h:128 * h + 128], start=False, stop=True),
                    waits=[("wq", wq0), vb_k] if tt == 0 else (), inc="mm")
                if tt == 0:
                    vtt = {}
                vtt[tt] = S.op("scalar", lambda sc, tt=tt, par=par: sc.activation(
                    vaug[:, 129 * tt:129 * tt + 128],
                    pss[:, 1024 + 512 * par:1024 + 512 * par + 128],
                    AF.Identity), waits=[("mm", mv)], inc="ev")
        vone = S.op("vector", lambda v: v.memset(
            vaug[:].rearrange("p (a n) -> p a n", n=129)[:, :, 128:129], 1.0),
            waits=[("ev", S.cnt["ev"]), ("ve", prev_tail["ve"])], inc="ve")

        ebn = bcalc(S, pss[:, 1024:2048], pss[:, 0:1024], sc1, nB, q2, qtb, ktb,
                    128, SC, onesbf, ones32,
                    asm_deps=[("ev", S.cnt["ev"]), ("ve", vone)])

        for qg in range(QGN):
            eex = {}
            for kti in range(TTN):
                w = []
                if kti == 0:
                    w = [("ev", ebn)]
                if kti < 4 and prev_e4f is not None:
                    w.append(("ev", prev_e4f[kti]))
                if kti >= 4:
                    w.append(("ev", eex[kti - 4]))
                smm = S.op("tensor", lambda t, qg=qg, kti=kti: t.matmul(
                    pss[:, 512 * (kti % 4):512 * (kti % 4) + 512],
                    ktb[:, 128 * kti:128 * kti + 128],
                    qtb[:, 512 * qg:512 * qg + 512], start=True, stop=True),
                    waits=w, inc="mm")
                eex[kti] = S.op("scalar", lambda sc, kti=kti: sc.activation(
                    ppr[:, 512 * kti:512 * kti + 512],
                    pss[:, 512 * (kti % 4):512 * (kti % 4) + 512],
                    AF.Exp, bias=nB[:, 0:1], scale=np.float32(SC)),
                    waits=[("mm", smm), ("ev", ebn)], inc="ev")
            prev_e4f = [eex[TTN - 4 + j] for j in range(4)]
            exp_all = S.cnt["ev"]
            for half in range(2):
                for j2 in range(2):
                    jt = 2 * half + j2
                    for kti in range(TTN):
                        S.op("tensor", lambda t, jt=jt, j2=j2, kti=kti: t.matmul(
                            psat[j2][0:128, 0:129],
                            ppr[:, 512 * kti + 128 * jt:512 * kti + 128 * jt + 128],
                            vaug[:, 129 * kti:129 * kti + 129],
                            start=(kti == 0), stop=(kti == TTN - 1)),
                            waits=[("ev", exp_all), ("ve", S.cnt["ve"])]
                            if (kti == 0 and j2 == 0) else (),
                            inc="mm" if kti == TTN - 1 else None)
                att_done = S.cnt["mm"]
                for j2 in range(2):
                    jt = 2 * half + j2
                    S.op("vector", lambda v, jt=jt, j2=j2: v.reciprocal(
                        sccol[:, jt:jt + 1], psat[j2][0:128, 128:129]),
                        waits=[("mm", att_done), ("ve", S.cnt["ve"])] if j2 == 0 else (),
                        inc="ve")
                vsc = S.cnt["ve"]
                for j2 in range(2):
                    jt = 2 * half + j2
                    S.op("vector", lambda v, jt=jt, j2=j2: v.tensor_scalar(
                        attsc[:, 128 * jt:128 * jt + 128], psat[j2][0:128, 0:128],
                        sccol[:, jt:jt + 1], None, ALU.mult),
                        waits=[("ve", vsc)] if j2 == 0 else (),
                        inc="ve" if j2 == 1 else None)
                vats = S.cnt["ve"]
                for j2 in range(2):
                    jt = 2 * half + j2
                    mt = S.op("tensor", lambda t, jt=jt: t.transpose(
                        psTs[jt % 2][:, :], attsc[:, 128 * jt:128 * jt + 128],
                        identbf[:]),
                        waits=[("ve", vats), ("ev", S.cnt["ev"])],
                        inc="mm")
                    S.op("scalar", lambda sc, jt=jt, qg=qg, h=h: sc.activation(
                        xbf[:, T * h + 512 * qg + 128 * jt:T * h + 512 * qg + 128 * jt + 128],
                        psTs[jt % 2][:, :], AF.Identity), waits=[("mm", mt)], inc="ev")
            prev_tail = dict(mm=S.cnt["mm"], ev=S.cnt["ev"], ve=S.cnt["ve"])

    S.fence_all()
    vres = None
    for e in range(4):
        mch = NCH >> e
        for ot in range(mch):
            eo = wtrans(owT, "l1f_ow", ot, 128 * mch, extra=[("ve", S.cnt["ve"])])
            for c in range(mch):
                w = ([("ev", eo)] + ([("ve", vres)] if vres else [])) if c == 0 else ()
                S.op("tensor", lambda t, e=e, c=c: t.matmul(
                    psp[:, 0:512], owT[:, 128 * c:128 * c + 128],
                    xbf[:, T * c + 512 * e:T * c + 512 * e + 512],
                    start=(c == 0), stop=False), waits=w)
            mo = S.op("tensor", lambda t, ot=ot: t.matmul(
                psp[:, 0:512], obrow[0:1, 128 * ot:128 * ot + 128],
                onesbf[0:1, 0:512], start=False, stop=True), inc="mm")
            vres = S.op("vector", lambda v, e=e, ot=ot: v.scalar_tensor_tensor(
                x32[:, T * ot + 512 * e:T * ot + 512 * e + 512],
                x32[:, T * ot + 512 * e:T * ot + 512 * e + 512], 2.0,
                psp[:, 0:512], ALU.mult, ALU.add),
                waits=[("mm", mo)], inc="ve")
        for ot in range(mch, NCH):
            vres = S.op("vector", lambda v, e=e, ot=ot: v.tensor_scalar(
                x32[:, T * ot + 512 * e:T * ot + 512 * e + 512],
                x32[:, T * ot + 512 * e:T * ot + 512 * e + 512], 2.0, None, ALU.mult),
                inc="ve")
    S.fence_all()
    S.emit()
    es.close()
    ln_boundary(st, "_l1f%d" % b)



def build_nc(nphase=99, custom_seq=None):
    nc = bass.Bass("TRN2", target_bir_lowering=False)
    xin = nc.dram_tensor("x_in", [BPC, T, D], F32, kind="ExternalInput")
    xout = nc.dram_tensor("x_out", [BPC, T, D], F32, kind="ExternalOutput")
    W = {}

    def din(name, shape):
        W[name] = nc.dram_tensor(name, shape, F32, kind="ExternalInput")

    din("l0a_wqkv", [D, 3 * D]); din("l0a_bqkv", [3 * D])
    din("l0a_wout", [D, D]); din("l0a_bout", [D])
    din("l0f_gw", [D, 8]); din("l0f_w1", [8, D, 512]); din("l0f_b1", [8, 512])
    din("l0f_w2", [8, 512, D]); din("l0f_b2", [8, D])
    din("l0f_ws1", [D, 512]); din("l0f_bs1", [512])
    din("l0f_ws2", [512, D]); din("l0f_bs2", [D])
    din("l0f_rs", [1, 1])
    din("l1a_wq", [D, D]); din("l1a_wqr", [D, 512]); din("l1a_wkr", [D, 32])
    din("l1a_wkvd", [D, 256]); din("l1a_wku", [256, D]); din("l1a_wvu", [256, D])
    din("l1a_wo", [D, D]); din("l1a_lobo", [1, 16])
    din("l1f_nw", [D]); din("l1f_nb", [D])
    din("l1f_qw", [D, D]); din("l1f_qb", [D]); din("l1f_kw", [D, D]); din("l1f_kb", [D])
    din("l1f_vw", [D, D]); din("l1f_vb", [D]); din("l1f_ow", [D, D]); din("l1f_ob", [D])

    CD = _mk_consts(nc)
    with ExitStack() as ctx:
        sb = lambda n, s, d: ctx.enter_context(nc.sbuf_tensor(n, s, d))
        x32 = sb("x32", [128, NCH * T], F32)
        xbf = sb("xbf", [128, NCH * T], BF16)
        arena = sb("arena", [128, NCH * T], BF16)
        identbf = sb("identbf_s", [128, 128], BF16)
        ident32 = sb("ident32_s", [128, 128], F32)
        onesbf = sb("onesbf_s", [128, 512], BF16)
        ones32 = sb("ones32_s", [128, 128], F32)
        maskc = sb("maskc_s", [128, 2048], BF16)
        pswap = sb("pswap_s", [128, 128], BF16)
        blk = ctx.enter_context(nc.Block())
        S = Sched(nc, ctx, blk)

        for dst, key in [(identbf, "identbf"), (ident32, "ident32"),
                         (onesbf, "onesbf"), (ones32, "ones32"),
                         (maskc, "maskc"), (pswap, "pswap")]:
            S.op("sync", lambda s, dst=dst, key=key: s.dma_start(
                out=dst[:], in_=CD[key].ap()), inc="wq", k=16)
        S.emit()

        st = dict(S=S, nc=nc, W=W, xin=xin, xout=xout, x32=x32, xbf=xbf,
                  arena=arena, identbf=identbf, ident32=ident32, onesbf=onesbf,
                  ones32=ones32, maskc=maskc, pswap=pswap, CD=CD)

        if custom_seq is not None:
            seq = [({"xload": phase_xload, "l0a": phase_l0a, "l0f": phase_l0f,
                     "l1a": phase_l1a, "l1f": phase_l1f, "store": phase_store}[nm], b)
                   for nm, b in custom_seq]
        else:
            seq = []
            for b in range(BPC):
                seq += [(phase_xload, b), (phase_l0a, b), (phase_l0f, b),
                        (phase_l1a, b), (phase_l1f, b), (phase_store, b)]
            seq = seq[:nphase]
        for fn, b in seq:
            fn(st, b)

        if S.cnt["out"]:
            S.op("sync", lambda s: None, waits=[("out", S.cnt["out"])])
        S.fence_all()
        S.emit()
    return nc



_IN_MAP = [
    ("l0a_wqkv", "L0_attn_W_qkv"), ("l0a_bqkv", "L0_attn_b_qkv"),
    ("l0a_wout", "L0_attn_W_out"), ("l0a_bout", "L0_attn_b_out"),
    ("l0f_gw", "L0_ffn_gate_W"), ("l0f_w1", "L0_ffn_W1"), ("l0f_b1", "L0_ffn_b1"),
    ("l0f_w2", "L0_ffn_W2"), ("l0f_b2", "L0_ffn_b2"),
    ("l0f_ws1", "L0_ffn_Ws1"), ("l0f_bs1", "L0_ffn_bs1"),
    ("l0f_ws2", "L0_ffn_Ws2"), ("l0f_bs2", "L0_ffn_bs2"),
    ("l1a_wq", "L1_attn_Wq"), ("l1a_wqr", "L1_attn_Wqr"), ("l1a_wkr", "L1_attn_Wkr"),
    ("l1a_wkvd", "L1_attn_Wkvd"), ("l1a_wku", "L1_attn_Wku"),
    ("l1a_wvu", "L1_attn_Wvu"), ("l1a_wo", "L1_attn_Wo"),
    ("l1f_nw", "L1_ffn_norm_w"), ("l1f_nb", "L1_ffn_norm_b"),
    ("l1f_qw", "L1_ffn_q_w"), ("l1f_qb", "L1_ffn_q_b"),
    ("l1f_kw", "L1_ffn_k_w"), ("l1f_kb", "L1_ffn_k_b"),
    ("l1f_vw", "L1_ffn_v_w"), ("l1f_vb", "L1_ffn_v_b"),
    ("l1f_ow", "L1_ffn_o_w"), ("l1f_ob", "L1_ffn_o_b"),
]

_CACHE = {}


def kernel(**inputs):
    n_cores = 8
    if "nc" not in _CACHE:
        _CACHE["nc"] = build_nc()
    nc = _CACHE["nc"]
    x = np.ascontiguousarray(np.asarray(inputs["L0_attn_x"], dtype=np.float32))
    wmap = {}
    for kname, iname in _IN_MAP:
        wmap[kname] = np.ascontiguousarray(np.asarray(inputs[iname], dtype=np.float32))
    wmap["l1a_lobo"] = np.ascontiguousarray(
        np.asarray(inputs["L1_attn_lobo_log"], dtype=np.float32).reshape(1, 16))
    wmap["l0f_rs"] = np.ascontiguousarray(
        np.asarray(inputs["L0_ffn_route_scale"], dtype=np.float32).reshape(1, 1))
    in_maps = []
    for c in range(n_cores):
        m = dict(wmap)
        m["x_in"] = np.ascontiguousarray(x[BPC * c:BPC * c + BPC])
        in_maps.append(m)
    res = run_bass_kernel_spmd(nc, in_maps, core_ids=list(range(n_cores)))
    out = np.concatenate([np.asarray(r["x_out"]) for r in res.results], axis=0)
    return out.astype(np.float32)
